# version 08
# baseline (speedup 1.0000x reference)
import numpy as np
import ml_dtypes
import concourse.bass as bass
import concourse.mybir as mybir
from concourse.bass_utils import run_bass_kernel_spmd

F32 = mybir.dt.float32
BF16 = mybir.dt.bfloat16
AF = mybir.ActivationFunctionType
ALU = mybir.AluOpType
AX = mybir.AxisListType

ENGINES = ("pe", "act", "dve", "pool", "sp")
_BLOCK_ATTR = {"pe": "tensor", "act": "scalar", "dve": "vector",
               "pool": "gpsimd", "sp": "sync"}

QP = "act"
QT = "sp"
T = 2048
C = 2048
NT = T // 128
NC = C // 128
NH = 16
HD = 128
I0 = 5632
NI0 = I0 // 128
NE = 8
I1 = 1408
NI1 = I1 // 128
DST = 16
ISQ = 1.0 / np.sqrt(128.0)

DEV_TAPS = ()



class Buf:
    __slots__ = ("ap", "name", "writer", "readers", "pending", "_rng")

    def __init__(self, ap, name=""):
        self.ap = ap
        self.name = name
        self.writer = None
        self.readers = []
        self.pending = []


class _Task:
    __slots__ = ("waits", "emit", "sem", "amt")


class Sched:
    DMA_RING = 8

    def __init__(self, nc):
        self.nc = nc
        self.tasks = {e: [] for e in ENGINES}
        self.cnt = {}
        self.sem_names = []
        for e in ENGINES:
            self._new_sem("p_" + e)
        self.dma_ring = {}
        self.dma_prev = {}
        for q in ("sp", "act", "pool"):
            names = []
            for i in range(self.DMA_RING):
                n = f"d_{q}{i}"
                self._new_sem(n)
                names.append(n)
            self.dma_ring[q] = names
            for n in names:
                self.dma_prev[n] = None
        self.dma_rr = {q: 0 for q in self.dma_ring}
        self.clock = {e: {} for e in ENGINES}
        self.n_waits = 0
        self.n_tasks = 0

    def _new_sem(self, name):
        self.sem_names.append(name)
        self.cnt[name] = 0

    def buf(self, ap, name=""):
        return Buf(ap, name)

    def task(self, engine, emit, reads=(), writes=(), dma=None, extra_waits=()):
        eng_clock = self.clock[engine]
        waits = {}

        def need(dep):
            if dep is None:
                return
            sem, val, _clk = dep
            if eng_clock.get(sem, 0) < val and waits.get(sem, 0) < val:
                waits[sem] = val

        for b in reads:
            need(b.writer)
        for b in writes:
            need(b.writer)
            for r in b.readers:
                need(r)
            for dp in b.pending:
                need(dp)
        for wdep in extra_waits:
            need(wdep)

        if dma is not None:
            ring = self.dma_ring[dma]
            sem = ring[self.dma_rr[dma] % len(ring)]
            self.dma_rr[dma] += 1
            amt = 16
            need(self.dma_prev[sem])
        else:
            sem, amt = "p_" + engine, 1
        self.cnt[sem] += amt
        val = self.cnt[sem]

        t = _Task()
        t.waits = sorted(waits.items())
        t.emit = emit
        t.sem, t.amt = sem, amt
        self.tasks[engine].append(t)
        self.n_waits += len(t.waits)
        self.n_tasks += 1

        for s, v in waits.items():
            if eng_clock.get(s, 0) < v:
                eng_clock[s] = v
        for b in reads:
            if b.writer is not None:
                self._merge(eng_clock, b.writer[2])
        for b in writes:
            if b.writer is not None:
                self._merge(eng_clock, b.writer[2])
            for r in b.readers:
                self._merge(eng_clock, r[2])
            b.pending = []

        clk = dict(eng_clock)
        clk[sem] = max(clk.get(sem, 0), val)
        dep = (sem, val, clk)
        if dma is not None:
            self.dma_prev[sem] = dep
        for b in writes:
            b.writer = dep
            b.readers = []
        for b in reads:
            b.readers.append(dep)
        return dep

    @staticmethod
    def _merge(dst, src):
        for s, v in src.items():
            if dst.get(s, 0) < v:
                dst[s] = v

    def barrier_wait(self, engine, deps):
        t = _Task()
        eng_clock = self.clock[engine]
        waits = {}
        for dep in deps:
            if dep is None:
                continue
            sem, val, _ = dep
            if eng_clock.get(sem, 0) < val and waits.get(sem, 0) < val:
                waits[sem] = val
        t.waits = sorted(waits.items())
        t.emit = None
        t.sem = None
        self.tasks[engine].append(t)
        for s, v in waits.items():
            if eng_clock.get(s, 0) < v:
                eng_clock[s] = v

    def build(self):
        from contextlib import ExitStack
        nc = self.nc
        with ExitStack() as ctx:
            sems = {}
            for name in self.sem_names:
                sems[name] = ctx.enter_context(nc.semaphore(name))
            block = ctx.enter_context(nc.Block())

            def mk(tasks):
                def emit_all(eng):
                    for t in tasks:
                        for s, v in t.waits:
                            eng.wait_ge(sems[s], v)
                        if t.emit is not None:
                            inst = t.emit(eng)
                            inst.then_inc(sems[t.sem], t.amt)
                return emit_all

            for e in ENGINES:
                getattr(block, _BLOCK_ATTR[e])(mk(self.tasks[e]))


class Arena:
    def __init__(self, nc, sch, name, nbytes):
        h = nc.alloc_sbuf_tensor(name, [128, nbytes], mybir.dt.uint8)
        self.base = nc.lookup_mloc(h).addr
        self.nbytes = nbytes
        self.nc = nc
        self.sch = sch
        self.off = 0
        self.live = []
        self.retired = []
        self.n = 0

    def alloc(self, shape, dtype, name=""):
        esz = 4 if dtype == F32 else 2
        fbytes = int(np.prod(shape[1:])) * esz
        off = (self.off + 31) & ~31
        assert off + fbytes <= self.nbytes, \
            f"arena overflow: {name} needs {fbytes}B at {off}/{self.nbytes}"
        self.off = off + fbytes
        h = self.nc.alloc_sbuf_tensor_at(
            f"ar{self.n}_{name}", list(shape), dtype, offset=self.base + off)
        self.n += 1
        b = self.sch.buf(h.ap(), name)
        pend = []
        for ro, rs_, deps in self.retired:
            if off < ro + rs_ and ro < off + fbytes:
                pend.extend(deps)
        b.pending = pend
        b._rng = (off, fbytes)
        self.live.append(b)
        return b

    def reset(self):
        retired = []
        for b in self.live:
            deps = list(b.readers)
            if b.writer is not None:
                deps.append(b.writer)
            if deps:
                retired.append((b._rng[0], b._rng[1], deps))
        self.retired = retired + self.retired
        if len(self.retired) > 400:
            keep, old = self.retired[:200], self.retired[200:]
            best = {}
            for _o, _s, deps in old:
                for dp in deps:
                    if best.get(dp[0], (None, -1))[1] < dp[1]:
                        best[dp[0]] = (dp, dp[1])
            self.retired = keep + [(0, self.nbytes,
                                    [v[0] for v in best.values()])]
        self.live = []
        self.off = 0


class Ctx:
    pass


def E_dma(out_ap, in_ap, **kw):
    return lambda e: e.dma_start(out=out_ap, in_=in_ap, **kw)


def dbx(ctx, name, lo=None, hi=None):
    regs = ctx.db[name]
    if lo is None or len(regs) == 1:
        return list(regs)
    nrows = ctx.d[name].shape[0]
    step = nrows // len(regs)
    return regs[lo // step:(hi + step - 1) // step]


def E_dma_gp(out_ap2d, in_ap2d, w):
    in3 = in_ap2d.rearrange("(g p) w -> p g w", p=128)
    out3 = out_ap2d.rearrange("p (g w) -> p g w", w=w)
    return lambda e: e.dma_start(out=out3, in_=in3)



def _mk_dram(ctx, dev_taps):
    nc = ctx.nc
    d = {}
    db = {}

    def din(name, shape, dt=F32):
        d[name] = nc.dram_tensor(name, list(shape), dt, kind="ExternalInput").ap()

    REGIONED = {"XN": 4, "GTM": 4, "XN2": 4, "HT": 4, "H2": 4,
                "AFM": 16, "SFM": 16}

    def dsc(name, shape, dt):
        kind = "ExternalOutput" if name in dev_taps else "Internal"
        d[name] = nc.dram_tensor(name, list(shape), dt, kind=kind).ap()
        nreg = REGIONED.get(name, 1)
        if nreg == 1:
            db[name] = [Buf(d[name], name)]
        else:
            step = shape[0] // nreg
            db[name] = [Buf(d[name][i * step:(i + 1) * step], f"{name}_{i}")
                        for i in range(nreg)]

    din("x", [T, C])
    din("wqT", [C, C]); din("wkT", [C, C]); din("wvT", [C, C])
    din("tau", [1, 1]); din("nu", [1, 1])
    din("gwT", [C, I0]); din("uwT", [C, I0]); din("dwT", [I0, C])
    din("gs", [128, NI0]); din("us", [128, NI0]); din("ds_row", [1, C])
    din("AlogR", [128, NC * DST]); din("Dr", [128, NC])
    din("xpT", [C, 160]); din("dtT", [128, C]); din("dtb", [128, NC])
    din("anw", [1, C]); din("fnw", [1, C])
    din("qwT", [C, C]); din("kwT", [C, 512]); din("vwT", [C, 512])
    din("owT", [C, C]); din("qnw", [1, HD]); din("knw", [1, HD])
    din("rwT", [C, NE])
    din("wgT", [NE, C, I1]); din("wuT", [NE, C, I1]); din("wdT", [NE, I1, C])
    din("rcos", [T, 64]); din("rsin", [T, 64])
    din("cmask", [4, 128, 512])

    d["out"] = nc.dram_tensor("out", [T, C], F32, kind="ExternalOutput").ap()

    dsc("XN", [T, C], BF16)
    dsc("Q0", [T, C], BF16); dsc("K0", [T, C], BF16); dsc("V0", [T, C], BF16)
    dsc("PD", [2, T, T], BF16)
    dsc("SFM", [C, T], BF16)
    dsc("GTM", [T, C], BF16)
    dsc("X1", [T, C], F32)
    dsc("XN2", [T, C], BF16)
    dsc("HT", [T, C], BF16)
    dsc("Q1", [T, C], BF16); dsc("K1", [T, 512], BF16); dsc("V1", [T, 512], BF16)
    dsc("AFM", [C, T], BF16)
    dsc("HID", [T, C], BF16)
    dsc("H2", [T, C], BF16)
    dsc("BCR", [32, T], F32)
    dsc("Y", [C, T], BF16)
    dsc("MOE", [T, C], F32)
    ctx.d = d
    ctx.db = db
    ctx.pd_reg = [[Buf(d["PD"][par, i * 128:(i + 1) * 128, :], f"pd{par}_{i}")
                   for i in range(NT)] for par in range(2)]


def _mk_static(ctx):
    nc, sch = ctx.nc, ctx.sch
    A = lambda name, shape, dt: sch.buf(
        nc.alloc_sbuf_tensor(name, list(shape), dt).ap(), name)

    ctx.XNF = [A(f"xnf{i}", [128, T], BF16) for i in range(NC)]
    ctx.cmask = [A(f"cmask{i}", [128, 512], F32) for i in range(4)]
    ctx.tau_bc = A("tau_bc", [128, 1], F32)
    ctx.nu_bc = A("nu_bc", [128, 1], F32)
    ctx.gs = A("gs_sb", [128, NI0], F32)
    ctx.us = A("us_sb", [128, NI0], F32)
    ctx.A_sb = A("A_sb", [128, NC * DST], F32)
    ctx.D_sb = A("D_sb", [128, NC], F32)
    ctx.dtb = A("dtb_sb", [128, NC], F32)
    ctx.qw_bc = A("qw_bc", [128, HD], F32)
    ctx.kw_bc = A("kw_bc", [128, HD], F32)
    ctx.rc = A("rc_sb", [128, NT * 64], F32)
    ctx.rs = A("rs_sb", [128, NT * 64], F32)
    ctx.hst = A("hst", [128, NC * DST], F32)
    ctx.c_eps5 = A("c_eps5", [128, 1], F32)
    ctx.neg30k = A("neg30k", [128, 512], F32)
    ctx.c_eps6 = A("c_eps6", [128, 1], F32)
    ctx.comb = [A(f"comb{i}", [128, NE], F32) for i in range(NT)]
    ctx.rw_sb = A("rw_sb", [128, NC * NE], BF16)

    ctx.PS = [sch.buf(nc.alloc_psum_tensor(f"ps{i}", [128, 512], F32).ap(),
                      f"ps{i}") for i in range(8)]
    ctx._ps_rr = 0

    avail = (nc.sbuf_bytes_remaining - 512) & ~31
    ctx.arena = Arena(nc, sch, "arena", avail)


def ps_take(ctx, n=1):
    out = []
    for _ in range(n):
        out.append(ctx.PS[ctx._ps_rr % 8])
        ctx._ps_rr += 1
    return out if n > 1 else out[0]


def load_consts(ctx):
    sch, d = ctx.sch, ctx.d
    for i in range(4):
        sch.task(QP, E_dma(ctx.cmask[i].ap, d["cmask"][i]),
                 writes=[ctx.cmask[i]], dma=QP)

    def bc(buf, src):
        sch.task("pool", E_dma(buf.ap, src.partition_broadcast(128)),
                 writes=[buf], dma="pool")

    sch.task("dve", lambda e: e.memset(ctx.c_eps5.ap, 1e-5), writes=[ctx.c_eps5])
    sch.task("dve", lambda e: e.memset(ctx.neg30k.ap, -30000.0),
             writes=[ctx.neg30k])
    sch.task("dve", lambda e: e.memset(ctx.c_eps6.ap, 1e-6), writes=[ctx.c_eps6])
    bc(ctx.tau_bc, d["tau"])
    bc(ctx.nu_bc, d["nu"])
    bc(ctx.qw_bc, d["qnw"])
    bc(ctx.kw_bc, d["knw"])
    for name in ("gs", "us", "dtb"):
        sch.task(QP, E_dma(getattr(ctx, name).ap, d[name]),
                 writes=[getattr(ctx, name)], dma=QP)
    sch.task(QP, E_dma(ctx.D_sb.ap, d["Dr"]), writes=[ctx.D_sb], dma=QP)
    sch.task(QP, E_dma(ctx.A_sb.ap, d["AlogR"]), writes=[ctx.A_sb], dma=QP)
    sch.task("act", lambda e: e.activation(ctx.A_sb.ap, ctx.A_sb.ap, AF.Exp),
             reads=[ctx.A_sb], writes=[ctx.A_sb])
    sch.task("dve", lambda e: e.tensor_scalar_mul(ctx.A_sb.ap, ctx.A_sb.ap, -1.0),
             reads=[ctx.A_sb], writes=[ctx.A_sb])
    sch.task(QP, E_dma_gp(ctx.rc.ap, d["rcos"], 64),
             writes=[ctx.rc], dma=QP)
    sch.task(QP, E_dma_gp(ctx.rs.ap, d["rsin"], 64),
             writes=[ctx.rs], dma=QP)
    sch.task("pool", E_dma_gp(ctx.rw_sb.ap, d["rwT"], NE),
             writes=[ctx.rw_sb], dma="pool")



def tload(ctx, dst_buf, dst_ap, src_ap, reads, q=QT):
    ctx.sch.task(q, E_dma(dst_ap, src_ap, transpose=True),
                 reads=reads, writes=[dst_buf], dma=q)


def fill_xnf(ctx, src_name):
    d = ctx.d
    for t4 in range(4):
        for cc in range(NC):
            tload(ctx, ctx.XNF[cc],
                  ctx.XNF[cc].ap[:, t4 * 512:(t4 + 1) * 512],
                  d[src_name][t4 * 512:(t4 + 1) * 512, cc * 128:(cc + 1) * 128],
                  dbx(ctx, src_name, t4 * 512, (t4 + 1) * 512))


def emit_rms(ctx, sm, src_buf, src_ap, wbc_buf, dst_buf, dst_ap):
    sch = ctx.sch
    ss, rstd, junk, t1 = sm["ss"], sm["rstd"], sm["junk"], sm["t1"]
    sch.task("act", lambda e: e.activation(junk.ap, src_ap, AF.Square,
                                           accum_out=ss.ap),
             reads=[src_buf], writes=[junk, ss])
    sch.task("act", lambda e: e.activation(rstd.ap, ss.ap, AF.Ln,
                                           scale=1.0 / C, bias=ctx.c_eps6.ap),
             reads=[ss, ctx.c_eps6], writes=[rstd])
    sch.task("act", lambda e: e.activation(rstd.ap, rstd.ap, AF.Exp, scale=-0.5),
             reads=[rstd], writes=[rstd])
    sch.task("dve", lambda e: e.tensor_scalar_mul(t1.ap, src_ap, rstd.ap),
             reads=[src_buf, rstd], writes=[t1])
    sch.task("dve", lambda e: e.tensor_tensor(dst_ap, t1.ap, wbc_buf.ap, ALU.mult),
             reads=[t1, wbc_buf], writes=[dst_buf])



def phase_ln0(ctx):
    sch, d, db, ar = ctx.sch, ctx.d, ctx.db, ctx.arena
    ar.reset()
    xt = [ar.alloc([128, C], F32, "p1x") for _ in range(2)]
    xn = [ar.alloc([128, C], BF16, "p1n") for _ in range(2)]
    sm = [dict(s1=ar.alloc([128, 1], F32, "s1"), ss=ar.alloc([128, 1], F32, "ss"),
               m=ar.alloc([128, 1], F32, "m"), v=ar.alloc([128, 1], F32, "v"),
               rstd=ar.alloc([128, 1], F32, "rstd"),
               nmr=ar.alloc([128, 1], F32, "nmr"),
               junk=ar.alloc([128, C], BF16, "junk")) for _ in range(2)]
    for tc in range(NT):
        xb, nb, s = xt[tc % 2], xn[tc % 2], sm[tc % 2]
        sch.task(QP, E_dma(xb.ap, d["x"][tc * 128:(tc + 1) * 128, :]),
                 writes=[xb], dma=QP)
        s1, ss, m, v, rstd, nmr, junk = (s[k] for k in
                                         ("s1", "ss", "m", "v", "rstd", "nmr", "junk"))
        sch.task("dve", lambda e, a=xb, o=s1: e.reduce_sum(o.ap, a.ap, axis=AX.X),
                 reads=[xb], writes=[s1])
        sch.task("act", lambda e, a=xb, j=junk, o=ss: e.activation(
            j.ap, a.ap, AF.Square, accum_out=o.ap),
                 reads=[xb], writes=[junk, ss])
        sch.task("dve", lambda e, a=s1, o=m: e.tensor_scalar_mul(o.ap, a.ap, 1.0 / C),
                 reads=[s1], writes=[m])
        sch.task("dve", lambda e, a=m, o=v: e.scalar_tensor_tensor(
            o.ap, a.ap, 1.0, a.ap, ALU.mult, ALU.mult),
                 reads=[m], writes=[v])
        sch.task("dve", lambda e, a=ss, o=v: e.scalar_tensor_tensor(
            o.ap, a.ap, 1.0 / C, o.ap, ALU.mult, ALU.subtract),
                 reads=[ss, v], writes=[v])
        sch.task("act", lambda e, a=v, o=rstd: e.activation(
            o.ap, a.ap, AF.Ln, bias=ctx.c_eps5.ap),
                 reads=[v, ctx.c_eps5], writes=[rstd])
        sch.task("act", lambda e, o=rstd: e.activation(
            o.ap, o.ap, AF.Exp, scale=-0.5),
                 reads=[rstd], writes=[rstd])
        sch.task("dve", lambda e, a=m, b=rstd, o=nmr: e.scalar_tensor_tensor(
            o.ap, a.ap, -1.0, b.ap, ALU.mult, ALU.mult),
                 reads=[m, rstd], writes=[nmr])
        sch.task("dve", lambda e, a=xb, b=rstd, c=nmr, o=nb: e.tensor_scalar(
            o.ap, a.ap, b.ap, c.ap, ALU.mult, ALU.add),
                 reads=[xb, rstd, nmr], writes=[nb])
        sch.task(QP, E_dma(d["XN"][tc * 128:(tc + 1) * 128, :], nb.ap),
                 reads=[nb], writes=dbx(ctx, "XN", tc * 128, (tc + 1) * 128),
                 dma=QP)



def phase_qkv0(ctx):
    sch, d, db, ar = ctx.sch, ctx.d, ctx.db, ctx.arena
    ar.reset()
    fill_xnf(ctx, "XN")
    W = [ar.alloc([128, C], BF16, f"w0_{i}") for i in range(NC)]
    qh = [ar.alloc([128, C], BF16, "qh") for _ in range(2)]
    ssl = [ar.alloc([128, 16], F32, "ssl") for _ in range(2)]
    rf = [ar.alloc([128, 16], F32, "rf") for _ in range(2)]
    junk = [ar.alloc([128, 128], BF16, "jk") for _ in range(2)]

    for wname, dst, is_q, is_v in (("wqT", "Q0", True, False),
                                   ("wkT", "K0", False, False),
                                   ("wvT", "V0", False, True)):
        for cc in range(NC):
            sch.task("pool", E_dma(W[cc].ap,
                                   d[wname][cc * 128:(cc + 1) * 128, :]),
                     writes=[W[cc]], dma="pool")
        for tc in range(NT):
            ps = [ps_take(ctx) for _ in range(4)]
            for c4 in range(4):
                def mme(e, tc=tc, c4=c4, p=ps[c4]):
                    for cc in range(NC):
                        inst = e.matmul(
                            p.ap,
                            ctx.XNF[cc].ap[:, tc * 128:(tc + 1) * 128],
                            W[cc].ap[:, c4 * 512:(c4 + 1) * 512],
                            start=(cc == 0), stop=(cc == NC - 1))
                    return inst
                sch.task("pe", mme, reads=ctx.XNF + W, writes=[ps[c4]])
            q, sl, r, jk = qh[tc % 2], ssl[tc % 2], rf[tc % 2], junk[tc % 2]
            for c4 in range(4):
                for j in range(4):
                    hidx = c4 * 4 + j
                    sch.task("act", lambda e, p=ps[c4], j=j, h=hidx, jk=jk, sl=sl:
                             e.activation(
                                 jk.ap, p.ap[:, j * 128:(j + 1) * 128], AF.Square,
                                 accum_out=sl.ap[:, h:h + 1]),
                             reads=[ps[c4]], writes=[jk, sl])
            sch.task("act", lambda e, a=sl, o=r: e.activation(o.ap, a.ap, AF.Ln),
                     reads=[sl], writes=[r])
            sch.task("act", lambda e, o=r: e.activation(o.ap, o.ap, AF.Exp,
                                                        scale=0.5),
                     reads=[r], writes=[r])
            sch.task("dve", lambda e, o=r: e.tensor_scalar_add(o.ap, o.ap, 1e-8),
                     reads=[r], writes=[r])
            sch.task("dve", lambda e, o=r: e.reciprocal(o.ap, o.ap),
                     reads=[r], writes=[r])
            if is_q:
                sch.task("dve", lambda e, o=r: e.tensor_scalar_mul(
                    o.ap, o.ap, ctx.tau_bc.ap),
                         reads=[r, ctx.tau_bc], writes=[r])
            for c4 in range(4):
                for j in range(4):
                    hidx = c4 * 4 + j
                    sch.task("dve", lambda e, p=ps[c4], j=j, h=hidx, q=q, r=r:
                             e.tensor_scalar_mul(
                                 q.ap[:, h * 128:(h + 1) * 128],
                                 p.ap[:, j * 128:(j + 1) * 128], r.ap[:, h:h + 1]),
                             reads=[ps[c4], r], writes=[q])
            sch.task(QP, E_dma(d[dst][tc * 128:(tc + 1) * 128, :], q.ap),
                     reads=[q], writes=db[dst], dma=QP)



def phase_attn(ctx, layer, hooks=None, reset=True):
    sch, d, db, ar = ctx.sch, ctx.d, ctx.db, ctx.arena
    if reset:
        ar.reset()
    if layer == 0:
        QN, KN, VN, ON = "Q0", "K0", "V0", "SFM"
        n_kv = 16
        kv_w = C
    else:
        QN, KN, VN, ON = "Q1", "K1", "V1", "AFM"
        n_kv = 4
        kv_w = 512
    rep = NH // n_kv

    qf = [ar.alloc([128, T], BF16, "qf") for _ in range(2)]
    kf = [ar.alloc([128, T], BF16, "kf") for _ in range(2)]
    vt = [ar.alloc([128, T], BF16, "vt") for _ in range(2)]
    psb = [[ar.alloc([128, T], BF16, "psb") for _ in range(2)]
           for _ in range(2)]
    pt = [ar.alloc([128, 512], BF16, "pt") for _ in range(2)]
    att = [ar.alloc([128, 512], BF16, "att") for _ in range(2)]
    negm = [[ar.alloc([128, 4], F32, "negm") for _ in range(2)]
            for _ in range(2)]
    rssb = [[ar.alloc([128, 4], F32, "rssb") for _ in range(2)]
            for _ in range(2)]
    den = [[ar.alloc([128, 1], F32, "den") for _ in range(2)]
           for _ in range(2)]

    def emit_S(h):
        par = h % 2
        g = h // rep
        tload(ctx, qf[par], qf[par].ap,
              d[QN][:, h * 128:(h + 1) * 128], db[QN])
        if h % rep == 0:
            tload(ctx, kf[g % 2], kf[g % 2].ap,
                  d[KN][:, g * 128:(g + 1) * 128], db[KN])
            for tkc in range(NT):
                sch.task(QP, E_dma(vt[g % 2].ap[:, tkc * 128:(tkc + 1) * 128],
                                     d[VN][tkc * 128:(tkc + 1) * 128,
                                           g * 128:(g + 1) * 128]),
                         reads=db[VN], writes=[vt[g % 2]], dma=QP)
        kbuf = kf[g % 2]

        def s_pair(pair):
            cinfo = []
            for tqc in pair:
                P, nm, rs, dn = (psb[par][tqc % 2], negm[par][tqc % 2],
                                 rssb[par][tqc % 2], den[par][tqc % 2])
                nch = tqc // 4 + 1
                banks = [ps_take(ctx) for _ in range(nch)]
                cinfo.append((tqc, P, nm, rs, dn, nch, banks))
            for tqc, P, nm, rs, dn, nch, banks in cinfo:
                for c, ps in enumerate(banks):
                    sch.task("pe", lambda e, p=ps, tqc=tqc, c=c, q=qf[par],
                             k=kbuf:
                             e.matmul(p.ap, q.ap[:, tqc * 128:(tqc + 1) * 128],
                                      k.ap[:, c * 512:(c + 1) * 512],
                                      start=True, stop=True),
                             reads=[qf[par], kbuf], writes=[ps])
            for tqc, P, nm, rs, dn, nch, banks in cinfo:
                dcb = banks[nch - 1]
                mv = tqc % 4
                sch.task("dve", lambda e, p=dcb, m=mv: e.scalar_tensor_tensor(
                    p.ap, p.ap, 30000.0, ctx.cmask[m].ap, ALU.add, ALU.mult),
                         reads=[dcb, ctx.cmask[mv]], writes=[dcb])
                sch.task("dve", lambda e, p=dcb: e.tensor_scalar_add(
                    p.ap, p.ap, -30000.0),
                         reads=[dcb], writes=[dcb])
            for tqc, P, nm, rs, dn, nch, banks in cinfo:
                for c, ps in enumerate(banks):
                    sch.task("dve", lambda e, p=ps, c=c, nm=nm: e.reduce_max(
                        nm.ap[:, c:c + 1], p.ap, axis=AX.X, negate=True),
                             reads=[ps], writes=[nm])
                if nch > 1:
                    sch.task("dve", lambda e, n=nch, nm=nm, dn=dn:
                             e.tensor_reduce(
                        dn.ap, nm.ap[:, 0:n], AX.X, ALU.min),
                             reads=[nm], writes=[dn])
                else:
                    sch.task("dve", lambda e, nm=nm, dn=dn: e.tensor_copy(
                        dn.ap, nm.ap[:, 0:1]),
                             reads=[nm], writes=[dn])
            for tqc, P, nm, rs, dn, nch, banks in cinfo:
                for c, ps in enumerate(banks):
                    sch.task("act", lambda e, p=ps, c=c, P=P, dn=dn, rs=rs:
                             e.activation(
                        P.ap[:, c * 512:(c + 1) * 512], p.ap, AF.Exp,
                        bias=dn.ap, accum_out=rs.ap[:, c:c + 1]),
                             reads=[ps, dn], writes=[P, rs])
            for tqc, P, nm, rs, dn, nch, banks in cinfo:
                sch.task("dve", lambda e, n=nch, rs=rs, dn=dn: e.reduce_sum(
                    dn.ap, rs.ap[:, 0:n], axis=AX.X),
                         reads=[rs], writes=[dn])
                sch.task("dve", lambda e, dn=dn: e.reciprocal(dn.ap, dn.ap),
                         reads=[dn], writes=[dn])
                if layer == 0:
                    sch.task("dve", lambda e, dn=dn: e.tensor_scalar_mul(
                        dn.ap, dn.ap, ctx.nu_bc.ap),
                             reads=[dn, ctx.nu_bc], writes=[dn])
                sch.task("dve", lambda e, n=nch, P=P, dn=dn: e.tensor_scalar_mul(
                    P.ap[:, 0:n * 512], P.ap[:, 0:n * 512], dn.ap),
                         reads=[P, dn], writes=[P])
                sch.task(QP, E_dma(
                    d["PD"][par, tqc * 128:(tqc + 1) * 128, 0:nch * 512],
                    P.ap[:, 0:nch * 512]),
                         reads=[P], writes=[ctx.pd_reg[par][tqc]], dma=QP)
        return s_pair

    def emit_PV_group(h, g4):
        par = h % 2
        g = h // rep
        vbuf = vt[g % 2]
        ao = att[g4 % 2]
        ntk = g4 * 4 + 4
        psum = ps_take(ctx)
        for tkc in range(ntk):
            ptb = pt[tkc % 2]
            regs = [ctx.pd_reg[par][g4 * 4 + i] for i in range(4)]
            tload(ctx, ptb, ptb.ap,
                  d["PD"][par, g4 * 512:(g4 + 1) * 512,
                          tkc * 128:(tkc + 1) * 128], regs)
            sch.task("pe", lambda e, p=psum, v=vbuf, tkc=tkc, b=ptb,
                     st=(tkc == 0), sp=(tkc == ntk - 1):
                     e.matmul(p.ap, v.ap[:, tkc * 128:(tkc + 1) * 128],
                              b.ap, start=st, stop=sp),
                     reads=[vbuf, ptb], writes=[psum])
        sch.task("act", lambda e, p=psum, ao=ao: e.activation(
            ao.ap, p.ap, AF.Copy),
                 reads=[psum], writes=[ao])
        sch.task(QP, E_dma(d[ON][h * 128:(h + 1) * 128,
                                 g4 * 512:(g4 + 1) * 512], ao.ap),
                 reads=[ao], writes=dbx(ctx, ON, h * 128, (h + 1) * 128),
                 dma=QP)

    for h in range(NH):
        s_pair = emit_S(h)
        for pi in range(8):
            s_pair((2 * pi, 2 * pi + 1))
            if h > 0 and pi % 2 == 1:
                emit_PV_group(h - 1, pi // 2)
        if hooks and h in hooks:
            hooks[h]()
    for g4 in range(4):
        emit_PV_group(NH - 1, g4)



def phase_ffn0(ctx):
    sch, d, db, ar = ctx.sch, ctx.d, ctx.db, ctx.arena
    ar.reset()
    gsl = [ar.alloc([128, 16 * 256], BF16, "gsl") for _ in range(2)]
    usl = [ar.alloc([128, 16 * 256], BF16, "usl") for _ in range(2)]
    hb = [ar.alloc([128, 512], BF16, f"hb{i}") for i in range(NI0)]
    dslb = [ar.alloc([128, 11 * 512], BF16, "dslb") for _ in range(2)]
    sgt = [ar.alloc([128, 512], BF16, "sgt") for _ in range(2)]
    sut = [ar.alloc([128, 512], BF16, "sut") for _ in range(2)]
    tmpf = [ar.alloc([128, 512], F32, "tmpf") for _ in range(2)]
    gt = [ar.alloc([128, 512], BF16, "gt") for _ in range(2)]
    dsbc = ar.alloc([128, C], F32, "dsbc")
    sch.task("pool", E_dma(dsbc.ap, d["ds_row"].partition_broadcast(128)),
             writes=[dsbc], dma="pool")

    f_loaded = set()

    def f_gu_load(t4, icg):
        if (t4, icg) in f_loaded:
            return
        f_loaded.add((t4, icg))
        i0 = icg * 256
        gb, ub = gsl[icg % 2], usl[icg % 2]
        sch.task("pool", E_dma_gp(gb.ap, d["gwT"][:, i0:i0 + 256], 256),
                 writes=[gb], dma="pool")
        sch.task("pool", E_dma_gp(ub.ap, d["uwT"][:, i0:i0 + 256], 256),
                 writes=[ub], dma="pool")

    for t4 in range(4):
        t0 = t4 * 512
        for icg in range(22):
            i0 = icg * 256
            gb, ub = gsl[icg % 2], usl[icg % 2]
            f_gu_load(t4, icg)
            for j in range(2):
                ic = icg * 2 + j
                pg = ps_take(ctx)
                pu = ps_take(ctx)

                def mme(e, wb=gb, j=j, p=pg, t0=t0):
                    for cc in range(NC):
                        inst = e.matmul(
                            p.ap,
                            wb.ap[:, cc * 256 + j * 128: cc * 256 + (j + 1) * 128],
                            ctx.XNF[cc].ap[:, t0:t0 + 512],
                            start=(cc == 0), stop=(cc == NC - 1))
                    return inst
                sch.task("pe", lambda e, f=mme: f(e),
                         reads=ctx.XNF + [gb], writes=[pg])
                sch.task("pe", lambda e, wb=ub, j=j, p=pu, f=mme: f(e, wb, j, p),
                         reads=ctx.XNF + [ub], writes=[pu])
                sg, su = sgt[ic % 2], sut[ic % 2]
                sch.task("act", lambda e, p=pg, ic=ic, o=sg: e.activation(
                    o.ap, p.ap, AF.Silu, scale=ctx.gs.ap[:, ic:ic + 1]),
                         reads=[pg, ctx.gs], writes=[sg])
                sch.task("act", lambda e, p=pu, ic=ic, o=su: e.activation(
                    o.ap, p.ap, AF.Copy, scale=ctx.us.ap[:, ic:ic + 1]),
                         reads=[pu, ctx.us], writes=[su])
                sch.task("dve", lambda e, a=sg, b=su, o=hb[ic]: e.tensor_tensor(
                    o.ap, a.ap, b.ap, ALU.mult),
                         reads=[sg, su], writes=[hb[ic]])
        if t4 + 1 < 4:
            f_gu_load(t4 + 1, 0)
            f_gu_load(t4 + 1, 1)
        for co4 in range(4):
            co0 = co4 * 512
            psds = [ps_take(ctx) for _ in range(4)]
            for qr in range(4):
                r0 = qr * 11 * 128
                wb = dslb[qr % 2]
                sch.task("pool", E_dma_gp(
                    wb.ap, d["dwT"][r0:r0 + 11 * 128, co0:co0 + 512], 512),
                         writes=[wb], dma="pool")
                for tci in range(4):
                    def mmd(e, tci=tci, p=psds[tci], wb=wb, qr=qr):
                        for k in range(11):
                            ic = qr * 11 + k
                            inst = e.matmul(
                                p.ap,
                                hb[ic].ap[:, tci * 128:(tci + 1) * 128],
                                wb.ap[:, k * 512:(k + 1) * 512],
                                start=(qr == 0 and k == 0),
                                stop=(qr == 3 and k == 10))
                        return inst
                    sch.task("pe", mmd,
                             reads=hb[qr * 11:(qr + 1) * 11] + [wb],
                             writes=[psds[tci]])
            for tci in range(4):
                psd = psds[tci]
                tf, g = tmpf[tci % 2], gt[tci % 2]
                sch.task("dve", lambda e, p=psd, o=tf, co0=co0: e.tensor_tensor(
                    o.ap, p.ap, dsbc.ap[:, co0:co0 + 512], ALU.mult),
                         reads=[psd, dsbc], writes=[tf])
                sch.task("act", lambda e, a=tf, o=g: e.activation(
                    o.ap, a.ap, AF.Silu),
                         reads=[tf], writes=[g])
                tc = t4 * 4 + tci
                sch.task(QP, E_dma(
                    d["GTM"][tc * 128:(tc + 1) * 128, co0:co0 + 512], g.ap),
                         reads=[g],
                         writes=dbx(ctx, "GTM", tc * 128, (tc + 1) * 128),
                         dma=QP)



def phase_combine0(ctx):
    sch, d, db, ar = ctx.sch, ctx.d, ctx.db, ctx.arena
    ar.reset()
    xt = [ar.alloc([128, C], F32, "c0x") for _ in range(2)]
    gt = [ar.alloc([128, C], BF16, "c0g") for _ in range(2)]
    st = [ar.alloc([128, C], BF16, "c0s") for _ in range(2)]
    sg = [ar.alloc([128, C], BF16, "c0sg") for _ in range(2)]
    nb = [ar.alloc([128, C], BF16, "c0n") for _ in range(2)]
    sm = [dict(s1=ar.alloc([128, 1], F32, "s1"), ss=ar.alloc([128, 1], F32, "ss"),
               m=ar.alloc([128, 1], F32, "m"), v=ar.alloc([128, 1], F32, "v"),
               rstd=ar.alloc([128, 1], F32, "rstd"),
               nmr=ar.alloc([128, 1], F32, "nmr"),
               junk=ar.alloc([128, C], BF16, "junk")) for _ in range(2)]
    for tc in range(NT):
        t0 = tc * 128
        xb, g, s, p, nbb, smm = (xt[tc % 2], gt[tc % 2], st[tc % 2],
                                 sg[tc % 2], nb[tc % 2], sm[tc % 2])
        sch.task(QP, E_dma(xb.ap, d["x"][t0:t0 + 128, :]),
                 writes=[xb], dma=QP)
        sch.task(QP, E_dma(g.ap, d["GTM"][t0:t0 + 128, :]),
                 reads=dbx(ctx, "GTM", t0, t0 + 128), writes=[g], dma=QP)
        for c4 in range(4):
            tload(ctx, s, s.ap[:, c4 * 512:(c4 + 1) * 512],
                  d["SFM"][c4 * 512:(c4 + 1) * 512, t0:t0 + 128],
                  dbx(ctx, "SFM", c4 * 512, (c4 + 1) * 512))
        sch.task("dve", lambda e, a=s, b=g, o=p: e.tensor_tensor(
            o.ap, a.ap, b.ap, ALU.mult), reads=[s, g], writes=[p])
        sch.task("dve", lambda e, a=xb, b=p: e.tensor_tensor(
            a.ap, a.ap, b.ap, ALU.add), reads=[xb, p], writes=[xb])
        sch.task(QP, E_dma(d["X1"][t0:t0 + 128, :], xb.ap),
                 reads=[xb], writes=db["X1"], dma=QP)
        s1, ss, m, v, rstd, nmr, junk = (smm[k] for k in
                                         ("s1", "ss", "m", "v", "rstd", "nmr", "junk"))
        sch.task("dve", lambda e, a=xb, o=s1: e.reduce_sum(o.ap, a.ap, axis=AX.X),
                 reads=[xb], writes=[s1])
        sch.task("act", lambda e, a=xb, j=junk, o=ss: e.activation(
            j.ap, a.ap, AF.Square, accum_out=o.ap),
                 reads=[xb], writes=[junk, ss])
        sch.task("dve", lambda e, a=s1, o=m: e.tensor_scalar_mul(o.ap, a.ap, 1.0 / C),
                 reads=[s1], writes=[m])
        sch.task("dve", lambda e, a=m, o=v: e.scalar_tensor_tensor(
            o.ap, a.ap, 1.0, a.ap, ALU.mult, ALU.mult), reads=[m], writes=[v])
        sch.task("dve", lambda e, a=ss, o=v: e.scalar_tensor_tensor(
            o.ap, a.ap, 1.0 / C, o.ap, ALU.mult, ALU.subtract),
                 reads=[ss, v], writes=[v])
        sch.task("act", lambda e, a=v, o=rstd: e.activation(
            o.ap, a.ap, AF.Ln, bias=ctx.c_eps5.ap),
                 reads=[v, ctx.c_eps5], writes=[rstd])
        sch.task("act", lambda e, o=rstd: e.activation(
            o.ap, o.ap, AF.Exp, scale=-0.5), reads=[rstd], writes=[rstd])
        sch.task("dve", lambda e, a=m, b=rstd, o=nmr: e.scalar_tensor_tensor(
            o.ap, a.ap, -1.0, b.ap, ALU.mult, ALU.mult),
                 reads=[m, rstd], writes=[nmr])
        sch.task("dve", lambda e, a=xb, b=rstd, c=nmr, o=nbb: e.tensor_scalar(
            o.ap, a.ap, b.ap, c.ap, ALU.mult, ALU.add),
                 reads=[xb, rstd, nmr], writes=[nbb])
        sch.task(QP, E_dma(d["XN2"][t0:t0 + 128, :], nbb.ap),
                 reads=[nbb], writes=dbx(ctx, "XN2", t0, t0 + 128), dma=QP)



def phase_rms_attn(ctx):
    sch, d, db, ar = ctx.sch, ctx.d, ctx.db, ctx.arena
    ar.reset()
    anw = ar.alloc([128, C], F32, "anwbc")
    sch.task("pool", E_dma(anw.ap, d["anw"].partition_broadcast(128)),
             writes=[anw], dma="pool")
    xb = [ar.alloc([128, C], BF16, "p5x") for _ in range(2)]
    hb = [ar.alloc([128, C], BF16, "p5h") for _ in range(2)]
    sm = [dict(ss=ar.alloc([128, 1], F32, "ss"),
               rstd=ar.alloc([128, 1], F32, "rstd"),
               junk=ar.alloc([128, C], BF16, "junk"),
               t1=ar.alloc([128, C], BF16, "t1")) for _ in range(2)]
    for tc in range(NT):
        t0 = tc * 128
        a, o, s = xb[tc % 2], hb[tc % 2], sm[tc % 2]
        sch.task(QP, E_dma(a.ap, d["XN2"][t0:t0 + 128, :]),
                 reads=dbx(ctx, "XN2", t0, t0 + 128), writes=[a], dma=QP)
        emit_rms(ctx, s, a, a.ap, anw, o, o.ap)
        sch.task(QP, E_dma(d["HT"][t0:t0 + 128, :], o.ap),
                 reads=[o], writes=dbx(ctx, "HT", t0, t0 + 128), dma=QP)



def phase_qkv1(ctx):
    sch, d, db, ar = ctx.sch, ctx.d, ctx.db, ctx.arena
    ar.reset()
    fill_xnf(ctx, "HT")
    W = [ar.alloc([128, C], BF16, f"w1_{i}") for i in range(NC)]
    kw = [ar.alloc([128, 512], BF16, f"kw{i}") for i in range(NC)]
    vw = [ar.alloc([128, 512], BF16, f"vw{i}") for i in range(NC)]
    qh = [ar.alloc([128, C], BF16, "q1h") for _ in range(2)]
    kh = [ar.alloc([128, 512], BF16, "k1h") for _ in range(2)]
    vh = [ar.alloc([128, 512], BF16, "v1h") for _ in range(2)]
    ssl = [ar.alloc([128, 16], F32, "ssl1") for _ in range(2)]
    rstd = [ar.alloc([128, 16], F32, "rstd1") for _ in range(2)]
    junk = [ar.alloc([128, 128], BF16, "jk1") for _ in range(2)]
    qn = [ar.alloc([128, 128], F32, "qn") for _ in range(2)]
    ma = [ar.alloc([128, 64], F32, "ma") for _ in range(2)]
    mb = [ar.alloc([128, 64], F32, "mb") for _ in range(2)]
    cw1q = ar.alloc([128, 64], F32, "cw1q"); sw2q = ar.alloc([128, 64], F32, "sw2q")
    cw2q = ar.alloc([128, 64], F32, "cw2q"); sw1q = ar.alloc([128, 64], F32, "sw1q")
    cw1k = ar.alloc([128, 64], F32, "cw1k"); sw2k = ar.alloc([128, 64], F32, "sw2k")
    cw2k = ar.alloc([128, 64], F32, "cw2k"); sw1k = ar.alloc([128, 64], F32, "sw1k")

    for cc in range(NC):
        sch.task("pool", E_dma(W[cc].ap, d["qwT"][cc * 128:(cc + 1) * 128, :]),
                 writes=[W[cc]], dma="pool")
        sch.task("pool", E_dma(kw[cc].ap, d["kwT"][cc * 128:(cc + 1) * 128, :]),
                 writes=[kw[cc]], dma="pool")
        sch.task("pool", E_dma(vw[cc].ap, d["vwT"][cc * 128:(cc + 1) * 128, :]),
                 writes=[vw[cc]], dma="pool")

    def rope_tabs(tc, w_bc, c1, s2, c2, s1):
        f0 = tc * 64
        sch.task("dve", lambda e: e.tensor_tensor(
            c1.ap, ctx.rc.ap[:, f0:f0 + 64], w_bc.ap[:, 0:64], ALU.mult),
                 reads=[ctx.rc, w_bc], writes=[c1])
        sch.task("dve", lambda e: e.tensor_tensor(
            s2.ap, ctx.rs.ap[:, f0:f0 + 64], w_bc.ap[:, 64:128], ALU.mult),
                 reads=[ctx.rs, w_bc], writes=[s2])
        sch.task("dve", lambda e: e.tensor_tensor(
            c2.ap, ctx.rc.ap[:, f0:f0 + 64], w_bc.ap[:, 64:128], ALU.mult),
                 reads=[ctx.rc, w_bc], writes=[c2])
        sch.task("dve", lambda e: e.tensor_tensor(
            s1.ap, ctx.rs.ap[:, f0:f0 + 64], w_bc.ap[:, 0:64], ALU.mult),
                 reads=[ctx.rs, w_bc], writes=[s1])

    def rstd_batch(sl, rr, n, extra_scale):
        sch.task("act", lambda e: e.activation(
            rr.ap[:, 0:n], sl.ap[:, 0:n], AF.Ln, scale=1.0 / HD,
            bias=ctx.c_eps6.ap),
                 reads=[sl, ctx.c_eps6], writes=[rr])
        sch.task("act", lambda e: e.activation(
            rr.ap[:, 0:n], rr.ap[:, 0:n], AF.Exp, scale=-0.5),
                 reads=[rr], writes=[rr])
        if extra_scale != 1.0:
            sch.task("dve", lambda e: e.tensor_scalar_mul(
                rr.ap[:, 0:n], rr.ap[:, 0:n], float(extra_scale)),
                     reads=[rr], writes=[rr])

    for tc in range(NT):
        t0 = tc * 128
        par = tc % 2
        rope_tabs(tc, ctx.qw_bc, cw1q, sw2q, cw2q, sw1q)
        rope_tabs(tc, ctx.kw_bc, cw1k, sw2k, cw2k, sw1k)
        ps = [ps_take(ctx) for _ in range(4)]
        for c4 in range(4):
            def mme(e, tc=tc, c4=c4, p=ps[c4]):
                for cc in range(NC):
                    inst = e.matmul(
                        p.ap, ctx.XNF[cc].ap[:, tc * 128:(tc + 1) * 128],
                        W[cc].ap[:, c4 * 512:(c4 + 1) * 512],
                        start=(cc == 0), stop=(cc == NC - 1))
                return inst
            sch.task("pe", mme, reads=ctx.XNF + W, writes=[ps[c4]])
        sl, rr = ssl[par], rstd[par]
        jk = junk[par]
        for c4 in range(4):
            for j in range(4):
                sch.task("act", lambda e, p=ps[c4], j=j, h=c4 * 4 + j,
                         jk=jk, sl=sl: e.activation(
                    jk.ap, p.ap[:, j * 128:(j + 1) * 128], AF.Square,
                    accum_out=sl.ap[:, h:h + 1]),
                         reads=[ps[c4]], writes=[jk, sl])
        rstd_batch(sl, rr, 16, ISQ)
        q, a, b, oh = qn[par], ma[par], mb[par], qh[par]
        for c4 in range(4):
            for j in range(4):
                hidx = c4 * 4 + j
                sch.task("dve", lambda e, p=ps[c4], j=j, h=hidx, q=q, rr=rr:
                         e.tensor_scalar_mul(
                             q.ap, p.ap[:, j * 128:(j + 1) * 128],
                             rr.ap[:, h:h + 1]),
                         reads=[ps[c4], rr], writes=[q])
                off = hidx * 128
                sch.task("dve", lambda e, a=a, q=q: e.tensor_tensor(
                    a.ap, q.ap[:, 0:64], cw1q.ap, ALU.mult),
                         reads=[q, cw1q], writes=[a])
                sch.task("dve", lambda e, b=b, q=q: e.tensor_tensor(
                    b.ap, q.ap[:, 64:128], sw2q.ap, ALU.mult),
                         reads=[q, sw2q], writes=[b])
                sch.task("dve", lambda e, o=off, a=a, b=b, oh=oh: e.tensor_tensor(
                    oh.ap[:, o:o + 64], a.ap, b.ap, ALU.subtract),
                         reads=[a, b], writes=[oh])
                sch.task("dve", lambda e, a=a, q=q: e.tensor_tensor(
                    a.ap, q.ap[:, 64:128], cw2q.ap, ALU.mult),
                         reads=[q, cw2q], writes=[a])
                sch.task("dve", lambda e, b=b, q=q: e.tensor_tensor(
                    b.ap, q.ap[:, 0:64], sw1q.ap, ALU.mult),
                         reads=[q, sw1q], writes=[b])
                sch.task("dve", lambda e, o=off, a=a, b=b, oh=oh: e.tensor_tensor(
                    oh.ap[:, o + 64:o + 128], a.ap, b.ap, ALU.add),
                         reads=[a, b], writes=[oh])
        sch.task(QP, E_dma(d["Q1"][t0:t0 + 128, :], oh.ap),
                 reads=[oh], writes=db["Q1"], dma=QP)
        pk = ps_take(ctx)
        pv = ps_take(ctx)
        for p, wws in ((pk, kw), (pv, vw)):
            def mme2(e, p=p, wws=wws, tc=tc):
                for cc in range(NC):
                    inst = e.matmul(
                        p.ap, ctx.XNF[cc].ap[:, tc * 128:(tc + 1) * 128],
                        wws[cc].ap, start=(cc == 0), stop=(cc == NC - 1))
                return inst
            sch.task("pe", mme2, reads=ctx.XNF + wws, writes=[p])
        for j in range(4):
            sch.task("act", lambda e, p=pk, j=j, jk=jk, sl=sl: e.activation(
                jk.ap, p.ap[:, j * 128:(j + 1) * 128], AF.Square,
                accum_out=sl.ap[:, j:j + 1]),
                     reads=[pk], writes=[jk, sl])
        rstd_batch(sl, rr, 4, 1.0)
        ok = kh[par]
        for j in range(4):
            sch.task("dve", lambda e, p=pk, j=j, q=q, rr=rr: e.tensor_scalar_mul(
                q.ap, p.ap[:, j * 128:(j + 1) * 128], rr.ap[:, j:j + 1]),
                     reads=[pk, rr], writes=[q])
            off = j * 128
            sch.task("dve", lambda e, a=a, q=q: e.tensor_tensor(
                a.ap, q.ap[:, 0:64], cw1k.ap, ALU.mult),
                     reads=[q, cw1k], writes=[a])
            sch.task("dve", lambda e, b=b, q=q: e.tensor_tensor(
                b.ap, q.ap[:, 64:128], sw2k.ap, ALU.mult),
                     reads=[q, sw2k], writes=[b])
            sch.task("dve", lambda e, o=off, a=a, b=b, ok=ok: e.tensor_tensor(
                ok.ap[:, o:o + 64], a.ap, b.ap, ALU.subtract),
                     reads=[a, b], writes=[ok])
            sch.task("dve", lambda e, a=a, q=q: e.tensor_tensor(
                a.ap, q.ap[:, 64:128], cw2k.ap, ALU.mult),
                     reads=[q, cw2k], writes=[a])
            sch.task("dve", lambda e, b=b, q=q: e.tensor_tensor(
                b.ap, q.ap[:, 0:64], sw1k.ap, ALU.mult),
                     reads=[q, sw1k], writes=[b])
            sch.task("dve", lambda e, o=off, a=a, b=b, ok=ok: e.tensor_tensor(
                ok.ap[:, o + 64:o + 128], a.ap, b.ap, ALU.add),
                     reads=[a, b], writes=[ok])
        sch.task(QP, E_dma(d["K1"][t0:t0 + 128, :], ok.ap),
                 reads=[ok], writes=db["K1"], dma=QP)
        sch.task("act", lambda e, p=pv, o=vh[par]: e.activation(
            o.ap, p.ap, AF.Copy), reads=[pv], writes=[vh[par]])
        sch.task(QP, E_dma(d["V1"][t0:t0 + 128, :], vh[par].ap),
                 reads=[vh[par]], writes=db["V1"], dma=QP)



def phase_oproj(ctx):
    sch, d, db, ar = ctx.sch, ctx.d, ctx.db, ctx.arena
    ar.reset()
    for hc in range(NC):
        sch.task(QP, E_dma(ctx.XNF[hc].ap, d["AFM"][hc * 128:(hc + 1) * 128, :]),
                 reads=dbx(ctx, "AFM", hc * 128, (hc + 1) * 128),
                 writes=[ctx.XNF[hc]], dma=QP)
    W = [ar.alloc([128, C], BF16, f"ow{i}") for i in range(NC)]
    for cc in range(NC):
        sch.task("pool", E_dma(W[cc].ap, d["owT"][cc * 128:(cc + 1) * 128, :]),
                 writes=[W[cc]], dma="pool")
    fnw = ar.alloc([128, C], F32, "fnwbc")
    sch.task("pool", E_dma(fnw.ap, d["fnw"].partition_broadcast(128)),
             writes=[fnw], dma="pool")
    xt = [ar.alloc([128, C], BF16, "p8x") for _ in range(2)]
    hid = [ar.alloc([128, C], BF16, "p8h") for _ in range(2)]
    h2 = [ar.alloc([128, C], BF16, "p8h2") for _ in range(2)]
    sm = [dict(ss=ar.alloc([128, 1], F32, "ss"),
               rstd=ar.alloc([128, 1], F32, "rstd"),
               junk=ar.alloc([128, C], BF16, "junk"),
               t1=ar.alloc([128, C], BF16, "t1")) for _ in range(2)]
    for tc in range(NT):
        t0 = tc * 128
        par = tc % 2
        xb, hh, h2b, s = xt[par], hid[par], h2[par], sm[par]
        sch.task(QP, E_dma(xb.ap, d["XN2"][t0:t0 + 128, :]),
                 reads=dbx(ctx, "XN2", t0, t0 + 128), writes=[xb], dma=QP)
        for c4 in range(4):
            p = ps_take(ctx)

            def mme(e, tc=tc, c4=c4, p=p):
                for hc in range(NC):
                    inst = e.matmul(
                        p.ap, ctx.XNF[hc].ap[:, tc * 128:(tc + 1) * 128],
                        W[hc].ap[:, c4 * 512:(c4 + 1) * 512],
                        start=(hc == 0), stop=(hc == NC - 1))
                return inst
            sch.task("pe", mme, reads=ctx.XNF + W, writes=[p])
            sch.task("dve", lambda e, p=p, c4=c4, a=xb, o=hh: e.tensor_tensor(
                o.ap[:, c4 * 512:(c4 + 1) * 512],
                a.ap[:, c4 * 512:(c4 + 1) * 512], p.ap, ALU.add),
                     reads=[p, xb], writes=[hh])
        sch.task(QP, E_dma(d["HID"][t0:t0 + 128, :], hh.ap),
                 reads=[hh], writes=db["HID"], dma=QP)
        emit_rms(ctx, s, hh, hh.ap, fnw, h2b, h2b.ap)
        sch.task(QP, E_dma(d["H2"][t0:t0 + 128, :], h2b.ap),
                 reads=[h2b], writes=dbx(ctx, "H2", t0, t0 + 128), dma=QP)



def scan_setup(ctx):
    sch, d, db, ar = ctx.sch, ctx.d, ctx.db, ctx.arena
    st = Ctx()
    xp = ar.alloc([128, NC * 160], BF16, "xpT")
    dt = ar.alloc([128, C], BF16, "dtT")
    sch.task("pool", E_dma_gp(xp.ap, d["xpT"], 160), writes=[xp], dma="pool")
    sch.task("pool", E_dma(dt.ap, d["dtT"]), writes=[dt], dma="pool")
    sch.task("dve", lambda e: e.memset(ctx.hst.ap, 0.0), writes=[ctx.hst])

    st.xp, st.dt = xp, dt
    st.u = [ar.alloc([128, 512], BF16, f"u{i}") for i in range(2)]
    st.u2 = [ar.alloc([128, 512], BF16, "u2_0")]
    st.dl = [ar.alloc([128, 512], F32, f"dl{i}") for i in range(2)]
    st.yac = [ar.alloc([128, 512], F32, f"y{i}") for i in range(2)]
    st.Bbc = [ar.alloc([128, 512], BF16, f"Bb{i}") for i in range(DST)]
    st.Cbc = [ar.alloc([128, 512], BF16, f"Cb{i}") for i in range(DST)]
    st.dr = [ar.alloc([128, 512], BF16, "dr")]
    st.dA = [ar.alloc([128, 512], F32, "dA") for _ in range(3)]
    st.dB = [ar.alloc([128, 512], BF16, "dB") for _ in range(2)]
    st.hsc = [ar.alloc([128, 512], BF16, "hsc") for _ in range(2)]
    st.yt = [ar.alloc([128, 512], BF16, "yt") for _ in range(2)]
    st.du = [ar.alloc([128, 512], BF16, "du")]
    st.t0b = [ar.alloc([128, 512], F32, "t0b")]
    st.nab = [ar.alloc([128, 512], F32, "nab")]
    st.bcv = [ar.alloc([32, 512], F32, "bcv")]
    st.ybf = [ar.alloc([128, 512], BF16, "ybf")]
    return st


def scan_chunk_gen(ctx, st, t4):
    sch, d, db = ctx.sch, ctx.d, ctx.db
    xp, dt = st.xp, st.dt
    u, dl, yac, Bbc, Cbc = st.u, st.dl, st.yac, st.Bbc, st.Cbc
    dr, dA, dB, hsc, yt, du = st.dr, st.dA, st.dB, st.hsc, st.yt, st.du
    t0b, nab, bcv, ybf = st.t0b, st.nab, st.bcv, st.ybf
    u2 = st.u2
    if True:
        t0 = t4 * 512
        pe1 = ps_take(ctx)
        pe2 = ps_take(ctx)
        for cc in range(NC):
            ub = u[cc % 2]
            tload(ctx, ub, ub.ap,
                  d["XN2"][t0:t0 + 512, cc * 128:(cc + 1) * 128],
                  dbx(ctx, "XN2", t0, t0 + 512))
            sch.task("pe", lambda e, p=pe1, ub=ub, cc=cc: e.matmul(
                p.ap, xp.ap[:, cc * 160:cc * 160 + 128], ub.ap,
                start=(cc == 0), stop=(cc == NC - 1)),
                     reads=[ub, xp], writes=[pe1])
            sch.task("pe", lambda e, p=pe2, ub=ub, cc=cc: e.matmul(
                p.ap[0:32, :], xp.ap[:, cc * 160 + 128:cc * 160 + 160], ub.ap,
                start=(cc == 0), stop=(cc == NC - 1)),
                     reads=[ub, xp], writes=[pe2])
        drb = dr[0]
        sch.task("act", lambda e, p=pe1, o=drb: e.activation(o.ap, p.ap, AF.Copy),
                 reads=[pe1], writes=[drb])
        bc = bcv[0]
        sch.task("dve", lambda e, p=pe2, o=bc: e.tensor_copy(
            o.ap, p.ap[0:32, :]), reads=[pe2], writes=[bc])
        sch.task(QP, E_dma(d["BCR"][:, t0:t0 + 512], bc.ap),
                 reads=[bc], writes=db["BCR"], dma=QP)
        for n in range(DST):
            sch.task("pool", E_dma(
                Bbc[n].ap,
                d["BCR"][n:n + 1, t0:t0 + 512].partition_broadcast(128)),
                     reads=db["BCR"], writes=[Bbc[n]], dma="pool")
            sch.task("pool", E_dma(
                Cbc[n].ap,
                d["BCR"][DST + n:DST + n + 1, t0:t0 + 512].partition_broadcast(128)),
                     reads=db["BCR"], writes=[Cbc[n]], dma="pool")
        for dc in range(NC):
            pd = ps_take(ctx)
            sch.task("pe", lambda e, p=pd, dc=dc, b=drb: e.matmul(
                p.ap, dt.ap[:, dc * 128:(dc + 1) * 128], b.ap,
                start=True, stop=True),
                     reads=[dt, drb], writes=[pd])
            tt, nn = t0b[0], nab[0]
            sch.task("dve", lambda e, p=pd, dc=dc, o=tt: e.tensor_scalar(
                o.ap, p.ap, ctx.dtb.ap[:, dc:dc + 1], None, ALU.add),
                     reads=[pd, ctx.dtb], writes=[tt])
            sch.task("dve", lambda e, a=tt, o=nn: e.scalar_tensor_tensor(
                o.ap, a.ap, -1.0, a.ap, ALU.mult, ALU.min),
                     reads=[tt], writes=[nn])
            sch.task("act", lambda e, o=nn: e.activation(o.ap, o.ap, AF.Exp),
                     reads=[nn], writes=[nn])
            sch.task("act", lambda e, o=nn: e.activation(o.ap, o.ap, AF.Ln,
                                                         bias=1.0),
                     reads=[nn], writes=[nn])
            sch.task("dve", lambda e, a=tt, o=tt: e.tensor_single_scalar(
                o.ap, a.ap, 0.0, ALU.max),
                     reads=[tt], writes=[tt])
            sch.task("dve", lambda e, a=tt, b=nn, o=dl[dc % 2]: e.tensor_tensor(
                o.ap, a.ap, b.ap, ALU.add),
                     reads=[tt, nn], writes=[dl[dc % 2]])
            yield
            ub2 = u2[0]
            tload(ctx, ub2, ub2.ap,
                  d["XN2"][t0:t0 + 512, dc * 128:(dc + 1) * 128],
                  dbx(ctx, "XN2", t0, t0 + 512))
            sch.task("dve", lambda e, a=dl[dc % 2], b=ub2, o=du[0]:
                     e.tensor_tensor(o.ap, a.ap, b.ap, ALU.mult),
                     reads=[dl[dc % 2], ub2], writes=[du[0]])
            for n in range(DST):
                col = dc * DST + n
                da, db_, hs, yy = dA[n % 3], dB[n % 2], hsc[n % 2], yt[n % 2]
                sch.task("act", lambda e, a=dl[dc % 2], o=da, col=col: e.activation(
                    o.ap, a.ap, AF.Exp, scale=ctx.A_sb.ap[:, col:col + 1]),
                         reads=[dl[dc % 2], ctx.A_sb], writes=[da])
                sch.task("dve", lambda e, a=du[0], b=Bbc[n], o=db_:
                         e.tensor_tensor(o.ap, a.ap, b.ap, ALU.mult),
                         reads=[du[0], Bbc[n]], writes=[db_])
                sch.task("dve", lambda e, a=da, b=db_, o=hs, col=col:
                         e.tensor_tensor_scan(o.ap, a.ap, b.ap,
                                              ctx.hst.ap[:, col:col + 1],
                                              ALU.mult, ALU.add),
                         reads=[da, db_, ctx.hst], writes=[hs])
                sch.task("dve", lambda e, a=hs, col=col: e.tensor_copy(
                    ctx.hst.ap[:, col:col + 1], a.ap[:, 511:512]),
                         reads=[hs], writes=[ctx.hst])
                sch.task("dve", lambda e, a=hs, b=Cbc[n], o=yy:
                         e.tensor_tensor(o.ap, a.ap, b.ap, ALU.mult),
                         reads=[hs, Cbc[n]], writes=[yy])
                if n == 0:
                    sch.task("pool", lambda e, a=yy, o=yac[dc % 2]: e.tensor_copy(
                        o.ap, a.ap), reads=[yy], writes=[yac[dc % 2]])
                else:
                    sch.task("pool", lambda e, a=yy, o=yac[dc % 2]: e.tensor_tensor(
                        o.ap, o.ap, a.ap, ALU.add),
                             reads=[yy, yac[dc % 2]], writes=[yac[dc % 2]])
                yield
            sch.task("dve", lambda e, a=ub2, o=yac[dc % 2], dc=dc:
                     e.scalar_tensor_tensor(o.ap, a.ap,
                                            ctx.D_sb.ap[:, dc:dc + 1],
                                            o.ap, ALU.mult, ALU.add),
                     reads=[ub2, ctx.D_sb, yac[dc % 2]], writes=[yac[dc % 2]])
            yb = ybf[0]
            sch.task("dve", lambda e, a=yac[dc % 2], o=yb: e.tensor_copy(o.ap, a.ap),
                     reads=[yac[dc % 2]], writes=[yb])
            sch.task(QP, E_dma(d["Y"][dc * 128:(dc + 1) * 128, t0:t0 + 512],
                                 yb.ap),
                     reads=[yb], writes=db["Y"], dma=QP)



def phase_moe(ctx):
    sch, d, db, ar = ctx.sch, ctx.d, ctx.db, ctx.arena
    ar.reset()
    fill_xnf(ctx, "H2")
    st = scan_setup(ctx)
    gsl = [ar.alloc([128, NC * 128], BF16, "mgsl") for _ in range(2)]
    usl = [ar.alloc([128, NC * 128], BF16, "musl") for _ in range(2)]
    hb = [ar.alloc([128, 512], BF16, f"mhb{i}") for i in range(NI1)]
    dslb = [ar.alloc([128, 6 * 512], BF16, "mdsl") for _ in range(2)]
    sgt = [ar.alloc([128, 512], BF16, "msg") for _ in range(2)]
    sut = [ar.alloc([128, 512], BF16, "msu") for _ in range(2)]
    mtmp = [ar.alloc([128, 512], F32, "mtmp") for _ in range(2)]
    r_m1 = ar.alloc([128, 1], F32, "rm1"); r_m2 = ar.alloc([128, 1], F32, "rm2")
    r_tr = ar.alloc([128, 8], F32, "rtr")
    r_p = ar.alloc([128, 8], F32, "rp"); r_sg = ar.alloc([128, 8], F32, "rsg")
    r_g2 = r_sg
    r_ms = ar.alloc([128, 8], F32, "rms"); r_s = ar.alloc([128, 1], F32, "rS")
    for tc in range(NT):
        pl = ps_take(ctx)

        def mmr(e, tc=tc, p=pl):
            for cc in range(NC):
                inst = e.matmul(p.ap[:, 0:8],
                                ctx.XNF[cc].ap[:, tc * 128:(tc + 1) * 128],
                                ctx.rw_sb.ap[:, cc * 8:(cc + 1) * 8],
                                start=(cc == 0), stop=(cc == NC - 1))
            return inst
        sch.task("pe", mmr, reads=ctx.XNF + [ctx.rw_sb], writes=[pl])
        sch.task("dve", lambda e, p=pl: e.reduce_max(
            r_m1.ap, p.ap[:, 0:8], axis=AX.X, negate=True),
                 reads=[pl], writes=[r_m1])
        sch.task("act", lambda e, p=pl: e.activation(
            r_p.ap, p.ap[:, 0:8], AF.Exp, bias=r_m1.ap, accum_out=r_s.ap),
                 reads=[pl, r_m1], writes=[r_p, r_s])
        sch.task("dve", lambda e: e.reciprocal(r_s.ap, r_s.ap),
                 reads=[r_s], writes=[r_s])
        sch.task("dve", lambda e: e.tensor_scalar_mul(r_p.ap, r_p.ap, r_s.ap),
                 reads=[r_p, r_s], writes=[r_p])
        sch.task("dve", lambda e: e.reduce_max(r_m1.ap, r_p.ap, axis=AX.X),
                 reads=[r_p], writes=[r_m1])
        sch.task("dve", lambda e: e.memset(r_tr.ap, -1e38), writes=[r_tr])
        sch.task("dve", lambda e: e.tensor_copy(r_tr.ap[:, 0:1], r_m1.ap),
                 reads=[r_m1], writes=[r_tr])
        sch.task("dve", lambda e: e.match_replace(r_g2.ap, r_tr.ap, r_p.ap, -1e30),
                 reads=[r_tr, r_p], writes=[r_g2])
        sch.task("dve", lambda e: e.reduce_max(r_m2.ap, r_g2.ap, axis=AX.X),
                 reads=[r_g2], writes=[r_m2])
        sch.task("act", lambda e: e.activation(r_sg.ap, r_p.ap, AF.Sign,
                                               scale=-1.0, bias=r_m2.ap),
                 reads=[r_p, r_m2], writes=[r_sg])
        sch.task("dve", lambda e: e.tensor_single_scalar(
            r_ms.ap, r_sg.ap, 0.0, ALU.max), reads=[r_sg], writes=[r_ms])
        sch.task("dve", lambda e: e.tensor_scalar(
            r_ms.ap, r_ms.ap, -1.0, 1.0, ALU.mult, ALU.add),
                 reads=[r_ms], writes=[r_ms])
        sch.task("dve", lambda e: e.tensor_tensor(
            r_m1.ap, r_m1.ap, r_m2.ap, ALU.add), reads=[r_m1, r_m2], writes=[r_m1])
        sch.task("dve", lambda e: e.tensor_scalar_add(r_m1.ap, r_m1.ap, 1e-9),
                 reads=[r_m1], writes=[r_m1])
        sch.task("dve", lambda e: e.reciprocal(r_m1.ap, r_m1.ap),
                 reads=[r_m1], writes=[r_m1])
        sch.task("dve", lambda e, tc=tc: e.scalar_tensor_tensor(
            ctx.comb[tc].ap, r_p.ap, r_m1.ap, r_ms.ap, ALU.mult, ALU.mult),
                 reads=[r_p, r_m1, r_ms], writes=[ctx.comb[tc]])

    gu_loaded = set()

    def gu_load(t4, e_i, ic):
        key = (t4, e_i, ic)
        if key in gu_loaded:
            return
        gu_loaded.add(key)
        i0 = ic * 128
        gb, ub = gsl[ic % 2], usl[ic % 2]
        sch.task("pool", E_dma_gp(gb.ap, d["wgT"][e_i, :, i0:i0 + 128], 128),
                 writes=[gb], dma="pool")
        sch.task("pool", E_dma_gp(ub.ap, d["wuT"][e_i, :, i0:i0 + 128], 128),
                 writes=[ub], dma="pool")

    def moe_gen():
        for t4 in range(4):
            t0 = t4 * 512
            for e_i in range(NE):
                for ic in range(NI1):
                    gu_load(t4, e_i, ic)
                    gb, ub = gsl[ic % 2], usl[ic % 2]
                    pg = ps_take(ctx)
                    pu = ps_take(ctx)
                    for p, wb in ((pg, gb), (pu, ub)):
                        def mme(e, p=p, wb=wb, t0=t0):
                            for cc in range(NC):
                                inst = e.matmul(
                                    p.ap,
                                    wb.ap[:, cc * 128:(cc + 1) * 128],
                                    ctx.XNF[cc].ap[:, t0:t0 + 512],
                                    start=(cc == 0), stop=(cc == NC - 1))
                            return inst
                        sch.task("pe", mme, reads=ctx.XNF + [wb], writes=[p])
                    sg, su = sgt[ic % 2], sut[ic % 2]
                    sch.task("act", lambda e, p=pg, o=sg: e.activation(
                        o.ap, p.ap, AF.Silu), reads=[pg], writes=[sg])
                    sch.task("act", lambda e, p=pu, o=su: e.activation(
                        o.ap, p.ap, AF.Copy), reads=[pu], writes=[su])
                    sch.task("dve", lambda e, a=sg, b=su, o=hb[ic]:
                             e.tensor_tensor(o.ap, a.ap, b.ap, ALU.mult),
                             reads=[sg, su], writes=[hb[ic]])
                    if ic == NI1 - 1:
                        if e_i + 1 < NE:
                            gu_load(t4, e_i + 1, 0)
                            gu_load(t4, e_i + 1, 1)
                        elif t4 + 1 < 4:
                            gu_load(t4 + 1, 0, 0)
                            gu_load(t4 + 1, 0, 1)
                    yield
                for co4 in range(4):
                    co0 = co4 * 512
                    for half in range(2):
                        i0 = half * 6 * 128
                        ni = 6 if half == 0 else 5
                        sch.task("pool", E_dma_gp(
                            dslb[half].ap[:, 0:ni * 512],
                            d["wdT"][e_i, i0:i0 + ni * 128, co0:co0 + 512], 512),
                                 writes=[dslb[half]], dma="pool")
                    for tci in range(4):
                        tc = t4 * 4 + tci
                        psd = ps_take(ctx)

                        def mmd(e, tci=tci, p=psd, b0=dslb[0], b1=dslb[1]):
                            for ic in range(NI1):
                                half = ic // 6
                                k = ic % 6
                                wb = b0 if half == 0 else b1
                                inst = e.matmul(
                                    p.ap,
                                    hb[ic].ap[:, tci * 128:(tci + 1) * 128],
                                    wb.ap[:, k * 512:(k + 1) * 512],
                                    start=(ic == 0), stop=(ic == NI1 - 1))
                            return inst
                        sch.task("pe", mmd, reads=hb + [dslb[0], dslb[1]],
                                 writes=[psd])
                        mt = mtmp[tci % 2]
                        sch.task("act", lambda e, p=psd, tc=tc, ei=e_i, o=mt:
                                 e.activation(o.ap, p.ap, AF.Copy,
                                              scale=ctx.comb[tc].ap[:, ei:ei + 1]),
                                 reads=[psd, ctx.comb[tc]], writes=[mt])
                        if e_i == 0:
                            sch.task("pool", E_dma(
                                d["MOE"][tc * 128:(tc + 1) * 128, co0:co0 + 512],
                                mt.ap),
                                     reads=[mt], writes=db["MOE"], dma="pool")
                        else:
                            sch.task("pool", E_dma(
                                d["MOE"][tc * 128:(tc + 1) * 128, co0:co0 + 512],
                                mt.ap, accum_op=ALU.add),
                                     reads=[mt], writes=db["MOE"], dma="pool")
                        yield

    def scan_gen():
        for t4 in range(4):
            for _ in scan_chunk_gen(ctx, st, t4):
                yield

    g1, g2 = moe_gen(), scan_gen()
    alive = [g1, g2]
    while alive:
        for g in list(alive):
            try:
                next(g)
            except StopIteration:
                alive.remove(g)



def phase_final(ctx):
    sch, d, db, ar = ctx.sch, ctx.d, ctx.db, ctx.arena
    ar.reset()
    x1 = [ar.alloc([128, C], F32, "fx1") for _ in range(2)]
    hid = [ar.alloc([128, C], BF16, "fhid") for _ in range(2)]
    mo = [ar.alloc([128, C], F32, "fmo") for _ in range(2)]
    s2 = [ar.alloc([128, C], BF16, "fs2") for _ in range(2)]
    g2 = [ar.alloc([128, C], BF16, "fg2") for _ in range(2)]
    for tc in range(NT):
        t0 = tc * 128
        par = tc % 2
        xb, hh, mm, ss, gg = x1[par], hid[par], mo[par], s2[par], g2[par]
        sch.task(QP, E_dma(xb.ap, d["X1"][t0:t0 + 128, :]),
                 reads=db["X1"], writes=[xb], dma=QP)
        sch.task(QP, E_dma(hh.ap, d["HID"][t0:t0 + 128, :]),
                 reads=db["HID"], writes=[hh], dma=QP)
        sch.task(QP, E_dma(mm.ap, d["MOE"][t0:t0 + 128, :]),
                 reads=db["MOE"], writes=[mm], dma=QP)
        for c4 in range(4):
            tload(ctx, ss, ss.ap[:, c4 * 512:(c4 + 1) * 512],
                  d["Y"][c4 * 512:(c4 + 1) * 512, t0:t0 + 128], db["Y"])
        sch.task("dve", lambda e, a=hh, b=mm, o=gg: e.tensor_tensor(
            o.ap, a.ap, b.ap, ALU.add), reads=[hh, mm], writes=[gg])
        sch.task("act", lambda e, o=gg: e.activation(o.ap, o.ap, AF.Silu),
                 reads=[gg], writes=[gg])
        sch.task("dve", lambda e, a=ss, b=gg, o=ss: e.tensor_tensor(
            o.ap, a.ap, b.ap, ALU.mult), reads=[ss, gg], writes=[ss])
        sch.task("dve", lambda e, a=xb, b=ss: e.tensor_tensor(
            a.ap, a.ap, b.ap, ALU.add), reads=[xb, ss], writes=[xb])
        dep = sch.task(QP, E_dma(d["out"][t0:t0 + 128, :], xb.ap),
                       reads=[xb], dma=QP)
        ctx.stores.append(dep)



PHASES = ("ln0", "qkv0", "attn0", "ffn0", "combine0", "rms_attn", "qkv1",
          "attn1", "oproj", "scan", "moe", "final")


def build(dev_taps=(), upto=None):
    ctx = Ctx()
    ctx.nc = bass.Bass("TRN2", target_bir_lowering=False)
    ctx.sch = Sched(ctx.nc)
    _mk_dram(ctx, dev_taps)
    _mk_static(ctx)
    ctx.stores = []
    load_consts(ctx)
    def attn1_with_scan(c):
        phase_attn(c, 1)

    def scan_phase(c):
        c.arena.reset()
        stt_ = scan_setup(c)
        for t4 in range(4):
            for _ in scan_chunk_gen(c, stt_, t4):
                pass

    emit = {
        "ln0": phase_ln0,
        "qkv0": phase_qkv0,
        "attn0": lambda c: phase_attn(c, 0),
        "ffn0": phase_ffn0,
        "combine0": phase_combine0,
        "rms_attn": phase_rms_attn,
        "qkv1": phase_qkv1,
        "attn1": attn1_with_scan,
        "oproj": phase_oproj,
        "scan": lambda c: None,
        "moe": phase_moe,
        "final": phase_final,
    }
    for ph in PHASES:
        emit[ph](ctx)
        if upto is not None and ph == upto:
            break
    deps = list(ctx.stores)
    for name, bl in ctx.db.items():
        for b in bl:
            if b.writer is not None:
                deps.append(b.writer)
    ctx.sch.barrier_wait(QP, deps)
    ctx.sch.build()
    return ctx


def host_prep(inputs):
    f = lambda a: np.ascontiguousarray(a, dtype=np.float32)
    p = {}
    p["wqT"] = f(inputs["L0_attn_wq"].T)
    p["wkT"] = f(inputs["L0_attn_wk"].T)
    p["wvT"] = f(inputs["L0_attn_wv"].T)
    p["tau"] = f(np.reshape(inputs["L0_attn_tau"], (1, 1)))
    p["nu"] = f(np.reshape(inputs["L0_attn_nu"], (1, 1)))
    p["gwT"] = f(inputs["L0_ffn_gate_w"].T)
    p["uwT"] = f(inputs["L0_ffn_up_w"].T)
    p["dwT"] = f(inputs["L0_ffn_down_w"].T)
    p["gs"] = f(np.reshape(inputs["L0_ffn_gate_s"], (NI0, 128)).T)
    p["us"] = f(np.reshape(inputs["L0_ffn_up_s"], (NI0, 128)).T)
    p["ds_row"] = f(np.reshape(inputs["L0_ffn_down_s"], (1, C)))
    p["AlogR"] = f(np.reshape(inputs["L1_attn_A_log"], (NC, 128, DST))
                   .transpose(1, 0, 2).reshape(128, NC * DST))
    p["Dr"] = f(np.reshape(inputs["L1_attn_D"], (NC, 128)).T)
    p["xpT"] = f(inputs["L1_attn_x_proj_w"].T)
    p["dtT"] = f(inputs["L1_attn_dt_proj_w"].T)
    p["dtb"] = f(np.reshape(inputs["L1_attn_dt_proj_b"], (NC, 128)).T)
    p["anw"] = f(np.reshape(inputs["L1_ffn_attn_norm_w"], (1, C)))
    p["fnw"] = f(np.reshape(inputs["L1_ffn_ffn_norm_w"], (1, C)))
    p["qwT"] = f(inputs["L1_ffn_q_w"].T)
    p["kwT"] = f(inputs["L1_ffn_k_w"].T)
    p["vwT"] = f(inputs["L1_ffn_v_w"].T)
    p["owT"] = f(inputs["L1_ffn_o_w"].T)
    p["qnw"] = f(np.reshape(inputs["L1_ffn_q_norm_w"], (1, HD)))
    p["knw"] = f(np.reshape(inputs["L1_ffn_k_norm_w"], (1, HD)))
    p["rwT"] = f(inputs["L1_ffn_gate_w"].T)
    p["wgT"] = f(inputs["L1_ffn_wg"].transpose(0, 2, 1))
    p["wuT"] = f(inputs["L1_ffn_wu"].transpose(0, 2, 1))
    p["wdT"] = f(inputs["L1_ffn_wd"].transpose(0, 2, 1))
    freqs = 1.0 / (1000000.0 ** (np.arange(0, HD, 2, dtype=np.float32) / HD))
    tt = np.arange(T, dtype=np.float32)
    fr = np.outer(tt, freqs)
    p["rcos"] = np.cos(fr).astype(np.float32)
    p["rsin"] = np.sin(fr).astype(np.float32)
    cm = np.zeros((4, 128, 512), np.float32)
    for v in range(4):
        for pp in range(128):
            cm[v, pp, :v * 128 + pp + 1] = 1.0
    p["cmask"] = cm
    return p


def kernel(**inputs):
    prep = host_prep(inputs)
    ctx = build(DEV_TAPS)
    x = np.asarray(inputs["L0_attn_x"], dtype=np.float32)
    in_maps = []
    for c in range(8):
        m = dict(prep)
        m["x"] = np.ascontiguousarray(x[c])
        in_maps.append(m)
    res = run_bass_kernel_spmd(ctx.nc, in_maps, core_ids=list(range(8)))
    return np.stack([np.asarray(res.results[c]["out"]) for c in range(8)],
                    axis=0)


# version 10
# speedup vs baseline: 1.2562x; 1.2562x over previous
import numpy as np
import ml_dtypes
import concourse.bass as bass
import concourse.mybir as mybir
from concourse.bass_utils import run_bass_kernel_spmd

F32 = mybir.dt.float32
BF16 = mybir.dt.bfloat16
AF = mybir.ActivationFunctionType
ALU = mybir.AluOpType
AX = mybir.AxisListType

ENGINES = ("pe", "act", "dve", "pool", "sp")
_BLOCK_ATTR = {"pe": "tensor", "act": "scalar", "dve": "vector",
               "pool": "gpsimd", "sp": "sync"}

QP = "act"
QT = "sp"
T = 2048
C = 2048
NT = T // 128
NC = C // 128
NH = 16
HD = 128
I0 = 5632
NI0 = I0 // 128
NE = 8
I1 = 1408
NI1 = I1 // 128
DST = 16
ISQ = 1.0 / np.sqrt(128.0)

DEV_TAPS = ()



class Buf:
    __slots__ = ("ap", "name", "writer", "readers", "pending", "_rng")

    def __init__(self, ap, name=""):
        self.ap = ap
        self.name = name
        self.writer = None
        self.readers = []
        self.pending = []


class _Task:
    __slots__ = ("waits", "emit", "sem", "amt")


class Sched:
    DMA_RING = 8

    def __init__(self, nc):
        self.nc = nc
        self.tasks = {e: [] for e in ENGINES}
        self.cnt = {}
        self.sem_names = []
        for e in ENGINES:
            self._new_sem("p_" + e)
        self.dma_ring = {}
        self.dma_prev = {}
        for q in ("sp", "act", "pool"):
            names = []
            for i in range(self.DMA_RING):
                n = f"d_{q}{i}"
                self._new_sem(n)
                names.append(n)
            self.dma_ring[q] = names
            for n in names:
                self.dma_prev[n] = None
        self.dma_rr = {q: 0 for q in self.dma_ring}
        self.clock = {e: {} for e in ENGINES}
        self.n_waits = 0
        self.n_tasks = 0

    def _new_sem(self, name):
        self.sem_names.append(name)
        self.cnt[name] = 0

    def buf(self, ap, name=""):
        return Buf(ap, name)

    def task(self, engine, emit, reads=(), writes=(), dma=None, extra_waits=()):
        eng_clock = self.clock[engine]
        waits = {}

        def need(dep):
            if dep is None:
                return
            sem, val, _clk = dep
            if eng_clock.get(sem, 0) < val and waits.get(sem, 0) < val:
                waits[sem] = val

        for b in reads:
            need(b.writer)
        for b in writes:
            need(b.writer)
            for r in b.readers:
                need(r)
            for dp in b.pending:
                need(dp)
        for wdep in extra_waits:
            need(wdep)

        if dma is not None:
            ring = self.dma_ring[dma]
            sem = ring[self.dma_rr[dma] % len(ring)]
            self.dma_rr[dma] += 1
            amt = 16
            need(self.dma_prev[sem])
        else:
            sem, amt = "p_" + engine, 1
        self.cnt[sem] += amt
        val = self.cnt[sem]

        t = _Task()
        t.waits = sorted(waits.items())
        t.emit = emit
        t.sem, t.amt = sem, amt
        self.tasks[engine].append(t)
        self.n_waits += len(t.waits)
        self.n_tasks += 1

        for s, v in waits.items():
            if eng_clock.get(s, 0) < v:
                eng_clock[s] = v
        for b in reads:
            if b.writer is not None:
                self._merge(eng_clock, b.writer[2])
        for b in writes:
            if b.writer is not None:
                self._merge(eng_clock, b.writer[2])
            for r in b.readers:
                self._merge(eng_clock, r[2])
            b.pending = []

        clk = dict(eng_clock)
        clk[sem] = max(clk.get(sem, 0), val)
        dep = (sem, val, clk)
        if dma is not None:
            self.dma_prev[sem] = dep
        for b in writes:
            b.writer = dep
            b.readers = []
        for b in reads:
            b.readers.append(dep)
        return dep

    @staticmethod
    def _merge(dst, src):
        for s, v in src.items():
            if dst.get(s, 0) < v:
                dst[s] = v

    def barrier_wait(self, engine, deps):
        t = _Task()
        eng_clock = self.clock[engine]
        waits = {}
        for dep in deps:
            if dep is None:
                continue
            sem, val, _ = dep
            if eng_clock.get(sem, 0) < val and waits.get(sem, 0) < val:
                waits[sem] = val
        t.waits = sorted(waits.items())
        t.emit = None
        t.sem = None
        self.tasks[engine].append(t)
        for s, v in waits.items():
            if eng_clock.get(s, 0) < v:
                eng_clock[s] = v

    def build(self):
        from contextlib import ExitStack
        nc = self.nc
        with ExitStack() as ctx:
            sems = {}
            for name in self.sem_names:
                sems[name] = ctx.enter_context(nc.semaphore(name))
            block = ctx.enter_context(nc.Block())

            def mk(tasks):
                def emit_all(eng):
                    for t in tasks:
                        for s, v in t.waits:
                            eng.wait_ge(sems[s], v)
                        if t.emit is not None:
                            inst = t.emit(eng)
                            inst.then_inc(sems[t.sem], t.amt)
                return emit_all

            for e in ENGINES:
                getattr(block, _BLOCK_ATTR[e])(mk(self.tasks[e]))


class Arena:
    def __init__(self, nc, sch, name, nbytes):
        h = nc.alloc_sbuf_tensor(name, [128, nbytes], mybir.dt.uint8)
        self.base = nc.lookup_mloc(h).addr
        self.nbytes = nbytes
        self.nc = nc
        self.sch = sch
        self.off = 0
        self.live = []
        self.retired = []
        self.n = 0

    def alloc(self, shape, dtype, name=""):
        esz = 4 if dtype == F32 else 2
        fbytes = int(np.prod(shape[1:])) * esz
        off = (self.off + 31) & ~31
        assert off + fbytes <= self.nbytes, \
            f"arena overflow: {name} needs {fbytes}B at {off}/{self.nbytes}"
        self.off = off + fbytes
        h = self.nc.alloc_sbuf_tensor_at(
            f"ar{self.n}_{name}", list(shape), dtype, offset=self.base + off)
        self.n += 1
        b = self.sch.buf(h.ap(), name)
        pend = []
        for ro, rs_, deps in self.retired:
            if off < ro + rs_ and ro < off + fbytes:
                pend.extend(deps)
        b.pending = pend
        b._rng = (off, fbytes)
        self.live.append(b)
        return b

    def reset(self):
        retired = []
        for b in self.live:
            deps = list(b.readers)
            if b.writer is not None:
                deps.append(b.writer)
            if deps:
                retired.append((b._rng[0], b._rng[1], deps))
        self.retired = retired + self.retired
        if len(self.retired) > 400:
            keep, old = self.retired[:200], self.retired[200:]
            best = {}
            for _o, _s, deps in old:
                for dp in deps:
                    if best.get(dp[0], (None, -1))[1] < dp[1]:
                        best[dp[0]] = (dp, dp[1])
            self.retired = keep + [(0, self.nbytes,
                                    [v[0] for v in best.values()])]
        self.live = []
        self.off = 0


class Ctx:
    pass


def E_dma(out_ap, in_ap, **kw):
    return lambda e: e.dma_start(out=out_ap, in_=in_ap, **kw)


def dbx(ctx, name, lo=None, hi=None):
    regs = ctx.db[name]
    if lo is None or len(regs) == 1:
        return list(regs)
    nrows = ctx.d[name].shape[0]
    step = nrows // len(regs)
    return regs[lo // step:(hi + step - 1) // step]


def E_dma_gp(out_ap2d, in_ap2d, w):
    in3 = in_ap2d.rearrange("(g p) w -> p g w", p=128)
    out3 = out_ap2d.rearrange("p (g w) -> p g w", w=w)
    return lambda e: e.dma_start(out=out3, in_=in3)



def _mk_dram(ctx, dev_taps):
    nc = ctx.nc
    d = {}
    db = {}

    def din(name, shape, dt=F32):
        d[name] = nc.dram_tensor(name, list(shape), dt, kind="ExternalInput").ap()

    REGIONED = {"XN": 4, "GTM": 4, "XN2": 4, "HT": 4, "H2": 4,
                "AFM": 16, "SFM": 16}

    def dsc(name, shape, dt):
        kind = "ExternalOutput" if name in dev_taps else "Internal"
        d[name] = nc.dram_tensor(name, list(shape), dt, kind=kind).ap()
        nreg = REGIONED.get(name, 1)
        if nreg == 1:
            db[name] = [Buf(d[name], name)]
        else:
            step = shape[0] // nreg
            db[name] = [Buf(d[name][i * step:(i + 1) * step], f"{name}_{i}")
                        for i in range(nreg)]

    din("x", [T, C])
    din("wqT", [C, C]); din("wkT", [C, C]); din("wvT", [C, C])
    din("tau", [1, 1]); din("nu", [1, 1])
    din("gwT", [C, I0]); din("uwT", [C, I0]); din("dwT", [I0, C])
    din("gs", [128, NI0]); din("us", [128, NI0]); din("ds_row", [1, C])
    din("AlogR", [128, NC * DST]); din("Dr", [128, NC])
    din("xpT", [C, 160]); din("dtT", [128, C]); din("dtb", [128, NC])
    din("anw", [1, C]); din("fnw", [1, C])
    din("qwT", [C, C]); din("kwT", [C, 512]); din("vwT", [C, 512])
    din("owT", [C, C]); din("qnw", [1, HD]); din("knw", [1, HD])
    din("rwT", [C, NE])
    din("wgT", [NE, C, I1]); din("wuT", [NE, C, I1]); din("wdT", [NE, I1, C])
    din("rcos", [T, 64]); din("rsin", [T, 64])
    din("cmask", [4, 128, 512])

    d["out"] = nc.dram_tensor("out", [T, C], F32, kind="ExternalOutput").ap()

    dsc("XN", [T, C], BF16)
    dsc("Q0", [T, C], BF16); dsc("K0", [T, C], BF16); dsc("V0", [T, C], BF16)
    dsc("PD", [2, T, T], BF16)
    dsc("SFM", [C, T], BF16)
    dsc("GTM", [T, C], BF16)
    dsc("X1", [T, C], F32)
    dsc("XN2", [T, C], BF16)
    dsc("HT", [T, C], BF16)
    dsc("Q1", [T, C], BF16); dsc("K1", [T, 512], BF16); dsc("V1", [T, 512], BF16)
    dsc("AFM", [C, T], BF16)
    dsc("HID", [T, C], BF16)
    dsc("H2", [T, C], BF16)
    dsc("BCR", [32, T], F32)
    dsc("Y", [C, T], BF16)
    dsc("MOE", [T, C], BF16)
    ctx.d = d
    ctx.db = db
    ctx.pd_reg = [[Buf(d["PD"][par, i * 128:(i + 1) * 128, :], f"pd{par}_{i}")
                   for i in range(NT)] for par in range(2)]


def _mk_static(ctx):
    nc, sch = ctx.nc, ctx.sch
    A = lambda name, shape, dt: sch.buf(
        nc.alloc_sbuf_tensor(name, list(shape), dt).ap(), name)

    ctx.XNF = [A(f"xnf{i}", [128, T], BF16) for i in range(NC)]
    ctx.cmask = [A(f"cmask{i}", [128, 512], F32) for i in range(4)]
    ctx.tau_bc = A("tau_bc", [128, 1], F32)
    ctx.nu_bc = A("nu_bc", [128, 1], F32)
    ctx.gs = A("gs_sb", [128, NI0], F32)
    ctx.us = A("us_sb", [128, NI0], F32)
    ctx.A_sb = A("A_sb", [128, NC * DST], F32)
    ctx.D_sb = A("D_sb", [128, NC], F32)
    ctx.dtb = A("dtb_sb", [128, NC], F32)
    ctx.qw_bc = A("qw_bc", [128, HD], F32)
    ctx.kw_bc = A("kw_bc", [128, HD], F32)
    ctx.rc = A("rc_sb", [128, NT * 64], F32)
    ctx.rs = A("rs_sb", [128, NT * 64], F32)
    ctx.hst = A("hst", [128, NC * DST], F32)
    ctx.c_eps5 = A("c_eps5", [128, 1], F32)
    ctx.neg30k = A("neg30k", [128, 512], F32)
    ctx.c_eps6 = A("c_eps6", [128, 1], F32)
    ctx.comb = [A(f"comb{i}", [128, NE], F32) for i in range(NT)]
    ctx.rw_sb = A("rw_sb", [128, NC * NE], BF16)

    ctx.PS = [sch.buf(nc.alloc_psum_tensor(f"ps{i}", [128, 512], F32).ap(),
                      f"ps{i}") for i in range(8)]
    ctx._ps_rr = 0

    avail = (nc.sbuf_bytes_remaining - 512) & ~31
    ctx.arena = Arena(nc, sch, "arena", avail)


def ps_take(ctx, n=1):
    out = []
    for _ in range(n):
        out.append(ctx.PS[ctx._ps_rr % 8])
        ctx._ps_rr += 1
    return out if n > 1 else out[0]


def load_consts(ctx):
    sch, d = ctx.sch, ctx.d
    for i in range(4):
        sch.task(QP, E_dma(ctx.cmask[i].ap, d["cmask"][i]),
                 writes=[ctx.cmask[i]], dma=QP)

    def bc(buf, src):
        sch.task("pool", E_dma(buf.ap, src.partition_broadcast(128)),
                 writes=[buf], dma="pool")

    sch.task("dve", lambda e: e.memset(ctx.c_eps5.ap, 1e-5), writes=[ctx.c_eps5])
    sch.task("dve", lambda e: e.memset(ctx.neg30k.ap, -30000.0),
             writes=[ctx.neg30k])
    sch.task("dve", lambda e: e.memset(ctx.c_eps6.ap, 1e-6), writes=[ctx.c_eps6])
    bc(ctx.tau_bc, d["tau"])
    bc(ctx.nu_bc, d["nu"])
    bc(ctx.qw_bc, d["qnw"])
    bc(ctx.kw_bc, d["knw"])
    for name in ("gs", "us", "dtb"):
        sch.task(QP, E_dma(getattr(ctx, name).ap, d[name]),
                 writes=[getattr(ctx, name)], dma=QP)
    sch.task(QP, E_dma(ctx.D_sb.ap, d["Dr"]), writes=[ctx.D_sb], dma=QP)
    sch.task(QP, E_dma(ctx.A_sb.ap, d["AlogR"]), writes=[ctx.A_sb], dma=QP)
    sch.task("act", lambda e: e.activation(ctx.A_sb.ap, ctx.A_sb.ap, AF.Exp),
             reads=[ctx.A_sb], writes=[ctx.A_sb])
    sch.task("dve", lambda e: e.tensor_scalar_mul(ctx.A_sb.ap, ctx.A_sb.ap, -1.0),
             reads=[ctx.A_sb], writes=[ctx.A_sb])
    sch.task(QP, E_dma_gp(ctx.rc.ap, d["rcos"], 64),
             writes=[ctx.rc], dma=QP)
    sch.task(QP, E_dma_gp(ctx.rs.ap, d["rsin"], 64),
             writes=[ctx.rs], dma=QP)
    sch.task("pool", E_dma_gp(ctx.rw_sb.ap, d["rwT"], NE),
             writes=[ctx.rw_sb], dma="pool")



def tload(ctx, dst_buf, dst_ap, src_ap, reads, q=QT):
    ctx.sch.task(q, E_dma(dst_ap, src_ap, transpose=True),
                 reads=reads, writes=[dst_buf], dma=q)


def fill_xnf(ctx, src_name):
    d = ctx.d
    for t4 in range(4):
        for cc in range(NC):
            tload(ctx, ctx.XNF[cc],
                  ctx.XNF[cc].ap[:, t4 * 512:(t4 + 1) * 512],
                  d[src_name][t4 * 512:(t4 + 1) * 512, cc * 128:(cc + 1) * 128],
                  dbx(ctx, src_name, t4 * 512, (t4 + 1) * 512))


def emit_rms(ctx, sm, src_buf, src_ap, wbc_buf, dst_buf, dst_ap):
    sch = ctx.sch
    ss, rstd, junk, t1 = sm["ss"], sm["rstd"], sm["junk"], sm["t1"]
    sch.task("act", lambda e: e.activation(junk.ap, src_ap, AF.Square,
                                           accum_out=ss.ap),
             reads=[src_buf], writes=[junk, ss])
    sch.task("act", lambda e: e.activation(rstd.ap, ss.ap, AF.Ln,
                                           scale=1.0 / C, bias=ctx.c_eps6.ap),
             reads=[ss, ctx.c_eps6], writes=[rstd])
    sch.task("act", lambda e: e.activation(rstd.ap, rstd.ap, AF.Exp, scale=-0.5),
             reads=[rstd], writes=[rstd])
    sch.task("dve", lambda e: e.tensor_scalar_mul(t1.ap, src_ap, rstd.ap),
             reads=[src_buf, rstd], writes=[t1])
    sch.task("dve", lambda e: e.tensor_tensor(dst_ap, t1.ap, wbc_buf.ap, ALU.mult),
             reads=[t1, wbc_buf], writes=[dst_buf])



def phase_ln0(ctx):
    sch, d, db, ar = ctx.sch, ctx.d, ctx.db, ctx.arena
    ar.reset()
    xt = [ar.alloc([128, C], F32, "p1x") for _ in range(2)]
    xn = [ar.alloc([128, C], BF16, "p1n") for _ in range(2)]
    sm = [dict(s1=ar.alloc([128, 1], F32, "s1"), ss=ar.alloc([128, 1], F32, "ss"),
               m=ar.alloc([128, 1], F32, "m"), v=ar.alloc([128, 1], F32, "v"),
               rstd=ar.alloc([128, 1], F32, "rstd"),
               nmr=ar.alloc([128, 1], F32, "nmr"),
               junk=ar.alloc([128, C], BF16, "junk")) for _ in range(2)]
    for tc in range(NT):
        xb, nb, s = xt[tc % 2], xn[tc % 2], sm[tc % 2]
        sch.task(QP, E_dma(xb.ap, d["x"][tc * 128:(tc + 1) * 128, :]),
                 writes=[xb], dma=QP)
        s1, ss, m, v, rstd, nmr, junk = (s[k] for k in
                                         ("s1", "ss", "m", "v", "rstd", "nmr", "junk"))
        sch.task("dve", lambda e, a=xb, o=s1: e.reduce_sum(o.ap, a.ap, axis=AX.X),
                 reads=[xb], writes=[s1])
        sch.task("act", lambda e, a=xb, j=junk, o=ss: e.activation(
            j.ap, a.ap, AF.Square, accum_out=o.ap),
                 reads=[xb], writes=[junk, ss])
        sch.task("dve", lambda e, a=s1, o=m: e.tensor_scalar_mul(o.ap, a.ap, 1.0 / C),
                 reads=[s1], writes=[m])
        sch.task("dve", lambda e, a=m, o=v: e.scalar_tensor_tensor(
            o.ap, a.ap, 1.0, a.ap, ALU.mult, ALU.mult),
                 reads=[m], writes=[v])
        sch.task("dve", lambda e, a=ss, o=v: e.scalar_tensor_tensor(
            o.ap, a.ap, 1.0 / C, o.ap, ALU.mult, ALU.subtract),
                 reads=[ss, v], writes=[v])
        sch.task("act", lambda e, a=v, o=rstd: e.activation(
            o.ap, a.ap, AF.Ln, bias=ctx.c_eps5.ap),
                 reads=[v, ctx.c_eps5], writes=[rstd])
        sch.task("act", lambda e, o=rstd: e.activation(
            o.ap, o.ap, AF.Exp, scale=-0.5),
                 reads=[rstd], writes=[rstd])
        sch.task("dve", lambda e, a=m, b=rstd, o=nmr: e.scalar_tensor_tensor(
            o.ap, a.ap, -1.0, b.ap, ALU.mult, ALU.mult),
                 reads=[m, rstd], writes=[nmr])
        sch.task("dve", lambda e, a=xb, b=rstd, c=nmr, o=nb: e.tensor_scalar(
            o.ap, a.ap, b.ap, c.ap, ALU.mult, ALU.add),
                 reads=[xb, rstd, nmr], writes=[nb])
        sch.task(QP, E_dma(d["XN"][tc * 128:(tc + 1) * 128, :], nb.ap),
                 reads=[nb], writes=dbx(ctx, "XN", tc * 128, (tc + 1) * 128),
                 dma=QP)



def phase_qkv0(ctx):
    sch, d, db, ar = ctx.sch, ctx.d, ctx.db, ctx.arena
    ar.reset()
    fill_xnf(ctx, "XN")
    W = [ar.alloc([128, C], BF16, f"w0_{i}") for i in range(NC)]
    qh = [ar.alloc([128, C], BF16, "qh") for _ in range(2)]
    ssl = [ar.alloc([128, 16], F32, "ssl") for _ in range(2)]
    rf = [ar.alloc([128, 16], F32, "rf") for _ in range(2)]
    junk = [ar.alloc([128, 128], BF16, "jk") for _ in range(2)]

    for wname, dst, is_q, is_v in (("wqT", "Q0", True, False),
                                   ("wkT", "K0", False, False),
                                   ("wvT", "V0", False, True)):
        for cc in range(NC):
            sch.task("pool", E_dma(W[cc].ap,
                                   d[wname][cc * 128:(cc + 1) * 128, :]),
                     writes=[W[cc]], dma="pool")
        for tc in range(NT):
            ps = [ps_take(ctx) for _ in range(4)]
            for c4 in range(4):
                def mme(e, tc=tc, c4=c4, p=ps[c4]):
                    for cc in range(NC):
                        inst = e.matmul(
                            p.ap,
                            ctx.XNF[cc].ap[:, tc * 128:(tc + 1) * 128],
                            W[cc].ap[:, c4 * 512:(c4 + 1) * 512],
                            start=(cc == 0), stop=(cc == NC - 1))
                    return inst
                sch.task("pe", mme, reads=ctx.XNF + W, writes=[ps[c4]])
            q, sl, r, jk = qh[tc % 2], ssl[tc % 2], rf[tc % 2], junk[tc % 2]
            for c4 in range(4):
                for j in range(4):
                    hidx = c4 * 4 + j
                    sch.task("act", lambda e, p=ps[c4], j=j, h=hidx, jk=jk, sl=sl:
                             e.activation(
                                 jk.ap, p.ap[:, j * 128:(j + 1) * 128], AF.Square,
                                 accum_out=sl.ap[:, h:h + 1]),
                             reads=[ps[c4]], writes=[jk, sl])
            sch.task("act", lambda e, a=sl, o=r: e.activation(o.ap, a.ap, AF.Ln),
                     reads=[sl], writes=[r])
            sch.task("act", lambda e, o=r: e.activation(o.ap, o.ap, AF.Exp,
                                                        scale=0.5),
                     reads=[r], writes=[r])
            sch.task("dve", lambda e, o=r: e.tensor_scalar_add(o.ap, o.ap, 1e-8),
                     reads=[r], writes=[r])
            sch.task("dve", lambda e, o=r: e.reciprocal(o.ap, o.ap),
                     reads=[r], writes=[r])
            if is_q:
                sch.task("dve", lambda e, o=r: e.tensor_scalar_mul(
                    o.ap, o.ap, ctx.tau_bc.ap),
                         reads=[r, ctx.tau_bc], writes=[r])
            for c4 in range(4):
                for j in range(4):
                    hidx = c4 * 4 + j
                    sch.task("dve", lambda e, p=ps[c4], j=j, h=hidx, q=q, r=r:
                             e.tensor_scalar_mul(
                                 q.ap[:, h * 128:(h + 1) * 128],
                                 p.ap[:, j * 128:(j + 1) * 128], r.ap[:, h:h + 1]),
                             reads=[ps[c4], r], writes=[q])
            sch.task(QP, E_dma(d[dst][tc * 128:(tc + 1) * 128, :], q.ap),
                     reads=[q], writes=db[dst], dma=QP)



def phase_attn(ctx, layer, hooks=None, reset=True):
    sch, d, db, ar = ctx.sch, ctx.d, ctx.db, ctx.arena
    if reset:
        ar.reset()
    if layer == 0:
        QN, KN, VN, ON = "Q0", "K0", "V0", "SFM"
        n_kv = 16
        kv_w = C
    else:
        QN, KN, VN, ON = "Q1", "K1", "V1", "AFM"
        n_kv = 4
        kv_w = 512
    rep = NH // n_kv

    qf = [ar.alloc([128, T], BF16, "qf") for _ in range(2)]
    kf = [ar.alloc([128, T], BF16, "kf") for _ in range(2)]
    vt = [ar.alloc([128, T], BF16, "vt") for _ in range(2)]
    psb = [[ar.alloc([128, T], BF16, "psb") for _ in range(2)]
           for _ in range(2)]
    pt = [ar.alloc([128, 512], BF16, "pt") for _ in range(4)]
    att = [ar.alloc([128, 512], BF16, "att") for _ in range(2)]
    negm = [[ar.alloc([128, 4], F32, "negm") for _ in range(2)]
            for _ in range(2)]
    rssb = [[ar.alloc([128, 4], F32, "rssb") for _ in range(2)]
            for _ in range(2)]
    den = [[ar.alloc([128, 1], F32, "den") for _ in range(2)]
           for _ in range(2)]

    def emit_S(h):
        par = h % 2
        g = h // rep
        tload(ctx, qf[par], qf[par].ap,
              d[QN][:, h * 128:(h + 1) * 128], db[QN])
        if h % rep == 0:
            tload(ctx, kf[g % 2], kf[g % 2].ap,
                  d[KN][:, g * 128:(g + 1) * 128], db[KN])
            for tkc in range(NT):
                sch.task(QP, E_dma(vt[g % 2].ap[:, tkc * 128:(tkc + 1) * 128],
                                     d[VN][tkc * 128:(tkc + 1) * 128,
                                           g * 128:(g + 1) * 128]),
                         reads=db[VN], writes=[vt[g % 2]], dma=QP)
        kbuf = kf[g % 2]

        def s_pair(pair):
            cinfo = []
            for tqc in pair:
                P, nm, rs, dn = (psb[par][tqc % 2], negm[par][tqc % 2],
                                 rssb[par][tqc % 2], den[par][tqc % 2])
                nch = tqc // 4 + 1
                banks = [ps_take(ctx) for _ in range(nch)]
                cinfo.append((tqc, P, nm, rs, dn, nch, banks))
            for tqc, P, nm, rs, dn, nch, banks in cinfo:
                for c, ps in enumerate(banks):
                    sch.task("pe", lambda e, p=ps, tqc=tqc, c=c, q=qf[par],
                             k=kbuf:
                             e.matmul(p.ap, q.ap[:, tqc * 128:(tqc + 1) * 128],
                                      k.ap[:, c * 512:(c + 1) * 512],
                                      start=True, stop=True),
                             reads=[qf[par], kbuf], writes=[ps])
            for tqc, P, nm, rs, dn, nch, banks in cinfo:
                dcb = banks[nch - 1]
                mv = tqc % 4
                sch.task("dve", lambda e, p=dcb, m=mv: e.scalar_tensor_tensor(
                    p.ap, p.ap, 30000.0, ctx.cmask[m].ap, ALU.add, ALU.mult),
                         reads=[dcb, ctx.cmask[mv]], writes=[dcb])
                sch.task("dve", lambda e, p=dcb: e.tensor_scalar_add(
                    p.ap, p.ap, -30000.0),
                         reads=[dcb], writes=[dcb])
            for tqc, P, nm, rs, dn, nch, banks in cinfo:
                for c, ps in enumerate(banks):
                    sch.task("dve", lambda e, p=ps, c=c, nm=nm: e.reduce_max(
                        nm.ap[:, c:c + 1], p.ap, axis=AX.X, negate=True),
                             reads=[ps], writes=[nm])
                if nch > 1:
                    sch.task("dve", lambda e, n=nch, nm=nm, dn=dn:
                             e.tensor_reduce(
                        dn.ap, nm.ap[:, 0:n], AX.X, ALU.min),
                             reads=[nm], writes=[dn])
                else:
                    sch.task("dve", lambda e, nm=nm, dn=dn: e.tensor_copy(
                        dn.ap, nm.ap[:, 0:1]),
                             reads=[nm], writes=[dn])
            for tqc, P, nm, rs, dn, nch, banks in cinfo:
                for c, ps in enumerate(banks):
                    sch.task("act", lambda e, p=ps, c=c, P=P, dn=dn, rs=rs:
                             e.activation(
                        P.ap[:, c * 512:(c + 1) * 512], p.ap, AF.Exp,
                        bias=dn.ap, accum_out=rs.ap[:, c:c + 1]),
                             reads=[ps, dn], writes=[P, rs])
            for tqc, P, nm, rs, dn, nch, banks in cinfo:
                sch.task("dve", lambda e, n=nch, rs=rs, dn=dn: e.reduce_sum(
                    dn.ap, rs.ap[:, 0:n], axis=AX.X),
                         reads=[rs], writes=[dn])
                sch.task("dve", lambda e, dn=dn: e.reciprocal(dn.ap, dn.ap),
                         reads=[dn], writes=[dn])
                if layer == 0:
                    sch.task("dve", lambda e, dn=dn: e.tensor_scalar_mul(
                        dn.ap, dn.ap, ctx.nu_bc.ap),
                             reads=[dn, ctx.nu_bc], writes=[dn])
                sch.task("dve", lambda e, n=nch, P=P, dn=dn: e.tensor_scalar_mul(
                    P.ap[:, 0:n * 512], P.ap[:, 0:n * 512], dn.ap),
                         reads=[P, dn], writes=[P])
                sch.task(QP, E_dma(
                    d["PD"][par, tqc * 128:(tqc + 1) * 128, 0:nch * 512],
                    P.ap[:, 0:nch * 512]),
                         reads=[P], writes=[ctx.pd_reg[par][tqc]], dma=QP)
        return s_pair

    def emit_PV_group(h, g4):
        par = h % 2
        g = h // rep
        vbuf = vt[g % 2]
        ao = att[g4 % 2]
        ntk = g4 * 4 + 4
        psum = ps_take(ctx)
        for tkc in range(ntk):
            ptb = pt[tkc % 4]
            regs = [ctx.pd_reg[par][g4 * 4 + i] for i in range(4)]
            tload(ctx, ptb, ptb.ap,
                  d["PD"][par, g4 * 512:(g4 + 1) * 512,
                          tkc * 128:(tkc + 1) * 128], regs)
            sch.task("pe", lambda e, p=psum, v=vbuf, tkc=tkc, b=ptb,
                     st=(tkc == 0), sp=(tkc == ntk - 1):
                     e.matmul(p.ap, v.ap[:, tkc * 128:(tkc + 1) * 128],
                              b.ap, start=st, stop=sp),
                     reads=[vbuf, ptb], writes=[psum])
        sch.task("act", lambda e, p=psum, ao=ao: e.activation(
            ao.ap, p.ap, AF.Copy),
                 reads=[psum], writes=[ao])
        sch.task(QP, E_dma(d[ON][h * 128:(h + 1) * 128,
                                 g4 * 512:(g4 + 1) * 512], ao.ap),
                 reads=[ao], writes=dbx(ctx, ON, h * 128, (h + 1) * 128),
                 dma=QP)

    for h in range(NH):
        s_pair = emit_S(h)
        for pi in range(8):
            s_pair((2 * pi, 2 * pi + 1))
            if h > 0 and pi % 2 == 1:
                emit_PV_group(h - 1, pi // 2)
        if hooks and h in hooks:
            hooks[h]()
    for g4 in range(4):
        emit_PV_group(NH - 1, g4)



def phase_ffn0(ctx):
    sch, d, db, ar = ctx.sch, ctx.d, ctx.db, ctx.arena
    ar.reset()
    gsl = [ar.alloc([128, 16 * 256], BF16, "gsl") for _ in range(2)]
    usl = [ar.alloc([128, 16 * 256], BF16, "usl") for _ in range(2)]
    hb = [ar.alloc([128, 512], BF16, f"hb{i}") for i in range(NI0)]
    dslb = [ar.alloc([128, 11 * 512], BF16, "dslb") for _ in range(2)]
    sgt = [ar.alloc([128, 512], BF16, "sgt") for _ in range(2)]
    sut = [ar.alloc([128, 512], BF16, "sut") for _ in range(2)]
    tmpf = [ar.alloc([128, 512], F32, "tmpf") for _ in range(2)]
    gt = [ar.alloc([128, 512], BF16, "gt") for _ in range(2)]
    dsbc = ar.alloc([128, C], F32, "dsbc")
    sch.task("pool", E_dma(dsbc.ap, d["ds_row"].partition_broadcast(128)),
             writes=[dsbc], dma="pool")

    f_loaded = set()

    def f_gu_load(t4, icg):
        if (t4, icg) in f_loaded:
            return
        f_loaded.add((t4, icg))
        i0 = icg * 256
        gb, ub = gsl[icg % 2], usl[icg % 2]
        sch.task("pool", E_dma_gp(gb.ap, d["gwT"][:, i0:i0 + 256], 256),
                 writes=[gb], dma="pool")
        sch.task("pool", E_dma_gp(ub.ap, d["uwT"][:, i0:i0 + 256], 256),
                 writes=[ub], dma="pool")

    for t4 in range(4):
        t0 = t4 * 512
        for icg in range(22):
            i0 = icg * 256
            gb, ub = gsl[icg % 2], usl[icg % 2]
            f_gu_load(t4, icg)
            for j in range(2):
                ic = icg * 2 + j
                pg = ps_take(ctx)
                pu = ps_take(ctx)

                def mme(e, wb=gb, j=j, p=pg, t0=t0):
                    for cc in range(NC):
                        inst = e.matmul(
                            p.ap,
                            wb.ap[:, cc * 256 + j * 128: cc * 256 + (j + 1) * 128],
                            ctx.XNF[cc].ap[:, t0:t0 + 512],
                            start=(cc == 0), stop=(cc == NC - 1))
                    return inst
                sch.task("pe", lambda e, f=mme: f(e),
                         reads=ctx.XNF + [gb], writes=[pg])
                sch.task("pe", lambda e, wb=ub, j=j, p=pu, f=mme: f(e, wb, j, p),
                         reads=ctx.XNF + [ub], writes=[pu])
                sg, su = sgt[ic % 2], sut[ic % 2]
                sch.task("act", lambda e, p=pg, ic=ic, o=sg: e.activation(
                    o.ap, p.ap, AF.Silu, scale=ctx.gs.ap[:, ic:ic + 1]),
                         reads=[pg, ctx.gs], writes=[sg])
                sch.task("act", lambda e, p=pu, ic=ic, o=su: e.activation(
                    o.ap, p.ap, AF.Copy, scale=ctx.us.ap[:, ic:ic + 1]),
                         reads=[pu, ctx.us], writes=[su])
                sch.task("dve", lambda e, a=sg, b=su, o=hb[ic]: e.tensor_tensor(
                    o.ap, a.ap, b.ap, ALU.mult),
                         reads=[sg, su], writes=[hb[ic]])
        if t4 + 1 < 4:
            f_gu_load(t4 + 1, 0)
            f_gu_load(t4 + 1, 1)
        for co4 in range(4):
            co0 = co4 * 512
            psds = [ps_take(ctx) for _ in range(4)]
            for qr in range(4):
                r0 = qr * 11 * 128
                wb = dslb[qr % 2]
                sch.task("pool", E_dma_gp(
                    wb.ap, d["dwT"][r0:r0 + 11 * 128, co0:co0 + 512], 512),
                         writes=[wb], dma="pool")
                for tci in range(4):
                    def mmd(e, tci=tci, p=psds[tci], wb=wb, qr=qr):
                        for k in range(11):
                            ic = qr * 11 + k
                            inst = e.matmul(
                                p.ap,
                                hb[ic].ap[:, tci * 128:(tci + 1) * 128],
                                wb.ap[:, k * 512:(k + 1) * 512],
                                start=(qr == 0 and k == 0),
                                stop=(qr == 3 and k == 10))
                        return inst
                    sch.task("pe", mmd,
                             reads=hb[qr * 11:(qr + 1) * 11] + [wb],
                             writes=[psds[tci]])
            for tci in range(4):
                psd = psds[tci]
                tf, g = tmpf[tci % 2], gt[tci % 2]
                sch.task("dve", lambda e, p=psd, o=tf, co0=co0: e.tensor_tensor(
                    o.ap, p.ap, dsbc.ap[:, co0:co0 + 512], ALU.mult),
                         reads=[psd, dsbc], writes=[tf])
                sch.task("act", lambda e, a=tf, o=g: e.activation(
                    o.ap, a.ap, AF.Silu),
                         reads=[tf], writes=[g])
                tc = t4 * 4 + tci
                sch.task(QP, E_dma(
                    d["GTM"][tc * 128:(tc + 1) * 128, co0:co0 + 512], g.ap),
                         reads=[g],
                         writes=dbx(ctx, "GTM", tc * 128, (tc + 1) * 128),
                         dma=QP)



def phase_combine0(ctx):
    sch, d, db, ar = ctx.sch, ctx.d, ctx.db, ctx.arena
    ar.reset()
    xt = [ar.alloc([128, C], F32, "c0x") for _ in range(2)]
    gt = [ar.alloc([128, C], BF16, "c0g") for _ in range(2)]
    st = [ar.alloc([128, C], BF16, "c0s") for _ in range(2)]
    sg = [ar.alloc([128, C], BF16, "c0sg") for _ in range(2)]
    nb = [ar.alloc([128, C], BF16, "c0n") for _ in range(2)]
    sm = [dict(s1=ar.alloc([128, 1], F32, "s1"), ss=ar.alloc([128, 1], F32, "ss"),
               m=ar.alloc([128, 1], F32, "m"), v=ar.alloc([128, 1], F32, "v"),
               rstd=ar.alloc([128, 1], F32, "rstd"),
               nmr=ar.alloc([128, 1], F32, "nmr"),
               junk=ar.alloc([128, C], BF16, "junk")) for _ in range(2)]
    for tc in range(NT):
        t0 = tc * 128
        xb, g, s, p, nbb, smm = (xt[tc % 2], gt[tc % 2], st[tc % 2],
                                 sg[tc % 2], nb[tc % 2], sm[tc % 2])
        sch.task(QP, E_dma(xb.ap, d["x"][t0:t0 + 128, :]),
                 writes=[xb], dma=QP)
        sch.task(QP, E_dma(g.ap, d["GTM"][t0:t0 + 128, :]),
                 reads=dbx(ctx, "GTM", t0, t0 + 128), writes=[g], dma=QP)
        for c4 in range(4):
            tload(ctx, s, s.ap[:, c4 * 512:(c4 + 1) * 512],
                  d["SFM"][c4 * 512:(c4 + 1) * 512, t0:t0 + 128],
                  dbx(ctx, "SFM", c4 * 512, (c4 + 1) * 512))
        sch.task("dve", lambda e, a=s, b=g, o=p: e.tensor_tensor(
            o.ap, a.ap, b.ap, ALU.mult), reads=[s, g], writes=[p])
        sch.task("dve", lambda e, a=xb, b=p: e.tensor_tensor(
            a.ap, a.ap, b.ap, ALU.add), reads=[xb, p], writes=[xb])
        sch.task(QP, E_dma(d["X1"][t0:t0 + 128, :], xb.ap),
                 reads=[xb], writes=db["X1"], dma=QP)
        s1, ss, m, v, rstd, nmr, junk = (smm[k] for k in
                                         ("s1", "ss", "m", "v", "rstd", "nmr", "junk"))
        sch.task("dve", lambda e, a=xb, o=s1: e.reduce_sum(o.ap, a.ap, axis=AX.X),
                 reads=[xb], writes=[s1])
        sch.task("act", lambda e, a=xb, j=junk, o=ss: e.activation(
            j.ap, a.ap, AF.Square, accum_out=o.ap),
                 reads=[xb], writes=[junk, ss])
        sch.task("dve", lambda e, a=s1, o=m: e.tensor_scalar_mul(o.ap, a.ap, 1.0 / C),
                 reads=[s1], writes=[m])
        sch.task("dve", lambda e, a=m, o=v: e.scalar_tensor_tensor(
            o.ap, a.ap, 1.0, a.ap, ALU.mult, ALU.mult), reads=[m], writes=[v])
        sch.task("dve", lambda e, a=ss, o=v: e.scalar_tensor_tensor(
            o.ap, a.ap, 1.0 / C, o.ap, ALU.mult, ALU.subtract),
                 reads=[ss, v], writes=[v])
        sch.task("act", lambda e, a=v, o=rstd: e.activation(
            o.ap, a.ap, AF.Ln, bias=ctx.c_eps5.ap),
                 reads=[v, ctx.c_eps5], writes=[rstd])
        sch.task("act", lambda e, o=rstd: e.activation(
            o.ap, o.ap, AF.Exp, scale=-0.5), reads=[rstd], writes=[rstd])
        sch.task("dve", lambda e, a=m, b=rstd, o=nmr: e.scalar_tensor_tensor(
            o.ap, a.ap, -1.0, b.ap, ALU.mult, ALU.mult),
                 reads=[m, rstd], writes=[nmr])
        sch.task("dve", lambda e, a=xb, b=rstd, c=nmr, o=nbb: e.tensor_scalar(
            o.ap, a.ap, b.ap, c.ap, ALU.mult, ALU.add),
                 reads=[xb, rstd, nmr], writes=[nbb])
        sch.task(QP, E_dma(d["XN2"][t0:t0 + 128, :], nbb.ap),
                 reads=[nbb], writes=dbx(ctx, "XN2", t0, t0 + 128), dma=QP)



def phase_rms_attn(ctx):
    sch, d, db, ar = ctx.sch, ctx.d, ctx.db, ctx.arena
    ar.reset()
    anw = ar.alloc([128, C], F32, "anwbc")
    sch.task("pool", E_dma(anw.ap, d["anw"].partition_broadcast(128)),
             writes=[anw], dma="pool")
    xb = [ar.alloc([128, C], BF16, "p5x") for _ in range(2)]
    hb = xb
    sm = []
    for _ in range(2):
        t1b = ar.alloc([128, C], BF16, "t1")
        sm.append(dict(ss=ar.alloc([128, 1], F32, "ss"),
                       rstd=ar.alloc([128, 1], F32, "rstd"),
                       junk=t1b, t1=t1b))
    for tc in range(NT):
        t0 = tc * 128
        a, o, s = xb[tc % 2], hb[tc % 2], sm[tc % 2]
        sch.task(QP, E_dma(a.ap, d["XN2"][t0:t0 + 128, :]),
                 reads=dbx(ctx, "XN2", t0, t0 + 128), writes=[a], dma=QP)
        emit_rms(ctx, s, a, a.ap, anw, o, o.ap)
        sch.task(QP, E_dma(d["HT"][t0:t0 + 128, :], o.ap),
                 reads=[o], writes=dbx(ctx, "HT", t0, t0 + 128), dma=QP)



def phase_qkv1(ctx):
    sch, d, db, ar = ctx.sch, ctx.d, ctx.db, ctx.arena
    fill_xnf(ctx, "HT")
    W = [ar.alloc([128, C], BF16, f"w1_{i}") for i in range(NC)]
    kw = [ar.alloc([128, 512], BF16, f"kw{i}") for i in range(NC)]
    vw = kw
    qh = [ar.alloc([128, C], BF16, "q1h") for _ in range(2)]
    kh = [ar.alloc([128, 512], BF16, "k1h") for _ in range(2)]
    vh = [ar.alloc([128, 512], BF16, "v1h") for _ in range(2)]
    ssl = [ar.alloc([128, 16], F32, "ssl1") for _ in range(2)]
    rstd = [ar.alloc([128, 16], F32, "rstd1") for _ in range(2)]
    junk = [ar.alloc([128, 128], BF16, "jk1") for _ in range(2)]
    qn = [ar.alloc([128, 128], F32, "qn") for _ in range(2)]
    ma = [ar.alloc([128, 64], F32, "ma") for _ in range(2)]
    mb = [ar.alloc([128, 64], F32, "mb") for _ in range(2)]
    cw1q = ar.alloc([128, 64], F32, "cw1q"); sw2q = ar.alloc([128, 64], F32, "sw2q")
    cw2q = ar.alloc([128, 64], F32, "cw2q"); sw1q = ar.alloc([128, 64], F32, "sw1q")
    cw1k = ar.alloc([128, 64], F32, "cw1k"); sw2k = ar.alloc([128, 64], F32, "sw2k")
    cw2k = ar.alloc([128, 64], F32, "cw2k"); sw1k = ar.alloc([128, 64], F32, "sw1k")

    for cc in range(NC):
        sch.task("pool", E_dma(W[cc].ap, d["qwT"][cc * 128:(cc + 1) * 128, :]),
                 writes=[W[cc]], dma="pool")
        sch.task("pool", E_dma(kw[cc].ap, d["kwT"][cc * 128:(cc + 1) * 128, :]),
                 writes=[kw[cc]], dma="pool")

    def rope_tabs(tc, w_bc, c1, s2, c2, s1):
        f0 = tc * 64
        sch.task("dve", lambda e: e.tensor_tensor(
            c1.ap, ctx.rc.ap[:, f0:f0 + 64], w_bc.ap[:, 0:64], ALU.mult),
                 reads=[ctx.rc, w_bc], writes=[c1])
        sch.task("dve", lambda e: e.tensor_tensor(
            s2.ap, ctx.rs.ap[:, f0:f0 + 64], w_bc.ap[:, 64:128], ALU.mult),
                 reads=[ctx.rs, w_bc], writes=[s2])
        sch.task("dve", lambda e: e.tensor_tensor(
            c2.ap, ctx.rc.ap[:, f0:f0 + 64], w_bc.ap[:, 64:128], ALU.mult),
                 reads=[ctx.rc, w_bc], writes=[c2])
        sch.task("dve", lambda e: e.tensor_tensor(
            s1.ap, ctx.rs.ap[:, f0:f0 + 64], w_bc.ap[:, 0:64], ALU.mult),
                 reads=[ctx.rs, w_bc], writes=[s1])

    def rstd_batch(sl, rr, n, extra_scale):
        sch.task("act", lambda e: e.activation(
            rr.ap[:, 0:n], sl.ap[:, 0:n], AF.Ln, scale=1.0 / HD,
            bias=ctx.c_eps6.ap),
                 reads=[sl, ctx.c_eps6], writes=[rr])
        sch.task("act", lambda e: e.activation(
            rr.ap[:, 0:n], rr.ap[:, 0:n], AF.Exp, scale=-0.5),
                 reads=[rr], writes=[rr])
        if extra_scale != 1.0:
            sch.task("dve", lambda e: e.tensor_scalar_mul(
                rr.ap[:, 0:n], rr.ap[:, 0:n], float(extra_scale)),
                     reads=[rr], writes=[rr])

    for tc in range(NT):
        t0 = tc * 128
        par = tc % 2
        rope_tabs(tc, ctx.qw_bc, cw1q, sw2q, cw2q, sw1q)
        rope_tabs(tc, ctx.kw_bc, cw1k, sw2k, cw2k, sw1k)
        ps = [ps_take(ctx) for _ in range(4)]
        for c4 in range(4):
            def mme(e, tc=tc, c4=c4, p=ps[c4]):
                for cc in range(NC):
                    inst = e.matmul(
                        p.ap, ctx.XNF[cc].ap[:, tc * 128:(tc + 1) * 128],
                        W[cc].ap[:, c4 * 512:(c4 + 1) * 512],
                        start=(cc == 0), stop=(cc == NC - 1))
                return inst
            sch.task("pe", mme, reads=ctx.XNF + W, writes=[ps[c4]])
        sl, rr = ssl[par], rstd[par]
        jk = junk[par]
        for c4 in range(4):
            for j in range(4):
                sch.task("act", lambda e, p=ps[c4], j=j, h=c4 * 4 + j,
                         jk=jk, sl=sl: e.activation(
                    jk.ap, p.ap[:, j * 128:(j + 1) * 128], AF.Square,
                    accum_out=sl.ap[:, h:h + 1]),
                         reads=[ps[c4]], writes=[jk, sl])
        rstd_batch(sl, rr, 16, ISQ)
        q, a, b, oh = qn[par], ma[par], mb[par], qh[par]
        for c4 in range(4):
            for j in range(4):
                hidx = c4 * 4 + j
                sch.task("dve", lambda e, p=ps[c4], j=j, h=hidx, q=q, rr=rr:
                         e.tensor_scalar_mul(
                             q.ap, p.ap[:, j * 128:(j + 1) * 128],
                             rr.ap[:, h:h + 1]),
                         reads=[ps[c4], rr], writes=[q])
                off = hidx * 128
                sch.task("dve", lambda e, a=a, q=q: e.tensor_tensor(
                    a.ap, q.ap[:, 0:64], cw1q.ap, ALU.mult),
                         reads=[q, cw1q], writes=[a])
                sch.task("dve", lambda e, b=b, q=q: e.tensor_tensor(
                    b.ap, q.ap[:, 64:128], sw2q.ap, ALU.mult),
                         reads=[q, sw2q], writes=[b])
                sch.task("dve", lambda e, o=off, a=a, b=b, oh=oh: e.tensor_tensor(
                    oh.ap[:, o:o + 64], a.ap, b.ap, ALU.subtract),
                         reads=[a, b], writes=[oh])
                sch.task("dve", lambda e, a=a, q=q: e.tensor_tensor(
                    a.ap, q.ap[:, 64:128], cw2q.ap, ALU.mult),
                         reads=[q, cw2q], writes=[a])
                sch.task("dve", lambda e, b=b, q=q: e.tensor_tensor(
                    b.ap, q.ap[:, 0:64], sw1q.ap, ALU.mult),
                         reads=[q, sw1q], writes=[b])
                sch.task("dve", lambda e, o=off, a=a, b=b, oh=oh: e.tensor_tensor(
                    oh.ap[:, o + 64:o + 128], a.ap, b.ap, ALU.add),
                         reads=[a, b], writes=[oh])
        sch.task(QP, E_dma(d["Q1"][t0:t0 + 128, :], oh.ap),
                 reads=[oh], writes=db["Q1"], dma=QP)
        pk = ps_take(ctx)

        def mme2(e, p=pk, tc=tc):
            for cc in range(NC):
                inst = e.matmul(
                    p.ap, ctx.XNF[cc].ap[:, tc * 128:(tc + 1) * 128],
                    kw[cc].ap, start=(cc == 0), stop=(cc == NC - 1))
            return inst
        sch.task("pe", mme2, reads=ctx.XNF + kw, writes=[pk])
        for j in range(4):
            sch.task("act", lambda e, p=pk, j=j, jk=jk, sl=sl: e.activation(
                jk.ap, p.ap[:, j * 128:(j + 1) * 128], AF.Square,
                accum_out=sl.ap[:, j:j + 1]),
                     reads=[pk], writes=[jk, sl])
        rstd_batch(sl, rr, 4, 1.0)
        ok = kh[par]
        for j in range(4):
            sch.task("dve", lambda e, p=pk, j=j, q=q, rr=rr: e.tensor_scalar_mul(
                q.ap, p.ap[:, j * 128:(j + 1) * 128], rr.ap[:, j:j + 1]),
                     reads=[pk, rr], writes=[q])
            off = j * 128
            sch.task("dve", lambda e, a=a, q=q: e.tensor_tensor(
                a.ap, q.ap[:, 0:64], cw1k.ap, ALU.mult),
                     reads=[q, cw1k], writes=[a])
            sch.task("dve", lambda e, b=b, q=q: e.tensor_tensor(
                b.ap, q.ap[:, 64:128], sw2k.ap, ALU.mult),
                     reads=[q, sw2k], writes=[b])
            sch.task("dve", lambda e, o=off, a=a, b=b, ok=ok: e.tensor_tensor(
                ok.ap[:, o:o + 64], a.ap, b.ap, ALU.subtract),
                     reads=[a, b], writes=[ok])
            sch.task("dve", lambda e, a=a, q=q: e.tensor_tensor(
                a.ap, q.ap[:, 64:128], cw2k.ap, ALU.mult),
                     reads=[q, cw2k], writes=[a])
            sch.task("dve", lambda e, b=b, q=q: e.tensor_tensor(
                b.ap, q.ap[:, 0:64], sw1k.ap, ALU.mult),
                     reads=[q, sw1k], writes=[b])
            sch.task("dve", lambda e, o=off, a=a, b=b, ok=ok: e.tensor_tensor(
                ok.ap[:, o + 64:o + 128], a.ap, b.ap, ALU.add),
                     reads=[a, b], writes=[ok])
        sch.task(QP, E_dma(d["K1"][t0:t0 + 128, :], ok.ap),
                 reads=[ok], writes=db["K1"], dma=QP)
    for cc in range(NC):
        sch.task("pool", E_dma(kw[cc].ap, d["vwT"][cc * 128:(cc + 1) * 128, :]),
                 writes=[kw[cc]], dma="pool")
    for tc in range(NT):
        t0 = tc * 128
        par = tc % 2
        pv = ps_take(ctx)

        def mmev(e, p=pv, tc=tc):
            for cc in range(NC):
                inst = e.matmul(
                    p.ap, ctx.XNF[cc].ap[:, tc * 128:(tc + 1) * 128],
                    kw[cc].ap, start=(cc == 0), stop=(cc == NC - 1))
            return inst
        sch.task("pe", mmev, reads=ctx.XNF + kw, writes=[pv])
        sch.task("act", lambda e, p=pv, o=vh[par]: e.activation(
            o.ap, p.ap, AF.Copy), reads=[pv], writes=[vh[par]])
        sch.task(QP, E_dma(d["V1"][t0:t0 + 128, :], vh[par].ap),
                 reads=[vh[par]], writes=db["V1"], dma=QP)



def phase_oproj(ctx):
    sch, d, db, ar = ctx.sch, ctx.d, ctx.db, ctx.arena
    ar.reset()
    for hc in range(NC):
        sch.task(QP, E_dma(ctx.XNF[hc].ap, d["AFM"][hc * 128:(hc + 1) * 128, :]),
                 reads=dbx(ctx, "AFM", hc * 128, (hc + 1) * 128),
                 writes=[ctx.XNF[hc]], dma=QP)
    W = [ar.alloc([128, C], BF16, f"ow{i}") for i in range(NC)]
    for cc in range(NC):
        sch.task("pool", E_dma(W[cc].ap, d["owT"][cc * 128:(cc + 1) * 128, :]),
                 writes=[W[cc]], dma="pool")
    fnw = ar.alloc([128, C], F32, "fnwbc")
    sch.task("pool", E_dma(fnw.ap, d["fnw"].partition_broadcast(128)),
             writes=[fnw], dma="pool")
    xt = [ar.alloc([128, C], BF16, "p8x") for _ in range(2)]
    hid = [ar.alloc([128, C], BF16, "p8h") for _ in range(2)]
    h2 = [ar.alloc([128, C], BF16, "p8h2") for _ in range(2)]
    sm = [dict(ss=ar.alloc([128, 1], F32, "ss"),
               rstd=ar.alloc([128, 1], F32, "rstd"),
               junk=ar.alloc([128, C], BF16, "junk"),
               t1=ar.alloc([128, C], BF16, "t1")) for _ in range(2)]
    for tc in range(NT):
        t0 = tc * 128
        par = tc % 2
        xb, hh, h2b, s = xt[par], hid[par], h2[par], sm[par]
        sch.task(QP, E_dma(xb.ap, d["XN2"][t0:t0 + 128, :]),
                 reads=dbx(ctx, "XN2", t0, t0 + 128), writes=[xb], dma=QP)
        for c4 in range(4):
            p = ps_take(ctx)

            def mme(e, tc=tc, c4=c4, p=p):
                for hc in range(NC):
                    inst = e.matmul(
                        p.ap, ctx.XNF[hc].ap[:, tc * 128:(tc + 1) * 128],
                        W[hc].ap[:, c4 * 512:(c4 + 1) * 512],
                        start=(hc == 0), stop=(hc == NC - 1))
                return inst
            sch.task("pe", mme, reads=ctx.XNF + W, writes=[p])
            sch.task("dve", lambda e, p=p, c4=c4, a=xb, o=hh: e.tensor_tensor(
                o.ap[:, c4 * 512:(c4 + 1) * 512],
                a.ap[:, c4 * 512:(c4 + 1) * 512], p.ap, ALU.add),
                     reads=[p, xb], writes=[hh])
        sch.task(QP, E_dma(d["HID"][t0:t0 + 128, :], hh.ap),
                 reads=[hh], writes=db["HID"], dma=QP)
        emit_rms(ctx, s, hh, hh.ap, fnw, h2b, h2b.ap)
        sch.task(QP, E_dma(d["H2"][t0:t0 + 128, :], h2b.ap),
                 reads=[h2b], writes=dbx(ctx, "H2", t0, t0 + 128), dma=QP)



def scan_setup(ctx):
    sch, d, db, ar = ctx.sch, ctx.d, ctx.db, ctx.arena
    st = Ctx()
    xp = ar.alloc([128, NC * 160], BF16, "xpT")
    dt = ar.alloc([128, C], BF16, "dtT")
    sch.task("pool", E_dma_gp(xp.ap, d["xpT"], 160), writes=[xp], dma="pool")
    sch.task("pool", E_dma(dt.ap, d["dtT"]), writes=[dt], dma="pool")
    sch.task("dve", lambda e: e.memset(ctx.hst.ap, 0.0), writes=[ctx.hst])

    st.xp, st.dt = xp, dt
    st.u = [ar.alloc([128, 512], BF16, f"u{i}") for i in range(2)]
    st.u2 = [ar.alloc([128, 512], BF16, f"u2_{i}") for i in range(2)]
    st.dl = [ar.alloc([128, 512], F32, f"dl{i}") for i in range(2)]
    st.yac = [ar.alloc([128, 512], F32, f"y{i}") for i in range(2)]
    st.Bbc = [ar.alloc([128, 512], BF16, f"Bb{i}") for i in range(DST)]
    st.Cbc = [ar.alloc([128, 512], BF16, f"Cb{i}") for i in range(DST)]
    st.dr = [ar.alloc([128, 512], BF16, "dr")]
    st.dA = [ar.alloc([128, 512], F32, "dA") for _ in range(3)]
    st.dB = [ar.alloc([128, 512], BF16, "dB") for _ in range(3)]
    st.hsc = [ar.alloc([128, 512], BF16, "hsc") for _ in range(3)]
    st.yt = [ar.alloc([128, 512], BF16, "yt") for _ in range(3)]
    st.du = [ar.alloc([128, 512], BF16, "du")]
    st.t0b = [ar.alloc([128, 512], F32, "t0b") for _ in range(2)]
    st.nab = [ar.alloc([128, 512], F32, "nab") for _ in range(2)]
    st.bcv = [ar.alloc([32, 512], F32, "bcv")]
    st.ybf = [ar.alloc([128, 512], BF16, "ybf")]
    return st


def scan_chunk(ctx, st, t4):
    sch, d, db = ctx.sch, ctx.d, ctx.db
    xp, dt = st.xp, st.dt
    u, dl, yac, Bbc, Cbc = st.u, st.dl, st.yac, st.Bbc, st.Cbc
    dr, dA, dB, hsc, yt, du = st.dr, st.dA, st.dB, st.hsc, st.yt, st.du
    t0b, nab, bcv, ybf = st.t0b, st.nab, st.bcv, st.ybf
    u2 = st.u2
    if True:
        t0 = t4 * 512
        pe1 = ps_take(ctx)
        pe2 = ps_take(ctx)
        for cc in range(NC):
            ub = u[cc % 2]
            tload(ctx, ub, ub.ap,
                  d["XN2"][t0:t0 + 512, cc * 128:(cc + 1) * 128],
                  dbx(ctx, "XN2", t0, t0 + 512))
            sch.task("pe", lambda e, p=pe1, ub=ub, cc=cc: e.matmul(
                p.ap, xp.ap[:, cc * 160:cc * 160 + 128], ub.ap,
                start=(cc == 0), stop=(cc == NC - 1)),
                     reads=[ub, xp], writes=[pe1])
            sch.task("pe", lambda e, p=pe2, ub=ub, cc=cc: e.matmul(
                p.ap[0:32, :], xp.ap[:, cc * 160 + 128:cc * 160 + 160], ub.ap,
                start=(cc == 0), stop=(cc == NC - 1)),
                     reads=[ub, xp], writes=[pe2])
        drb = dr[0]
        sch.task("act", lambda e, p=pe1, o=drb: e.activation(o.ap, p.ap, AF.Copy),
                 reads=[pe1], writes=[drb])
        bc = bcv[0]
        sch.task("dve", lambda e, p=pe2, o=bc: e.tensor_copy(
            o.ap, p.ap[0:32, :]), reads=[pe2], writes=[bc])
        sch.task(QP, E_dma(d["BCR"][:, t0:t0 + 512], bc.ap),
                 reads=[bc], writes=db["BCR"], dma=QP)
        for n in range(DST):
            sch.task("pool", E_dma(
                Bbc[n].ap,
                d["BCR"][n:n + 1, t0:t0 + 512].partition_broadcast(128)),
                     reads=db["BCR"], writes=[Bbc[n]], dma="pool")
            sch.task("pool", E_dma(
                Cbc[n].ap,
                d["BCR"][DST + n:DST + n + 1, t0:t0 + 512].partition_broadcast(128)),
                     reads=db["BCR"], writes=[Cbc[n]], dma="pool")
        for dc in range(NC):
            pd = ps_take(ctx)
            sch.task("pe", lambda e, p=pd, dc=dc, b=drb: e.matmul(
                p.ap, dt.ap[:, dc * 128:(dc + 1) * 128], b.ap,
                start=True, stop=True),
                     reads=[dt, drb], writes=[pd])
            tt, nn = t0b[dc % 2], nab[dc % 2]
            sch.task("dve", lambda e, p=pd, dc=dc, o=tt: e.tensor_scalar(
                o.ap, p.ap, ctx.dtb.ap[:, dc:dc + 1], None, ALU.add),
                     reads=[pd, ctx.dtb], writes=[tt])
            sch.task("dve", lambda e, a=tt, o=nn: e.scalar_tensor_tensor(
                o.ap, a.ap, -1.0, a.ap, ALU.mult, ALU.min),
                     reads=[tt], writes=[nn])
            sch.task("act", lambda e, o=nn: e.activation(o.ap, o.ap, AF.Exp),
                     reads=[nn], writes=[nn])
            sch.task("act", lambda e, o=nn: e.activation(o.ap, o.ap, AF.Ln,
                                                         bias=1.0),
                     reads=[nn], writes=[nn])
            sch.task("dve", lambda e, a=tt, o=tt: e.tensor_single_scalar(
                o.ap, a.ap, 0.0, ALU.max),
                     reads=[tt], writes=[tt])
            sch.task("dve", lambda e, a=tt, b=nn, o=dl[dc % 2]: e.tensor_tensor(
                o.ap, a.ap, b.ap, ALU.add),
                     reads=[tt, nn], writes=[dl[dc % 2]])
            ub2 = u2[dc % 2]
            tload(ctx, ub2, ub2.ap,
                  d["XN2"][t0:t0 + 512, dc * 128:(dc + 1) * 128],
                  dbx(ctx, "XN2", t0, t0 + 512))
            sch.task("dve", lambda e, a=dl[dc % 2], b=ub2, o=du[0]:
                     e.tensor_tensor(o.ap, a.ap, b.ap, ALU.mult),
                     reads=[dl[dc % 2], ub2], writes=[du[0]])
            for n in range(DST):
                col = dc * DST + n
                da, db_, hs, yy = dA[n % 3], dB[n % 3], hsc[n % 3], yt[n % 3]
                sch.task("act", lambda e, a=dl[dc % 2], o=da, col=col: e.activation(
                    o.ap, a.ap, AF.Exp, scale=ctx.A_sb.ap[:, col:col + 1]),
                         reads=[dl[dc % 2], ctx.A_sb], writes=[da])
                sch.task("dve", lambda e, a=du[0], b=Bbc[n], o=db_:
                         e.tensor_tensor(o.ap, a.ap, b.ap, ALU.mult),
                         reads=[du[0], Bbc[n]], writes=[db_])
                sch.task("dve", lambda e, a=da, b=db_, o=hs, col=col:
                         e.tensor_tensor_scan(o.ap, a.ap, b.ap,
                                              ctx.hst.ap[:, col:col + 1],
                                              ALU.mult, ALU.add),
                         reads=[da, db_, ctx.hst], writes=[hs])
                sch.task("dve", lambda e, a=hs, col=col: e.tensor_copy(
                    ctx.hst.ap[:, col:col + 1], a.ap[:, 511:512]),
                         reads=[hs], writes=[ctx.hst])
                sch.task("dve", lambda e, a=hs, b=Cbc[n], o=yy:
                         e.tensor_tensor(o.ap, a.ap, b.ap, ALU.mult),
                         reads=[hs, Cbc[n]], writes=[yy])
                if n == 0:
                    sch.task("pool", lambda e, a=yy, o=yac[dc % 2]: e.tensor_copy(
                        o.ap, a.ap), reads=[yy], writes=[yac[dc % 2]])
                else:
                    sch.task("pool", lambda e, a=yy, o=yac[dc % 2]: e.tensor_tensor(
                        o.ap, o.ap, a.ap, ALU.add),
                             reads=[yy, yac[dc % 2]], writes=[yac[dc % 2]])
            sch.task("dve", lambda e, a=ub2, o=yac[dc % 2], dc=dc:
                     e.scalar_tensor_tensor(o.ap, a.ap,
                                            ctx.D_sb.ap[:, dc:dc + 1],
                                            o.ap, ALU.mult, ALU.add),
                     reads=[ub2, ctx.D_sb, yac[dc % 2]], writes=[yac[dc % 2]])
            yb = ybf[0]
            sch.task("dve", lambda e, a=yac[dc % 2], o=yb: e.tensor_copy(o.ap, a.ap),
                     reads=[yac[dc % 2]], writes=[yb])
            sch.task(QP, E_dma(d["Y"][dc * 128:(dc + 1) * 128, t0:t0 + 512],
                                 yb.ap),
                     reads=[yb], writes=db["Y"], dma=QP)



def phase_moe(ctx):
    sch, d, db, ar = ctx.sch, ctx.d, ctx.db, ctx.arena
    ar.reset()
    fill_xnf(ctx, "H2")
    moe = [ar.alloc([128, C], BF16, f"moe{i}") for i in range(8)]
    gsl = [ar.alloc([128, 16 * 256], BF16, "mgsl") for _ in range(2)]
    usl = [ar.alloc([128, 16 * 256], BF16, "musl") for _ in range(2)]
    hb = [ar.alloc([128, 1024], BF16, f"mhb{i}") for i in range(NI1)]
    dslb = [ar.alloc([128, 6 * 512], BF16, "mdsl") for _ in range(2)]
    sgt = [ar.alloc([128, 512], BF16, "msg") for _ in range(2)]
    sut = [ar.alloc([128, 512], BF16, "msu") for _ in range(2)]
    r_m1 = ar.alloc([128, 1], F32, "rm1"); r_m2 = ar.alloc([128, 1], F32, "rm2")
    r_tr = ar.alloc([128, 8], F32, "rtr"); r_g2 = ar.alloc([128, 8], F32, "rg2")
    r_p = ar.alloc([128, 8], F32, "rp"); r_sg = ar.alloc([128, 8], F32, "rsg")
    r_ms = ar.alloc([128, 8], F32, "rms"); r_s = ar.alloc([128, 1], F32, "rS")
    for tc in range(NT):
        pl = ps_take(ctx)

        def mmr(e, tc=tc, p=pl):
            for cc in range(NC):
                inst = e.matmul(p.ap[:, 0:8],
                                ctx.XNF[cc].ap[:, tc * 128:(tc + 1) * 128],
                                ctx.rw_sb.ap[:, cc * 8:(cc + 1) * 8],
                                start=(cc == 0), stop=(cc == NC - 1))
            return inst
        sch.task("pe", mmr, reads=ctx.XNF + [ctx.rw_sb], writes=[pl])
        sch.task("dve", lambda e, p=pl: e.reduce_max(
            r_m1.ap, p.ap[:, 0:8], axis=AX.X, negate=True),
                 reads=[pl], writes=[r_m1])
        sch.task("act", lambda e, p=pl: e.activation(
            r_p.ap, p.ap[:, 0:8], AF.Exp, bias=r_m1.ap, accum_out=r_s.ap),
                 reads=[pl, r_m1], writes=[r_p, r_s])
        sch.task("dve", lambda e: e.reciprocal(r_s.ap, r_s.ap),
                 reads=[r_s], writes=[r_s])
        sch.task("dve", lambda e: e.tensor_scalar_mul(r_p.ap, r_p.ap, r_s.ap),
                 reads=[r_p, r_s], writes=[r_p])
        sch.task("dve", lambda e: e.reduce_max(r_m1.ap, r_p.ap, axis=AX.X),
                 reads=[r_p], writes=[r_m1])
        sch.task("dve", lambda e: e.memset(r_tr.ap, -1e38), writes=[r_tr])
        sch.task("dve", lambda e: e.tensor_copy(r_tr.ap[:, 0:1], r_m1.ap),
                 reads=[r_m1], writes=[r_tr])
        sch.task("dve", lambda e: e.match_replace(r_g2.ap, r_tr.ap, r_p.ap, -1e30),
                 reads=[r_tr, r_p], writes=[r_g2])
        sch.task("dve", lambda e: e.reduce_max(r_m2.ap, r_g2.ap, axis=AX.X),
                 reads=[r_g2], writes=[r_m2])
        sch.task("act", lambda e: e.activation(r_sg.ap, r_p.ap, AF.Sign,
                                               scale=-1.0, bias=r_m2.ap),
                 reads=[r_p, r_m2], writes=[r_sg])
        sch.task("dve", lambda e: e.tensor_single_scalar(
            r_ms.ap, r_sg.ap, 0.0, ALU.max), reads=[r_sg], writes=[r_ms])
        sch.task("dve", lambda e: e.tensor_scalar(
            r_ms.ap, r_ms.ap, -1.0, 1.0, ALU.mult, ALU.add),
                 reads=[r_ms], writes=[r_ms])
        sch.task("dve", lambda e: e.tensor_tensor(
            r_m1.ap, r_m1.ap, r_m2.ap, ALU.add), reads=[r_m1, r_m2], writes=[r_m1])
        sch.task("dve", lambda e: e.tensor_scalar_add(r_m1.ap, r_m1.ap, 1e-9),
                 reads=[r_m1], writes=[r_m1])
        sch.task("dve", lambda e: e.reciprocal(r_m1.ap, r_m1.ap),
                 reads=[r_m1], writes=[r_m1])
        sch.task("dve", lambda e, tc=tc: e.scalar_tensor_tensor(
            ctx.comb[tc].ap, r_p.ap, r_m1.ap, r_ms.ap, ALU.mult, ALU.mult),
                 reads=[r_p, r_m1, r_ms], writes=[ctx.comb[tc]])

    gu_loaded = set()

    def gu_load(th, e_i, icg):
        key = (th, e_i, icg)
        if key in gu_loaded:
            return
        gu_loaded.add(key)
        i0 = icg * 256
        w = min(256, I1 - i0)
        gb, ub = gsl[icg % 2], usl[icg % 2]
        sch.task("pool", E_dma_gp(
            gb.ap[:, 0:NC * w], d["wgT"][e_i, :, i0:i0 + w], w),
                 writes=[gb], dma="pool")
        sch.task("pool", E_dma_gp(
            ub.ap[:, 0:NC * w], d["wuT"][e_i, :, i0:i0 + w], w),
                 writes=[ub], dma="pool")

    for th in range(2):
        th0 = th * 1024
        for e_i in range(NE):
            for icg in range(6):
                i0 = icg * 256
                w = min(256, I1 - i0)
                gb, ub = gsl[icg % 2], usl[icg % 2]
                gu_load(th, e_i, icg)
                for j in range(w // 128):
                    ic = icg * 2 + j
                    for t5 in range(2):
                        t0 = th0 + t5 * 512
                        pg = ps_take(ctx)
                        pu = ps_take(ctx)
                        for p, wb in ((pg, gb), (pu, ub)):
                            def mme(e, p=p, wb=wb, j=j, w=w, t0=t0):
                                for cc in range(NC):
                                    inst = e.matmul(
                                        p.ap,
                                        wb.ap[:, cc * w + j * 128:
                                              cc * w + (j + 1) * 128],
                                        ctx.XNF[cc].ap[:, t0:t0 + 512],
                                        start=(cc == 0), stop=(cc == NC - 1))
                                return inst
                            sch.task("pe", mme, reads=ctx.XNF + [wb], writes=[p])
                        sg, su = sgt[t5], sut[t5]
                        sch.task("act", lambda e, p=pg, o=sg: e.activation(
                            o.ap, p.ap, AF.Silu), reads=[pg], writes=[sg])
                        sch.task("act", lambda e, p=pu, o=su: e.activation(
                            o.ap, p.ap, AF.Copy), reads=[pu], writes=[su])
                        sch.task("dve", lambda e, a=sg, b=su, o=hb[ic], t5=t5:
                                 e.tensor_tensor(
                                     o.ap[:, t5 * 512:(t5 + 1) * 512],
                                     a.ap, b.ap, ALU.mult),
                                 reads=[sg, su], writes=[hb[ic]])
            if e_i + 1 < NE:
                gu_load(th, e_i + 1, 0)
                gu_load(th, e_i + 1, 1)
            elif th == 0:
                gu_load(1, 0, 0)
                gu_load(1, 0, 1)
            for co4 in range(4):
                co0 = co4 * 512
                for half in range(2):
                    i0 = half * 6 * 128
                    ni = 6 if half == 0 else 5
                    sch.task("pool", E_dma_gp(
                        dslb[half].ap[:, 0:ni * 512],
                        d["wdT"][e_i, i0:i0 + ni * 128, co0:co0 + 512], 512),
                             writes=[dslb[half]], dma="pool")
                for tci in range(8):
                    tc = th * 8 + tci
                    psd = ps_take(ctx)

                    def mmd(e, tci=tci, p=psd, b0=dslb[0], b1=dslb[1]):
                        for ic in range(NI1):
                            half = ic // 6
                            k = ic % 6
                            wb = b0 if half == 0 else b1
                            inst = e.matmul(
                                p.ap, hb[ic].ap[:, tci * 128:(tci + 1) * 128],
                                wb.ap[:, k * 512:(k + 1) * 512],
                                start=(ic == 0), stop=(ic == NI1 - 1))
                        return inst
                    sch.task("pe", mmd, reads=hb + [dslb[0], dslb[1]],
                             writes=[psd])
                    mtile = moe[tci]
                    if e_i == 0:
                        sch.task("dve", lambda e, p=psd, tc=tc, co0=co0, m=mtile:
                                 e.tensor_scalar_mul(
                                     m.ap[:, co0:co0 + 512], p.ap,
                                     ctx.comb[tc].ap[:, 0:1]),
                                 reads=[psd, ctx.comb[tc]], writes=[mtile])
                    else:
                        sch.task("dve", lambda e, p=psd, tc=tc, co0=co0, ei=e_i,
                                 m=mtile:
                                 e.scalar_tensor_tensor(
                                     m.ap[:, co0:co0 + 512], p.ap,
                                     ctx.comb[tc].ap[:, ei:ei + 1],
                                     m.ap[:, co0:co0 + 512],
                                     ALU.mult, ALU.add),
                                 reads=[psd, ctx.comb[tc], mtile],
                                 writes=[mtile])
        for tci in range(8):
            tc = th * 8 + tci
            sch.task(QP, E_dma(d["MOE"][tc * 128:(tc + 1) * 128, :],
                                 moe[tci].ap),
                     reads=[moe[tci]], writes=db["MOE"], dma=QP)



def phase_final(ctx):
    sch, d, db, ar = ctx.sch, ctx.d, ctx.db, ctx.arena
    ar.reset()
    x1 = [ar.alloc([128, C], F32, "fx1") for _ in range(2)]
    hid = [ar.alloc([128, C], BF16, "fhid") for _ in range(2)]
    mo = [ar.alloc([128, C], BF16, "fmo") for _ in range(2)]
    s2 = [ar.alloc([128, C], BF16, "fs2") for _ in range(2)]
    g2 = [ar.alloc([128, C], BF16, "fg2") for _ in range(2)]
    for tc in range(NT):
        t0 = tc * 128
        par = tc % 2
        xb, hh, mm, ss, gg = x1[par], hid[par], mo[par], s2[par], g2[par]
        sch.task(QP, E_dma(xb.ap, d["X1"][t0:t0 + 128, :]),
                 reads=db["X1"], writes=[xb], dma=QP)
        sch.task(QP, E_dma(hh.ap, d["HID"][t0:t0 + 128, :]),
                 reads=db["HID"], writes=[hh], dma=QP)
        sch.task(QP, E_dma(mm.ap, d["MOE"][t0:t0 + 128, :]),
                 reads=db["MOE"], writes=[mm], dma=QP)
        for c4 in range(4):
            tload(ctx, ss, ss.ap[:, c4 * 512:(c4 + 1) * 512],
                  d["Y"][c4 * 512:(c4 + 1) * 512, t0:t0 + 128], db["Y"])
        sch.task("dve", lambda e, a=hh, b=mm, o=gg: e.tensor_tensor(
            o.ap, a.ap, b.ap, ALU.add), reads=[hh, mm], writes=[gg])
        sch.task("act", lambda e, o=gg: e.activation(o.ap, o.ap, AF.Silu),
                 reads=[gg], writes=[gg])
        sch.task("dve", lambda e, a=ss, b=gg, o=ss: e.tensor_tensor(
            o.ap, a.ap, b.ap, ALU.mult), reads=[ss, gg], writes=[ss])
        sch.task("dve", lambda e, a=xb, b=ss: e.tensor_tensor(
            a.ap, a.ap, b.ap, ALU.add), reads=[xb, ss], writes=[xb])
        dep = sch.task(QP, E_dma(d["out"][t0:t0 + 128, :], xb.ap),
                       reads=[xb], dma=QP)
        ctx.stores.append(dep)



PHASES = ("ln0", "qkv0", "attn0", "ffn0", "combine0", "rms_attn", "qkv1",
          "attn1", "oproj", "scan", "moe", "final")


def build(dev_taps=(), upto=None):
    ctx = Ctx()
    ctx.nc = bass.Bass("TRN2", target_bir_lowering=False)
    ctx.sch = Sched(ctx.nc)
    _mk_dram(ctx, dev_taps)
    _mk_static(ctx)
    ctx.stores = []
    load_consts(ctx)
    def attn1_with_scan(c):
        phase_attn(c, 1)

    def scan_phase(c):
        c.arena.reset()
        stt_ = scan_setup(c)
        for t4 in range(4):
            scan_chunk(c, stt_, t4)

    emit = {
        "ln0": phase_ln0,
        "qkv0": phase_qkv0,
        "attn0": lambda c: phase_attn(c, 0),
        "ffn0": phase_ffn0,
        "combine0": phase_combine0,
        "rms_attn": phase_rms_attn,
        "qkv1": phase_qkv1,
        "attn1": attn1_with_scan,
        "oproj": phase_oproj,
        "scan": scan_phase,
        "moe": phase_moe,
        "final": phase_final,
    }
    for ph in PHASES:
        emit[ph](ctx)
        if upto is not None and ph == upto:
            break
    deps = list(ctx.stores)
    for name, bl in ctx.db.items():
        for b in bl:
            if b.writer is not None:
                deps.append(b.writer)
    ctx.sch.barrier_wait(QP, deps)
    ctx.sch.build()
    return ctx


def host_prep(inputs):
    f = lambda a: np.ascontiguousarray(a, dtype=np.float32)
    p = {}
    p["wqT"] = f(inputs["L0_attn_wq"].T)
    p["wkT"] = f(inputs["L0_attn_wk"].T)
    p["wvT"] = f(inputs["L0_attn_wv"].T)
    p["tau"] = f(np.reshape(inputs["L0_attn_tau"], (1, 1)))
    p["nu"] = f(np.reshape(inputs["L0_attn_nu"], (1, 1)))
    p["gwT"] = f(inputs["L0_ffn_gate_w"].T)
    p["uwT"] = f(inputs["L0_ffn_up_w"].T)
    p["dwT"] = f(inputs["L0_ffn_down_w"].T)
    p["gs"] = f(np.reshape(inputs["L0_ffn_gate_s"], (NI0, 128)).T)
    p["us"] = f(np.reshape(inputs["L0_ffn_up_s"], (NI0, 128)).T)
    p["ds_row"] = f(np.reshape(inputs["L0_ffn_down_s"], (1, C)))
    p["AlogR"] = f(np.reshape(inputs["L1_attn_A_log"], (NC, 128, DST))
                   .transpose(1, 0, 2).reshape(128, NC * DST))
    p["Dr"] = f(np.reshape(inputs["L1_attn_D"], (NC, 128)).T)
    p["xpT"] = f(inputs["L1_attn_x_proj_w"].T)
    p["dtT"] = f(inputs["L1_attn_dt_proj_w"].T)
    p["dtb"] = f(np.reshape(inputs["L1_attn_dt_proj_b"], (NC, 128)).T)
    p["anw"] = f(np.reshape(inputs["L1_ffn_attn_norm_w"], (1, C)))
    p["fnw"] = f(np.reshape(inputs["L1_ffn_ffn_norm_w"], (1, C)))
    p["qwT"] = f(inputs["L1_ffn_q_w"].T)
    p["kwT"] = f(inputs["L1_ffn_k_w"].T)
    p["vwT"] = f(inputs["L1_ffn_v_w"].T)
    p["owT"] = f(inputs["L1_ffn_o_w"].T)
    p["qnw"] = f(np.reshape(inputs["L1_ffn_q_norm_w"], (1, HD)))
    p["knw"] = f(np.reshape(inputs["L1_ffn_k_norm_w"], (1, HD)))
    p["rwT"] = f(inputs["L1_ffn_gate_w"].T)
    p["wgT"] = f(inputs["L1_ffn_wg"].transpose(0, 2, 1))
    p["wuT"] = f(inputs["L1_ffn_wu"].transpose(0, 2, 1))
    p["wdT"] = f(inputs["L1_ffn_wd"].transpose(0, 2, 1))
    freqs = 1.0 / (1000000.0 ** (np.arange(0, HD, 2, dtype=np.float32) / HD))
    tt = np.arange(T, dtype=np.float32)
    fr = np.outer(tt, freqs)
    p["rcos"] = np.cos(fr).astype(np.float32)
    p["rsin"] = np.sin(fr).astype(np.float32)
    cm = np.zeros((4, 128, 512), np.float32)
    for v in range(4):
        for pp in range(128):
            cm[v, pp, :v * 128 + pp + 1] = 1.0
    p["cmask"] = cm
    return p


def kernel(**inputs):
    prep = host_prep(inputs)
    ctx = build(DEV_TAPS)
    x = np.asarray(inputs["L0_attn_x"], dtype=np.float32)
    in_maps = []
    for c in range(8):
        m = dict(prep)
        m["x"] = np.ascontiguousarray(x[c])
        in_maps.append(m)
    res = run_bass_kernel_spmd(ctx.nc, in_maps, core_ids=list(range(8)))
    return np.stack([np.asarray(res.results[c]["out"]) for c in range(8)],
                    axis=0)


# version 11
# speedup vs baseline: 1.2638x; 1.0060x over previous
import numpy as np
import ml_dtypes
import concourse.bass as bass
import concourse.mybir as mybir
from concourse.bass_utils import run_bass_kernel_spmd

F32 = mybir.dt.float32
BF16 = mybir.dt.bfloat16
AF = mybir.ActivationFunctionType
ALU = mybir.AluOpType
AX = mybir.AxisListType

ENGINES = ("pe", "act", "dve", "pool", "sp")
_BLOCK_ATTR = {"pe": "tensor", "act": "scalar", "dve": "vector",
               "pool": "gpsimd", "sp": "sync"}

QP = "act"
QT = "sp"
T = 2048
C = 2048
NT = T // 128
NC = C // 128
NH = 16
HD = 128
I0 = 5632
NI0 = I0 // 128
NE = 8
I1 = 1408
NI1 = I1 // 128
DST = 16
ISQ = 1.0 / np.sqrt(128.0)

DEV_TAPS = ()



class Buf:
    __slots__ = ("ap", "name", "writer", "readers", "pending", "_rng")

    def __init__(self, ap, name=""):
        self.ap = ap
        self.name = name
        self.writer = None
        self.readers = []
        self.pending = []


class _Task:
    __slots__ = ("waits", "emit", "sem", "amt")


class Sched:
    DMA_RING = 12

    def __init__(self, nc):
        self.nc = nc
        self.tasks = {e: [] for e in ENGINES}
        self.cnt = {}
        self.sem_names = []
        for e in ENGINES:
            self._new_sem("p_" + e)
        self.dma_ring = {}
        self.dma_prev = {}
        for q in ("sp", "act", "pool"):
            names = []
            for i in range(self.DMA_RING):
                n = f"d_{q}{i}"
                self._new_sem(n)
                names.append(n)
            self.dma_ring[q] = names
            for n in names:
                self.dma_prev[n] = None
        self.dma_rr = {q: 0 for q in self.dma_ring}
        self.clock = {e: {} for e in ENGINES}
        self.n_waits = 0
        self.n_tasks = 0

    def _new_sem(self, name):
        self.sem_names.append(name)
        self.cnt[name] = 0

    def buf(self, ap, name=""):
        return Buf(ap, name)

    def task(self, engine, emit, reads=(), writes=(), dma=None, extra_waits=()):
        eng_clock = self.clock[engine]
        waits = {}

        def need(dep):
            if dep is None:
                return
            sem, val, _clk = dep
            if eng_clock.get(sem, 0) < val and waits.get(sem, 0) < val:
                waits[sem] = val

        for b in reads:
            need(b.writer)
        for b in writes:
            need(b.writer)
            for r in b.readers:
                need(r)
            for dp in b.pending:
                need(dp)
        for wdep in extra_waits:
            need(wdep)

        if dma is not None:
            ring = self.dma_ring[dma]
            sem = ring[self.dma_rr[dma] % len(ring)]
            self.dma_rr[dma] += 1
            amt = 16
            need(self.dma_prev[sem])
        else:
            sem, amt = "p_" + engine, 1
        self.cnt[sem] += amt
        val = self.cnt[sem]

        t = _Task()
        t.waits = sorted(waits.items())
        t.emit = emit
        t.sem, t.amt = sem, amt
        self.tasks[engine].append(t)
        self.n_waits += len(t.waits)
        self.n_tasks += 1

        for s, v in waits.items():
            if eng_clock.get(s, 0) < v:
                eng_clock[s] = v
        for b in reads:
            if b.writer is not None:
                self._merge(eng_clock, b.writer[2])
        for b in writes:
            if b.writer is not None:
                self._merge(eng_clock, b.writer[2])
            for r in b.readers:
                self._merge(eng_clock, r[2])
            b.pending = []

        clk = dict(eng_clock)
        clk[sem] = max(clk.get(sem, 0), val)
        dep = (sem, val, clk)
        if dma is not None:
            self.dma_prev[sem] = dep
        for b in writes:
            b.writer = dep
            b.readers = []
        for b in reads:
            b.readers.append(dep)
        return dep

    @staticmethod
    def _merge(dst, src):
        for s, v in src.items():
            if dst.get(s, 0) < v:
                dst[s] = v

    def barrier_wait(self, engine, deps):
        t = _Task()
        eng_clock = self.clock[engine]
        waits = {}
        for dep in deps:
            if dep is None:
                continue
            sem, val, _ = dep
            if eng_clock.get(sem, 0) < val and waits.get(sem, 0) < val:
                waits[sem] = val
        t.waits = sorted(waits.items())
        t.emit = None
        t.sem = None
        self.tasks[engine].append(t)
        for s, v in waits.items():
            if eng_clock.get(s, 0) < v:
                eng_clock[s] = v

    def build(self):
        from contextlib import ExitStack
        nc = self.nc
        with ExitStack() as ctx:
            sems = {}
            for name in self.sem_names:
                sems[name] = ctx.enter_context(nc.semaphore(name))
            block = ctx.enter_context(nc.Block())

            def mk(tasks):
                def emit_all(eng):
                    for t in tasks:
                        for s, v in t.waits:
                            eng.wait_ge(sems[s], v)
                        if t.emit is not None:
                            inst = t.emit(eng)
                            inst.then_inc(sems[t.sem], t.amt)
                return emit_all

            for e in ENGINES:
                getattr(block, _BLOCK_ATTR[e])(mk(self.tasks[e]))


class Arena:
    def __init__(self, nc, sch, name, nbytes):
        h = nc.alloc_sbuf_tensor(name, [128, nbytes], mybir.dt.uint8)
        self.base = nc.lookup_mloc(h).addr
        self.nbytes = nbytes
        self.nc = nc
        self.sch = sch
        self.off = 0
        self.live = []
        self.retired = []
        self.n = 0

    def alloc(self, shape, dtype, name=""):
        esz = 4 if dtype == F32 else 2
        fbytes = int(np.prod(shape[1:])) * esz
        off = (self.off + 31) & ~31
        assert off + fbytes <= self.nbytes, \
            f"arena overflow: {name} needs {fbytes}B at {off}/{self.nbytes}"
        self.off = off + fbytes
        h = self.nc.alloc_sbuf_tensor_at(
            f"ar{self.n}_{name}", list(shape), dtype, offset=self.base + off)
        self.n += 1
        b = self.sch.buf(h.ap(), name)
        pend = []
        for ro, rs_, deps in self.retired:
            if off < ro + rs_ and ro < off + fbytes:
                pend.extend(deps)
        b.pending = pend
        b._rng = (off, fbytes)
        self.live.append(b)
        return b

    def reset(self):
        retired = []
        for b in self.live:
            deps = list(b.readers)
            if b.writer is not None:
                deps.append(b.writer)
            if deps:
                retired.append((b._rng[0], b._rng[1], deps))
        self.retired = retired + self.retired
        if len(self.retired) > 400:
            keep, old = self.retired[:200], self.retired[200:]
            best = {}
            for _o, _s, deps in old:
                for dp in deps:
                    if best.get(dp[0], (None, -1))[1] < dp[1]:
                        best[dp[0]] = (dp, dp[1])
            self.retired = keep + [(0, self.nbytes,
                                    [v[0] for v in best.values()])]
        self.live = []
        self.off = 0


class Ctx:
    pass


def E_dma(out_ap, in_ap, **kw):
    return lambda e: e.dma_start(out=out_ap, in_=in_ap, **kw)


def dbx(ctx, name, lo=None, hi=None):
    regs = ctx.db[name]
    if lo is None or len(regs) == 1:
        return list(regs)
    nrows = ctx.d[name].shape[0]
    step = nrows // len(regs)
    return regs[lo // step:(hi + step - 1) // step]


def E_dma_gp(out_ap2d, in_ap2d, w):
    in3 = in_ap2d.rearrange("(g p) w -> p g w", p=128)
    out3 = out_ap2d.rearrange("p (g w) -> p g w", w=w)
    return lambda e: e.dma_start(out=out3, in_=in3)



def _mk_dram(ctx, dev_taps):
    nc = ctx.nc
    d = {}
    db = {}

    def din(name, shape, dt=F32):
        d[name] = nc.dram_tensor(name, list(shape), dt, kind="ExternalInput").ap()

    REGIONED = {"XN": 4, "GTM": 4, "XN2": 4, "HT": 4, "H2": 4,
                "AFM": 16, "SFM": 16}

    def dsc(name, shape, dt):
        kind = "ExternalOutput" if name in dev_taps else "Internal"
        d[name] = nc.dram_tensor(name, list(shape), dt, kind=kind).ap()
        nreg = REGIONED.get(name, 1)
        if nreg == 1:
            db[name] = [Buf(d[name], name)]
        else:
            step = shape[0] // nreg
            db[name] = [Buf(d[name][i * step:(i + 1) * step], f"{name}_{i}")
                        for i in range(nreg)]

    din("x", [T, C])
    din("wqT", [C, C]); din("wkT", [C, C]); din("wvT", [C, C])
    din("tau", [1, 1]); din("nu", [1, 1])
    din("gwT", [C, I0]); din("uwT", [C, I0]); din("dwT", [I0, C])
    din("gs", [128, NI0]); din("us", [128, NI0]); din("ds_row", [1, C])
    din("AlogR", [128, NC * DST]); din("Dr", [128, NC])
    din("xpT", [C, 160]); din("dtT", [128, C]); din("dtb", [128, NC])
    din("anw", [1, C]); din("fnw", [1, C])
    din("qwT", [C, C]); din("kwT", [C, 512]); din("vwT", [C, 512])
    din("owT", [C, C]); din("qnw", [1, HD]); din("knw", [1, HD])
    din("rwT", [C, NE])
    din("wgT", [NE, C, I1]); din("wuT", [NE, C, I1]); din("wdT", [NE, I1, C])
    din("rcos", [T, 64]); din("rsin", [T, 64])
    din("cmask", [4, 128, 512])

    d["out"] = nc.dram_tensor("out", [T, C], F32, kind="ExternalOutput").ap()

    dsc("XN", [T, C], BF16)
    dsc("Q0", [T, C], BF16); dsc("K0", [T, C], BF16); dsc("V0", [T, C], BF16)
    dsc("PD", [2, T, T], BF16)
    dsc("SFM", [C, T], BF16)
    dsc("GTM", [T, C], BF16)
    dsc("X1", [T, C], F32)
    dsc("XN2", [T, C], BF16)
    dsc("HT", [T, C], BF16)
    dsc("Q1", [T, C], BF16); dsc("K1", [T, 512], BF16); dsc("V1", [T, 512], BF16)
    dsc("AFM", [C, T], BF16)
    dsc("HID", [T, C], BF16)
    dsc("H2", [T, C], BF16)
    dsc("BCR", [32, T], F32)
    dsc("Y", [C, T], BF16)
    dsc("MOE", [T, C], BF16)
    ctx.d = d
    ctx.db = db
    ctx.pd_reg = [[Buf(d["PD"][par, i * 128:(i + 1) * 128, :], f"pd{par}_{i}")
                   for i in range(NT)] for par in range(2)]


def _mk_static(ctx):
    nc, sch = ctx.nc, ctx.sch
    A = lambda name, shape, dt: sch.buf(
        nc.alloc_sbuf_tensor(name, list(shape), dt).ap(), name)

    ctx.XNF = [A(f"xnf{i}", [128, T], BF16) for i in range(NC)]
    ctx.cmask = [A(f"cmask{i}", [128, 512], F32) for i in range(4)]
    ctx.tau_bc = A("tau_bc", [128, 1], F32)
    ctx.nu_bc = A("nu_bc", [128, 1], F32)
    ctx.gs = A("gs_sb", [128, NI0], F32)
    ctx.us = A("us_sb", [128, NI0], F32)
    ctx.A_sb = A("A_sb", [128, NC * DST], F32)
    ctx.D_sb = A("D_sb", [128, NC], F32)
    ctx.dtb = A("dtb_sb", [128, NC], F32)
    ctx.qw_bc = A("qw_bc", [128, HD], F32)
    ctx.kw_bc = A("kw_bc", [128, HD], F32)
    ctx.rc = A("rc_sb", [128, NT * 64], F32)
    ctx.rs = A("rs_sb", [128, NT * 64], F32)
    ctx.hst = A("hst", [128, NC * DST], F32)
    ctx.c_eps5 = A("c_eps5", [128, 1], F32)
    ctx.neg30k = A("neg30k", [128, 512], F32)
    ctx.c_eps6 = A("c_eps6", [128, 1], F32)
    ctx.comb = [A(f"comb{i}", [128, NE], F32) for i in range(NT)]
    ctx.rw_sb = A("rw_sb", [128, NC * NE], BF16)

    ctx.PS = [sch.buf(nc.alloc_psum_tensor(f"ps{i}", [128, 512], F32).ap(),
                      f"ps{i}") for i in range(8)]
    ctx._ps_rr = 0

    avail = (nc.sbuf_bytes_remaining - 512) & ~31
    ctx.arena = Arena(nc, sch, "arena", avail)


def ps_take(ctx, n=1):
    out = []
    for _ in range(n):
        out.append(ctx.PS[ctx._ps_rr % 8])
        ctx._ps_rr += 1
    return out if n > 1 else out[0]


def load_consts(ctx):
    sch, d = ctx.sch, ctx.d
    for i in range(4):
        sch.task(QP, E_dma(ctx.cmask[i].ap, d["cmask"][i]),
                 writes=[ctx.cmask[i]], dma=QP)

    def bc(buf, src):
        sch.task("pool", E_dma(buf.ap, src.partition_broadcast(128)),
                 writes=[buf], dma="pool")

    sch.task("dve", lambda e: e.memset(ctx.c_eps5.ap, 1e-5), writes=[ctx.c_eps5])
    sch.task("dve", lambda e: e.memset(ctx.neg30k.ap, -30000.0),
             writes=[ctx.neg30k])
    sch.task("dve", lambda e: e.memset(ctx.c_eps6.ap, 1e-6), writes=[ctx.c_eps6])
    bc(ctx.tau_bc, d["tau"])
    bc(ctx.nu_bc, d["nu"])
    bc(ctx.qw_bc, d["qnw"])
    bc(ctx.kw_bc, d["knw"])
    for name in ("gs", "us", "dtb"):
        sch.task(QP, E_dma(getattr(ctx, name).ap, d[name]),
                 writes=[getattr(ctx, name)], dma=QP)
    sch.task(QP, E_dma(ctx.D_sb.ap, d["Dr"]), writes=[ctx.D_sb], dma=QP)
    sch.task(QP, E_dma(ctx.A_sb.ap, d["AlogR"]), writes=[ctx.A_sb], dma=QP)
    sch.task("act", lambda e: e.activation(ctx.A_sb.ap, ctx.A_sb.ap, AF.Exp),
             reads=[ctx.A_sb], writes=[ctx.A_sb])
    sch.task("dve", lambda e: e.tensor_scalar_mul(ctx.A_sb.ap, ctx.A_sb.ap, -1.0),
             reads=[ctx.A_sb], writes=[ctx.A_sb])
    sch.task(QP, E_dma_gp(ctx.rc.ap, d["rcos"], 64),
             writes=[ctx.rc], dma=QP)
    sch.task(QP, E_dma_gp(ctx.rs.ap, d["rsin"], 64),
             writes=[ctx.rs], dma=QP)
    sch.task("pool", E_dma_gp(ctx.rw_sb.ap, d["rwT"], NE),
             writes=[ctx.rw_sb], dma="pool")



def tload(ctx, dst_buf, dst_ap, src_ap, reads, q=QT):
    ctx.sch.task(q, E_dma(dst_ap, src_ap, transpose=True),
                 reads=reads, writes=[dst_buf], dma=q)


def fill_xnf(ctx, src_name):
    d = ctx.d
    for t4 in range(4):
        for cc in range(NC):
            tload(ctx, ctx.XNF[cc],
                  ctx.XNF[cc].ap[:, t4 * 512:(t4 + 1) * 512],
                  d[src_name][t4 * 512:(t4 + 1) * 512, cc * 128:(cc + 1) * 128],
                  dbx(ctx, src_name, t4 * 512, (t4 + 1) * 512))


def emit_rms(ctx, sm, src_buf, src_ap, wbc_buf, dst_buf, dst_ap):
    sch = ctx.sch
    ss, rstd, junk, t1 = sm["ss"], sm["rstd"], sm["junk"], sm["t1"]
    sch.task("act", lambda e: e.activation(junk.ap, src_ap, AF.Square,
                                           accum_out=ss.ap),
             reads=[src_buf], writes=[junk, ss])
    sch.task("act", lambda e: e.activation(rstd.ap, ss.ap, AF.Ln,
                                           scale=1.0 / C, bias=ctx.c_eps6.ap),
             reads=[ss, ctx.c_eps6], writes=[rstd])
    sch.task("act", lambda e: e.activation(rstd.ap, rstd.ap, AF.Exp, scale=-0.5),
             reads=[rstd], writes=[rstd])
    sch.task("dve", lambda e: e.tensor_scalar_mul(t1.ap, src_ap, rstd.ap),
             reads=[src_buf, rstd], writes=[t1])
    sch.task("dve", lambda e: e.tensor_tensor(dst_ap, t1.ap, wbc_buf.ap, ALU.mult),
             reads=[t1, wbc_buf], writes=[dst_buf])



def phase_ln0(ctx):
    sch, d, db, ar = ctx.sch, ctx.d, ctx.db, ctx.arena
    ar.reset()
    xt = [ar.alloc([128, C], F32, "p1x") for _ in range(2)]
    xn = [ar.alloc([128, C], BF16, "p1n") for _ in range(2)]
    sm = [dict(s1=ar.alloc([128, 1], F32, "s1"), ss=ar.alloc([128, 1], F32, "ss"),
               m=ar.alloc([128, 1], F32, "m"), v=ar.alloc([128, 1], F32, "v"),
               rstd=ar.alloc([128, 1], F32, "rstd"),
               nmr=ar.alloc([128, 1], F32, "nmr"),
               junk=ar.alloc([128, C], BF16, "junk")) for _ in range(2)]
    for tc in range(NT):
        xb, nb, s = xt[tc % 2], xn[tc % 2], sm[tc % 2]
        sch.task(QP, E_dma(xb.ap, d["x"][tc * 128:(tc + 1) * 128, :]),
                 writes=[xb], dma=QP)
        s1, ss, m, v, rstd, nmr, junk = (s[k] for k in
                                         ("s1", "ss", "m", "v", "rstd", "nmr", "junk"))
        sch.task("dve", lambda e, a=xb, o=s1: e.reduce_sum(o.ap, a.ap, axis=AX.X),
                 reads=[xb], writes=[s1])
        sch.task("act", lambda e, a=xb, j=junk, o=ss: e.activation(
            j.ap, a.ap, AF.Square, accum_out=o.ap),
                 reads=[xb], writes=[junk, ss])
        sch.task("dve", lambda e, a=s1, o=m: e.tensor_scalar_mul(o.ap, a.ap, 1.0 / C),
                 reads=[s1], writes=[m])
        sch.task("dve", lambda e, a=m, o=v: e.scalar_tensor_tensor(
            o.ap, a.ap, 1.0, a.ap, ALU.mult, ALU.mult),
                 reads=[m], writes=[v])
        sch.task("dve", lambda e, a=ss, o=v: e.scalar_tensor_tensor(
            o.ap, a.ap, 1.0 / C, o.ap, ALU.mult, ALU.subtract),
                 reads=[ss, v], writes=[v])
        sch.task("act", lambda e, a=v, o=rstd: e.activation(
            o.ap, a.ap, AF.Ln, bias=ctx.c_eps5.ap),
                 reads=[v, ctx.c_eps5], writes=[rstd])
        sch.task("act", lambda e, o=rstd: e.activation(
            o.ap, o.ap, AF.Exp, scale=-0.5),
                 reads=[rstd], writes=[rstd])
        sch.task("dve", lambda e, a=m, b=rstd, o=nmr: e.scalar_tensor_tensor(
            o.ap, a.ap, -1.0, b.ap, ALU.mult, ALU.mult),
                 reads=[m, rstd], writes=[nmr])
        sch.task("dve", lambda e, a=xb, b=rstd, c=nmr, o=nb: e.tensor_scalar(
            o.ap, a.ap, b.ap, c.ap, ALU.mult, ALU.add),
                 reads=[xb, rstd, nmr], writes=[nb])
        sch.task(QP, E_dma(d["XN"][tc * 128:(tc + 1) * 128, :], nb.ap),
                 reads=[nb], writes=dbx(ctx, "XN", tc * 128, (tc + 1) * 128),
                 dma=QP)



def phase_qkv0(ctx):
    sch, d, db, ar = ctx.sch, ctx.d, ctx.db, ctx.arena
    ar.reset()
    fill_xnf(ctx, "XN")
    W = [ar.alloc([128, C], BF16, f"w0_{i}") for i in range(NC)]
    qh = [ar.alloc([128, C], BF16, "qh") for _ in range(2)]
    ssl = [ar.alloc([128, 16], F32, "ssl") for _ in range(2)]
    rf = [ar.alloc([128, 16], F32, "rf") for _ in range(2)]
    junk = [ar.alloc([128, 128], BF16, "jk") for _ in range(2)]

    for wname, dst, is_q, is_v in (("wqT", "Q0", True, False),
                                   ("wkT", "K0", False, False),
                                   ("wvT", "V0", False, True)):
        for cc in range(NC):
            sch.task("pool", E_dma(W[cc].ap,
                                   d[wname][cc * 128:(cc + 1) * 128, :]),
                     writes=[W[cc]], dma="pool")
        for tc in range(NT):
            ps = [ps_take(ctx) for _ in range(4)]
            for c4 in range(4):
                def mme(e, tc=tc, c4=c4, p=ps[c4]):
                    for cc in range(NC):
                        inst = e.matmul(
                            p.ap,
                            ctx.XNF[cc].ap[:, tc * 128:(tc + 1) * 128],
                            W[cc].ap[:, c4 * 512:(c4 + 1) * 512],
                            start=(cc == 0), stop=(cc == NC - 1))
                    return inst
                sch.task("pe", mme, reads=ctx.XNF + W, writes=[ps[c4]])
            q, sl, r, jk = qh[tc % 2], ssl[tc % 2], rf[tc % 2], junk[tc % 2]
            for c4 in range(4):
                for j in range(4):
                    hidx = c4 * 4 + j
                    sch.task("act", lambda e, p=ps[c4], j=j, h=hidx, jk=jk, sl=sl:
                             e.activation(
                                 jk.ap, p.ap[:, j * 128:(j + 1) * 128], AF.Square,
                                 accum_out=sl.ap[:, h:h + 1]),
                             reads=[ps[c4]], writes=[jk, sl])
            sch.task("act", lambda e, a=sl, o=r: e.activation(o.ap, a.ap, AF.Ln),
                     reads=[sl], writes=[r])
            sch.task("act", lambda e, o=r: e.activation(o.ap, o.ap, AF.Exp,
                                                        scale=0.5),
                     reads=[r], writes=[r])
            sch.task("dve", lambda e, o=r: e.tensor_scalar_add(o.ap, o.ap, 1e-8),
                     reads=[r], writes=[r])
            sch.task("dve", lambda e, o=r: e.reciprocal(o.ap, o.ap),
                     reads=[r], writes=[r])
            if is_q:
                sch.task("dve", lambda e, o=r: e.tensor_scalar_mul(
                    o.ap, o.ap, ctx.tau_bc.ap),
                         reads=[r, ctx.tau_bc], writes=[r])
            for c4 in range(4):
                for j in range(4):
                    hidx = c4 * 4 + j
                    sch.task("dve", lambda e, p=ps[c4], j=j, h=hidx, q=q, r=r:
                             e.tensor_scalar_mul(
                                 q.ap[:, h * 128:(h + 1) * 128],
                                 p.ap[:, j * 128:(j + 1) * 128], r.ap[:, h:h + 1]),
                             reads=[ps[c4], r], writes=[q])
            sch.task(QP, E_dma(d[dst][tc * 128:(tc + 1) * 128, :], q.ap),
                     reads=[q], writes=db[dst], dma=QP)



def phase_attn(ctx, layer, hooks=None, reset=True):
    sch, d, db, ar = ctx.sch, ctx.d, ctx.db, ctx.arena
    if reset:
        ar.reset()
    if layer == 0:
        QN, KN, VN, ON = "Q0", "K0", "V0", "SFM"
        n_kv = 16
        kv_w = C
    else:
        QN, KN, VN, ON = "Q1", "K1", "V1", "AFM"
        n_kv = 4
        kv_w = 512
    rep = NH // n_kv

    qf = [ar.alloc([128, T], BF16, "qf") for _ in range(2)]
    kf = [ar.alloc([128, T], BF16, "kf") for _ in range(2)]
    vt = [ar.alloc([128, T], BF16, "vt") for _ in range(2)]
    psb = [[ar.alloc([128, T], BF16, "psb") for _ in range(2)]
           for _ in range(2)]
    pt = [ar.alloc([128, 512], BF16, "pt") for _ in range(4)]
    att = [ar.alloc([128, 512], BF16, "att") for _ in range(4)]
    negm = [[ar.alloc([128, 4], F32, "negm") for _ in range(2)]
            for _ in range(2)]
    rssb = [[ar.alloc([128, 4], F32, "rssb") for _ in range(2)]
            for _ in range(2)]
    den = [[ar.alloc([128, 1], F32, "den") for _ in range(2)]
           for _ in range(2)]

    def emit_S(h):
        par = h % 2
        g = h // rep
        tload(ctx, qf[par], qf[par].ap,
              d[QN][:, h * 128:(h + 1) * 128], db[QN])
        if h % rep == 0:
            tload(ctx, kf[g % 2], kf[g % 2].ap,
                  d[KN][:, g * 128:(g + 1) * 128], db[KN])
            for tkc in range(NT):
                sch.task(QP, E_dma(vt[g % 2].ap[:, tkc * 128:(tkc + 1) * 128],
                                     d[VN][tkc * 128:(tkc + 1) * 128,
                                           g * 128:(g + 1) * 128]),
                         reads=db[VN], writes=[vt[g % 2]], dma=QP)
        kbuf = kf[g % 2]

        def s_pair(pair):
            cinfo = []
            for tqc in pair:
                P, nm, rs, dn = (psb[par][tqc % 2], negm[par][tqc % 2],
                                 rssb[par][tqc % 2], den[par][tqc % 2])
                nch = tqc // 4 + 1
                banks = [ps_take(ctx) for _ in range(nch)]
                cinfo.append((tqc, P, nm, rs, dn, nch, banks))
            for tqc, P, nm, rs, dn, nch, banks in cinfo:
                for c, ps in enumerate(banks):
                    sch.task("pe", lambda e, p=ps, tqc=tqc, c=c, q=qf[par],
                             k=kbuf:
                             e.matmul(p.ap, q.ap[:, tqc * 128:(tqc + 1) * 128],
                                      k.ap[:, c * 512:(c + 1) * 512],
                                      start=True, stop=True),
                             reads=[qf[par], kbuf], writes=[ps])
            for tqc, P, nm, rs, dn, nch, banks in cinfo:
                dcb = banks[nch - 1]
                mv = tqc % 4
                sch.task("dve", lambda e, p=dcb, m=mv: e.scalar_tensor_tensor(
                    p.ap, p.ap, 30000.0, ctx.cmask[m].ap, ALU.add, ALU.mult),
                         reads=[dcb, ctx.cmask[mv]], writes=[dcb])
                sch.task("dve", lambda e, p=dcb: e.tensor_scalar_add(
                    p.ap, p.ap, -30000.0),
                         reads=[dcb], writes=[dcb])
            for tqc, P, nm, rs, dn, nch, banks in cinfo:
                for c, ps in enumerate(banks):
                    sch.task("dve", lambda e, p=ps, c=c, nm=nm: e.reduce_max(
                        nm.ap[:, c:c + 1], p.ap, axis=AX.X, negate=True),
                             reads=[ps], writes=[nm])
                if nch > 1:
                    sch.task("dve", lambda e, n=nch, nm=nm, dn=dn:
                             e.tensor_reduce(
                        dn.ap, nm.ap[:, 0:n], AX.X, ALU.min),
                             reads=[nm], writes=[dn])
                else:
                    sch.task("dve", lambda e, nm=nm, dn=dn: e.tensor_copy(
                        dn.ap, nm.ap[:, 0:1]),
                             reads=[nm], writes=[dn])
            for tqc, P, nm, rs, dn, nch, banks in cinfo:
                for c, ps in enumerate(banks):
                    sch.task("act", lambda e, p=ps, c=c, P=P, dn=dn, rs=rs:
                             e.activation(
                        P.ap[:, c * 512:(c + 1) * 512], p.ap, AF.Exp,
                        bias=dn.ap, accum_out=rs.ap[:, c:c + 1]),
                             reads=[ps, dn], writes=[P, rs])
            for tqc, P, nm, rs, dn, nch, banks in cinfo:
                sch.task("dve", lambda e, n=nch, rs=rs, dn=dn: e.reduce_sum(
                    dn.ap, rs.ap[:, 0:n], axis=AX.X),
                         reads=[rs], writes=[dn])
                sch.task("dve", lambda e, dn=dn: e.reciprocal(dn.ap, dn.ap),
                         reads=[dn], writes=[dn])
                if layer == 0:
                    sch.task("dve", lambda e, dn=dn: e.tensor_scalar_mul(
                        dn.ap, dn.ap, ctx.nu_bc.ap),
                             reads=[dn, ctx.nu_bc], writes=[dn])
                sch.task("dve", lambda e, n=nch, P=P, dn=dn: e.tensor_scalar_mul(
                    P.ap[:, 0:n * 512], P.ap[:, 0:n * 512], dn.ap),
                         reads=[P, dn], writes=[P])
                sch.task(QP, E_dma(
                    d["PD"][par, tqc * 128:(tqc + 1) * 128, 0:nch * 512],
                    P.ap[:, 0:nch * 512]),
                         reads=[P], writes=[ctx.pd_reg[par][tqc]], dma=QP)
        return s_pair

    def emit_PV_group(h, g4):
        par = h % 2
        g = h // rep
        vbuf = vt[g % 2]
        ao = att[g4 % 2]
        ntk = g4 * 4 + 4
        psum = ps_take(ctx)
        for tkc in range(ntk):
            ptb = pt[tkc % 4]
            regs = [ctx.pd_reg[par][g4 * 4 + i] for i in range(4)]
            tload(ctx, ptb, ptb.ap,
                  d["PD"][par, g4 * 512:(g4 + 1) * 512,
                          tkc * 128:(tkc + 1) * 128], regs)
            sch.task("pe", lambda e, p=psum, v=vbuf, tkc=tkc, b=ptb,
                     st=(tkc == 0), sp=(tkc == ntk - 1):
                     e.matmul(p.ap, v.ap[:, tkc * 128:(tkc + 1) * 128],
                              b.ap, start=st, stop=sp),
                     reads=[vbuf, ptb], writes=[psum])
        sch.task("act", lambda e, p=psum, ao=ao: e.activation(
            ao.ap, p.ap, AF.Copy),
                 reads=[psum], writes=[ao])
        sch.task(QP, E_dma(d[ON][h * 128:(h + 1) * 128,
                                 g4 * 512:(g4 + 1) * 512], ao.ap),
                 reads=[ao], writes=dbx(ctx, ON, h * 128, (h + 1) * 128),
                 dma=QP)

    for h in range(NH):
        s_pair = emit_S(h)
        for pi in range(8):
            s_pair((2 * pi, 2 * pi + 1))
            if h > 0 and pi % 2 == 1:
                emit_PV_group(h - 1, pi // 2)
        if hooks and h in hooks:
            hooks[h]()
    for g4 in range(4):
        emit_PV_group(NH - 1, g4)



def phase_ffn0(ctx):
    sch, d, db, ar = ctx.sch, ctx.d, ctx.db, ctx.arena
    ar.reset()
    gsl = [ar.alloc([128, 16 * 256], BF16, "gsl") for _ in range(2)]
    usl = [ar.alloc([128, 16 * 256], BF16, "usl") for _ in range(2)]
    hb = [ar.alloc([128, 512], BF16, f"hb{i}") for i in range(NI0)]
    dslb = [ar.alloc([128, 11 * 512], BF16, "dslb") for _ in range(2)]
    sgt = [ar.alloc([128, 512], BF16, "sgt") for _ in range(2)]
    sut = [ar.alloc([128, 512], BF16, "sut") for _ in range(2)]
    tmpf = [ar.alloc([128, 512], F32, "tmpf") for _ in range(2)]
    gt = [ar.alloc([128, 512], BF16, "gt") for _ in range(2)]
    dsbc = ar.alloc([128, C], F32, "dsbc")
    sch.task("pool", E_dma(dsbc.ap, d["ds_row"].partition_broadcast(128)),
             writes=[dsbc], dma="pool")

    f_loaded = set()

    def f_gu_load(t4, icg):
        if (t4, icg) in f_loaded:
            return
        f_loaded.add((t4, icg))
        i0 = icg * 256
        gb, ub = gsl[icg % 2], usl[icg % 2]
        sch.task("pool", E_dma_gp(gb.ap, d["gwT"][:, i0:i0 + 256], 256),
                 writes=[gb], dma="pool")
        sch.task("pool", E_dma_gp(ub.ap, d["uwT"][:, i0:i0 + 256], 256),
                 writes=[ub], dma="pool")

    for t4 in range(4):
        t0 = t4 * 512
        for icg in range(22):
            i0 = icg * 256
            gb, ub = gsl[icg % 2], usl[icg % 2]
            f_gu_load(t4, icg)
            for j in range(2):
                ic = icg * 2 + j
                pg = ps_take(ctx)
                pu = ps_take(ctx)

                def mme(e, wb=gb, j=j, p=pg, t0=t0):
                    for cc in range(NC):
                        inst = e.matmul(
                            p.ap,
                            wb.ap[:, cc * 256 + j * 128: cc * 256 + (j + 1) * 128],
                            ctx.XNF[cc].ap[:, t0:t0 + 512],
                            start=(cc == 0), stop=(cc == NC - 1))
                    return inst
                sch.task("pe", lambda e, f=mme: f(e),
                         reads=ctx.XNF + [gb], writes=[pg])
                sch.task("pe", lambda e, wb=ub, j=j, p=pu, f=mme: f(e, wb, j, p),
                         reads=ctx.XNF + [ub], writes=[pu])
                sg, su = sgt[ic % 2], sut[ic % 2]
                sch.task("act", lambda e, p=pg, ic=ic, o=sg: e.activation(
                    o.ap, p.ap, AF.Silu, scale=ctx.gs.ap[:, ic:ic + 1]),
                         reads=[pg, ctx.gs], writes=[sg])
                sch.task("act", lambda e, p=pu, ic=ic, o=su: e.activation(
                    o.ap, p.ap, AF.Copy, scale=ctx.us.ap[:, ic:ic + 1]),
                         reads=[pu, ctx.us], writes=[su])
                sch.task("dve", lambda e, a=sg, b=su, o=hb[ic]: e.tensor_tensor(
                    o.ap, a.ap, b.ap, ALU.mult),
                         reads=[sg, su], writes=[hb[ic]])
        if t4 + 1 < 4:
            f_gu_load(t4 + 1, 0)
            f_gu_load(t4 + 1, 1)
        for co4 in range(4):
            co0 = co4 * 512
            psds = [ps_take(ctx) for _ in range(4)]
            for qr in range(4):
                r0 = qr * 11 * 128
                wb = dslb[qr % 2]
                sch.task("pool", E_dma_gp(
                    wb.ap, d["dwT"][r0:r0 + 11 * 128, co0:co0 + 512], 512),
                         writes=[wb], dma="pool")
                for tci in range(4):
                    def mmd(e, tci=tci, p=psds[tci], wb=wb, qr=qr):
                        for k in range(11):
                            ic = qr * 11 + k
                            inst = e.matmul(
                                p.ap,
                                hb[ic].ap[:, tci * 128:(tci + 1) * 128],
                                wb.ap[:, k * 512:(k + 1) * 512],
                                start=(qr == 0 and k == 0),
                                stop=(qr == 3 and k == 10))
                        return inst
                    sch.task("pe", mmd,
                             reads=hb[qr * 11:(qr + 1) * 11] + [wb],
                             writes=[psds[tci]])
            for tci in range(4):
                psd = psds[tci]
                tf, g = tmpf[tci % 2], gt[tci % 2]
                sch.task("dve", lambda e, p=psd, o=tf, co0=co0: e.tensor_tensor(
                    o.ap, p.ap, dsbc.ap[:, co0:co0 + 512], ALU.mult),
                         reads=[psd, dsbc], writes=[tf])
                sch.task("act", lambda e, a=tf, o=g: e.activation(
                    o.ap, a.ap, AF.Silu),
                         reads=[tf], writes=[g])
                tc = t4 * 4 + tci
                sch.task(QP, E_dma(
                    d["GTM"][tc * 128:(tc + 1) * 128, co0:co0 + 512], g.ap),
                         reads=[g],
                         writes=dbx(ctx, "GTM", tc * 128, (tc + 1) * 128),
                         dma=QP)



def phase_combine0(ctx):
    sch, d, db, ar = ctx.sch, ctx.d, ctx.db, ctx.arena
    ar.reset()
    xt = [ar.alloc([128, C], F32, "c0x") for _ in range(2)]
    gt = [ar.alloc([128, C], BF16, "c0g") for _ in range(2)]
    st = [ar.alloc([128, C], BF16, "c0s") for _ in range(2)]
    sg = [ar.alloc([128, C], BF16, "c0sg") for _ in range(2)]
    nb = [ar.alloc([128, C], BF16, "c0n") for _ in range(2)]
    sm = [dict(s1=ar.alloc([128, 1], F32, "s1"), ss=ar.alloc([128, 1], F32, "ss"),
               m=ar.alloc([128, 1], F32, "m"), v=ar.alloc([128, 1], F32, "v"),
               rstd=ar.alloc([128, 1], F32, "rstd"),
               nmr=ar.alloc([128, 1], F32, "nmr"),
               junk=ar.alloc([128, C], BF16, "junk")) for _ in range(2)]
    for tc in range(NT):
        t0 = tc * 128
        xb, g, s, p, nbb, smm = (xt[tc % 2], gt[tc % 2], st[tc % 2],
                                 sg[tc % 2], nb[tc % 2], sm[tc % 2])
        sch.task(QP, E_dma(xb.ap, d["x"][t0:t0 + 128, :]),
                 writes=[xb], dma=QP)
        sch.task(QP, E_dma(g.ap, d["GTM"][t0:t0 + 128, :]),
                 reads=dbx(ctx, "GTM", t0, t0 + 128), writes=[g], dma=QP)
        for c4 in range(4):
            tload(ctx, s, s.ap[:, c4 * 512:(c4 + 1) * 512],
                  d["SFM"][c4 * 512:(c4 + 1) * 512, t0:t0 + 128],
                  dbx(ctx, "SFM", c4 * 512, (c4 + 1) * 512))
        sch.task("dve", lambda e, a=s, b=g, o=p: e.tensor_tensor(
            o.ap, a.ap, b.ap, ALU.mult), reads=[s, g], writes=[p])
        sch.task("dve", lambda e, a=xb, b=p: e.tensor_tensor(
            a.ap, a.ap, b.ap, ALU.add), reads=[xb, p], writes=[xb])
        sch.task(QP, E_dma(d["X1"][t0:t0 + 128, :], xb.ap),
                 reads=[xb], writes=db["X1"], dma=QP)
        s1, ss, m, v, rstd, nmr, junk = (smm[k] for k in
                                         ("s1", "ss", "m", "v", "rstd", "nmr", "junk"))
        sch.task("dve", lambda e, a=xb, o=s1: e.reduce_sum(o.ap, a.ap, axis=AX.X),
                 reads=[xb], writes=[s1])
        sch.task("act", lambda e, a=xb, j=junk, o=ss: e.activation(
            j.ap, a.ap, AF.Square, accum_out=o.ap),
                 reads=[xb], writes=[junk, ss])
        sch.task("dve", lambda e, a=s1, o=m: e.tensor_scalar_mul(o.ap, a.ap, 1.0 / C),
                 reads=[s1], writes=[m])
        sch.task("dve", lambda e, a=m, o=v: e.scalar_tensor_tensor(
            o.ap, a.ap, 1.0, a.ap, ALU.mult, ALU.mult), reads=[m], writes=[v])
        sch.task("dve", lambda e, a=ss, o=v: e.scalar_tensor_tensor(
            o.ap, a.ap, 1.0 / C, o.ap, ALU.mult, ALU.subtract),
                 reads=[ss, v], writes=[v])
        sch.task("act", lambda e, a=v, o=rstd: e.activation(
            o.ap, a.ap, AF.Ln, bias=ctx.c_eps5.ap),
                 reads=[v, ctx.c_eps5], writes=[rstd])
        sch.task("act", lambda e, o=rstd: e.activation(
            o.ap, o.ap, AF.Exp, scale=-0.5), reads=[rstd], writes=[rstd])
        sch.task("dve", lambda e, a=m, b=rstd, o=nmr: e.scalar_tensor_tensor(
            o.ap, a.ap, -1.0, b.ap, ALU.mult, ALU.mult),
                 reads=[m, rstd], writes=[nmr])
        sch.task("dve", lambda e, a=xb, b=rstd, c=nmr, o=nbb: e.tensor_scalar(
            o.ap, a.ap, b.ap, c.ap, ALU.mult, ALU.add),
                 reads=[xb, rstd, nmr], writes=[nbb])
        sch.task(QP, E_dma(d["XN2"][t0:t0 + 128, :], nbb.ap),
                 reads=[nbb], writes=dbx(ctx, "XN2", t0, t0 + 128), dma=QP)



def phase_rms_attn(ctx):
    sch, d, db, ar = ctx.sch, ctx.d, ctx.db, ctx.arena
    ar.reset()
    anw = ar.alloc([128, C], F32, "anwbc")
    sch.task("pool", E_dma(anw.ap, d["anw"].partition_broadcast(128)),
             writes=[anw], dma="pool")
    xb = [ar.alloc([128, C], BF16, "p5x") for _ in range(2)]
    hb = [ar.alloc([128, C], BF16, "p5h") for _ in range(2)]
    sm = [dict(ss=ar.alloc([128, 1], F32, "ss"),
               rstd=ar.alloc([128, 1], F32, "rstd"),
               junk=ar.alloc([128, C], BF16, "junk"),
               t1=ar.alloc([128, C], BF16, "t1")) for _ in range(2)]
    for tc in range(NT):
        t0 = tc * 128
        a, o, s = xb[tc % 2], hb[tc % 2], sm[tc % 2]
        sch.task(QP, E_dma(a.ap, d["XN2"][t0:t0 + 128, :]),
                 reads=dbx(ctx, "XN2", t0, t0 + 128), writes=[a], dma=QP)
        emit_rms(ctx, s, a, a.ap, anw, o, o.ap)
        sch.task(QP, E_dma(d["HT"][t0:t0 + 128, :], o.ap),
                 reads=[o], writes=dbx(ctx, "HT", t0, t0 + 128), dma=QP)



def phase_qkv1(ctx):
    sch, d, db, ar = ctx.sch, ctx.d, ctx.db, ctx.arena
    ar.reset()
    fill_xnf(ctx, "HT")
    W = [ar.alloc([128, C], BF16, f"w1_{i}") for i in range(NC)]
    kw = [ar.alloc([128, 512], BF16, f"kw{i}") for i in range(NC)]
    vw = [ar.alloc([128, 512], BF16, f"vw{i}") for i in range(NC)]
    qh = [ar.alloc([128, C], BF16, "q1h") for _ in range(2)]
    kh = [ar.alloc([128, 512], BF16, "k1h") for _ in range(2)]
    vh = [ar.alloc([128, 512], BF16, "v1h") for _ in range(2)]
    ssl = [ar.alloc([128, 16], F32, "ssl1") for _ in range(2)]
    rstd = [ar.alloc([128, 16], F32, "rstd1") for _ in range(2)]
    junk = [ar.alloc([128, 128], BF16, "jk1") for _ in range(2)]
    qn = [ar.alloc([128, 128], F32, "qn") for _ in range(2)]
    ma = [ar.alloc([128, 64], F32, "ma") for _ in range(2)]
    mb = [ar.alloc([128, 64], F32, "mb") for _ in range(2)]
    cw1q = ar.alloc([128, 64], F32, "cw1q"); sw2q = ar.alloc([128, 64], F32, "sw2q")
    cw2q = ar.alloc([128, 64], F32, "cw2q"); sw1q = ar.alloc([128, 64], F32, "sw1q")
    cw1k = ar.alloc([128, 64], F32, "cw1k"); sw2k = ar.alloc([128, 64], F32, "sw2k")
    cw2k = ar.alloc([128, 64], F32, "cw2k"); sw1k = ar.alloc([128, 64], F32, "sw1k")

    for cc in range(NC):
        sch.task("pool", E_dma(W[cc].ap, d["qwT"][cc * 128:(cc + 1) * 128, :]),
                 writes=[W[cc]], dma="pool")
        sch.task("pool", E_dma(kw[cc].ap, d["kwT"][cc * 128:(cc + 1) * 128, :]),
                 writes=[kw[cc]], dma="pool")
        sch.task("pool", E_dma(vw[cc].ap, d["vwT"][cc * 128:(cc + 1) * 128, :]),
                 writes=[vw[cc]], dma="pool")

    def rope_tabs(tc, w_bc, c1, s2, c2, s1):
        f0 = tc * 64
        sch.task("dve", lambda e: e.tensor_tensor(
            c1.ap, ctx.rc.ap[:, f0:f0 + 64], w_bc.ap[:, 0:64], ALU.mult),
                 reads=[ctx.rc, w_bc], writes=[c1])
        sch.task("dve", lambda e: e.tensor_tensor(
            s2.ap, ctx.rs.ap[:, f0:f0 + 64], w_bc.ap[:, 64:128], ALU.mult),
                 reads=[ctx.rs, w_bc], writes=[s2])
        sch.task("dve", lambda e: e.tensor_tensor(
            c2.ap, ctx.rc.ap[:, f0:f0 + 64], w_bc.ap[:, 64:128], ALU.mult),
                 reads=[ctx.rc, w_bc], writes=[c2])
        sch.task("dve", lambda e: e.tensor_tensor(
            s1.ap, ctx.rs.ap[:, f0:f0 + 64], w_bc.ap[:, 0:64], ALU.mult),
                 reads=[ctx.rs, w_bc], writes=[s1])

    def rstd_batch(sl, rr, n, extra_scale):
        sch.task("act", lambda e: e.activation(
            rr.ap[:, 0:n], sl.ap[:, 0:n], AF.Ln, scale=1.0 / HD,
            bias=ctx.c_eps6.ap),
                 reads=[sl, ctx.c_eps6], writes=[rr])
        sch.task("act", lambda e: e.activation(
            rr.ap[:, 0:n], rr.ap[:, 0:n], AF.Exp, scale=-0.5),
                 reads=[rr], writes=[rr])
        if extra_scale != 1.0:
            sch.task("dve", lambda e: e.tensor_scalar_mul(
                rr.ap[:, 0:n], rr.ap[:, 0:n], float(extra_scale)),
                     reads=[rr], writes=[rr])

    for tc in range(NT):
        t0 = tc * 128
        par = tc % 2
        rope_tabs(tc, ctx.qw_bc, cw1q, sw2q, cw2q, sw1q)
        rope_tabs(tc, ctx.kw_bc, cw1k, sw2k, cw2k, sw1k)
        ps = [ps_take(ctx) for _ in range(4)]
        for c4 in range(4):
            def mme(e, tc=tc, c4=c4, p=ps[c4]):
                for cc in range(NC):
                    inst = e.matmul(
                        p.ap, ctx.XNF[cc].ap[:, tc * 128:(tc + 1) * 128],
                        W[cc].ap[:, c4 * 512:(c4 + 1) * 512],
                        start=(cc == 0), stop=(cc == NC - 1))
                return inst
            sch.task("pe", mme, reads=ctx.XNF + W, writes=[ps[c4]])
        sl, rr = ssl[par], rstd[par]
        jk = junk[par]
        for c4 in range(4):
            for j in range(4):
                sch.task("act", lambda e, p=ps[c4], j=j, h=c4 * 4 + j,
                         jk=jk, sl=sl: e.activation(
                    jk.ap, p.ap[:, j * 128:(j + 1) * 128], AF.Square,
                    accum_out=sl.ap[:, h:h + 1]),
                         reads=[ps[c4]], writes=[jk, sl])
        rstd_batch(sl, rr, 16, ISQ)
        q, a, b, oh = qn[par], ma[par], mb[par], qh[par]
        for c4 in range(4):
            for j in range(4):
                hidx = c4 * 4 + j
                sch.task("dve", lambda e, p=ps[c4], j=j, h=hidx, q=q, rr=rr:
                         e.tensor_scalar_mul(
                             q.ap, p.ap[:, j * 128:(j + 1) * 128],
                             rr.ap[:, h:h + 1]),
                         reads=[ps[c4], rr], writes=[q])
                off = hidx * 128
                sch.task("dve", lambda e, a=a, q=q: e.tensor_tensor(
                    a.ap, q.ap[:, 0:64], cw1q.ap, ALU.mult),
                         reads=[q, cw1q], writes=[a])
                sch.task("dve", lambda e, b=b, q=q: e.tensor_tensor(
                    b.ap, q.ap[:, 64:128], sw2q.ap, ALU.mult),
                         reads=[q, sw2q], writes=[b])
                sch.task("dve", lambda e, o=off, a=a, b=b, oh=oh: e.tensor_tensor(
                    oh.ap[:, o:o + 64], a.ap, b.ap, ALU.subtract),
                         reads=[a, b], writes=[oh])
                sch.task("dve", lambda e, a=a, q=q: e.tensor_tensor(
                    a.ap, q.ap[:, 64:128], cw2q.ap, ALU.mult),
                         reads=[q, cw2q], writes=[a])
                sch.task("dve", lambda e, b=b, q=q: e.tensor_tensor(
                    b.ap, q.ap[:, 0:64], sw1q.ap, ALU.mult),
                         reads=[q, sw1q], writes=[b])
                sch.task("dve", lambda e, o=off, a=a, b=b, oh=oh: e.tensor_tensor(
                    oh.ap[:, o + 64:o + 128], a.ap, b.ap, ALU.add),
                         reads=[a, b], writes=[oh])
        sch.task(QP, E_dma(d["Q1"][t0:t0 + 128, :], oh.ap),
                 reads=[oh], writes=db["Q1"], dma=QP)
        pk = ps_take(ctx)
        pv = ps_take(ctx)
        for p, wws in ((pk, kw), (pv, vw)):
            def mme2(e, p=p, wws=wws, tc=tc):
                for cc in range(NC):
                    inst = e.matmul(
                        p.ap, ctx.XNF[cc].ap[:, tc * 128:(tc + 1) * 128],
                        wws[cc].ap, start=(cc == 0), stop=(cc == NC - 1))
                return inst
            sch.task("pe", mme2, reads=ctx.XNF + wws, writes=[p])
        for j in range(4):
            sch.task("act", lambda e, p=pk, j=j, jk=jk, sl=sl: e.activation(
                jk.ap, p.ap[:, j * 128:(j + 1) * 128], AF.Square,
                accum_out=sl.ap[:, j:j + 1]),
                     reads=[pk], writes=[jk, sl])
        rstd_batch(sl, rr, 4, 1.0)
        ok = kh[par]
        for j in range(4):
            sch.task("dve", lambda e, p=pk, j=j, q=q, rr=rr: e.tensor_scalar_mul(
                q.ap, p.ap[:, j * 128:(j + 1) * 128], rr.ap[:, j:j + 1]),
                     reads=[pk, rr], writes=[q])
            off = j * 128
            sch.task("dve", lambda e, a=a, q=q: e.tensor_tensor(
                a.ap, q.ap[:, 0:64], cw1k.ap, ALU.mult),
                     reads=[q, cw1k], writes=[a])
            sch.task("dve", lambda e, b=b, q=q: e.tensor_tensor(
                b.ap, q.ap[:, 64:128], sw2k.ap, ALU.mult),
                     reads=[q, sw2k], writes=[b])
            sch.task("dve", lambda e, o=off, a=a, b=b, ok=ok: e.tensor_tensor(
                ok.ap[:, o:o + 64], a.ap, b.ap, ALU.subtract),
                     reads=[a, b], writes=[ok])
            sch.task("dve", lambda e, a=a, q=q: e.tensor_tensor(
                a.ap, q.ap[:, 64:128], cw2k.ap, ALU.mult),
                     reads=[q, cw2k], writes=[a])
            sch.task("dve", lambda e, b=b, q=q: e.tensor_tensor(
                b.ap, q.ap[:, 0:64], sw1k.ap, ALU.mult),
                     reads=[q, sw1k], writes=[b])
            sch.task("dve", lambda e, o=off, a=a, b=b, ok=ok: e.tensor_tensor(
                ok.ap[:, o + 64:o + 128], a.ap, b.ap, ALU.add),
                     reads=[a, b], writes=[ok])
        sch.task(QP, E_dma(d["K1"][t0:t0 + 128, :], ok.ap),
                 reads=[ok], writes=db["K1"], dma=QP)
        sch.task("act", lambda e, p=pv, o=vh[par]: e.activation(
            o.ap, p.ap, AF.Copy), reads=[pv], writes=[vh[par]])
        sch.task(QP, E_dma(d["V1"][t0:t0 + 128, :], vh[par].ap),
                 reads=[vh[par]], writes=db["V1"], dma=QP)



def phase_oproj(ctx):
    sch, d, db, ar = ctx.sch, ctx.d, ctx.db, ctx.arena
    ar.reset()
    for hc in range(NC):
        sch.task(QP, E_dma(ctx.XNF[hc].ap, d["AFM"][hc * 128:(hc + 1) * 128, :]),
                 reads=dbx(ctx, "AFM", hc * 128, (hc + 1) * 128),
                 writes=[ctx.XNF[hc]], dma=QP)
    W = [ar.alloc([128, C], BF16, f"ow{i}") for i in range(NC)]
    for cc in range(NC):
        sch.task("pool", E_dma(W[cc].ap, d["owT"][cc * 128:(cc + 1) * 128, :]),
                 writes=[W[cc]], dma="pool")
    fnw = ar.alloc([128, C], F32, "fnwbc")
    sch.task("pool", E_dma(fnw.ap, d["fnw"].partition_broadcast(128)),
             writes=[fnw], dma="pool")
    xt = [ar.alloc([128, C], BF16, "p8x") for _ in range(2)]
    hid = [ar.alloc([128, C], BF16, "p8h") for _ in range(2)]
    h2 = [ar.alloc([128, C], BF16, "p8h2") for _ in range(2)]
    sm = [dict(ss=ar.alloc([128, 1], F32, "ss"),
               rstd=ar.alloc([128, 1], F32, "rstd"),
               junk=ar.alloc([128, C], BF16, "junk"),
               t1=ar.alloc([128, C], BF16, "t1")) for _ in range(2)]
    for tc in range(NT):
        t0 = tc * 128
        par = tc % 2
        xb, hh, h2b, s = xt[par], hid[par], h2[par], sm[par]
        sch.task(QP, E_dma(xb.ap, d["XN2"][t0:t0 + 128, :]),
                 reads=dbx(ctx, "XN2", t0, t0 + 128), writes=[xb], dma=QP)
        for c4 in range(4):
            p = ps_take(ctx)

            def mme(e, tc=tc, c4=c4, p=p):
                for hc in range(NC):
                    inst = e.matmul(
                        p.ap, ctx.XNF[hc].ap[:, tc * 128:(tc + 1) * 128],
                        W[hc].ap[:, c4 * 512:(c4 + 1) * 512],
                        start=(hc == 0), stop=(hc == NC - 1))
                return inst
            sch.task("pe", mme, reads=ctx.XNF + W, writes=[p])
            sch.task("dve", lambda e, p=p, c4=c4, a=xb, o=hh: e.tensor_tensor(
                o.ap[:, c4 * 512:(c4 + 1) * 512],
                a.ap[:, c4 * 512:(c4 + 1) * 512], p.ap, ALU.add),
                     reads=[p, xb], writes=[hh])
        sch.task(QP, E_dma(d["HID"][t0:t0 + 128, :], hh.ap),
                 reads=[hh], writes=db["HID"], dma=QP)
        emit_rms(ctx, s, hh, hh.ap, fnw, h2b, h2b.ap)
        sch.task(QP, E_dma(d["H2"][t0:t0 + 128, :], h2b.ap),
                 reads=[h2b], writes=dbx(ctx, "H2", t0, t0 + 128), dma=QP)



def scan_setup(ctx):
    sch, d, db, ar = ctx.sch, ctx.d, ctx.db, ctx.arena
    st = Ctx()
    xp = ar.alloc([128, NC * 160], BF16, "xpT")
    dt = ar.alloc([128, C], BF16, "dtT")
    sch.task("pool", E_dma_gp(xp.ap, d["xpT"], 160), writes=[xp], dma="pool")
    sch.task("pool", E_dma(dt.ap, d["dtT"]), writes=[dt], dma="pool")
    sch.task("dve", lambda e: e.memset(ctx.hst.ap, 0.0), writes=[ctx.hst])

    st.xp, st.dt = xp, dt
    st.u = [ar.alloc([128, 512], BF16, f"u{i}") for i in range(2)]
    st.u2 = [ar.alloc([128, 512], BF16, f"u2_{i}") for i in range(2)]
    st.dl = [ar.alloc([128, 512], F32, f"dl{i}") for i in range(2)]
    st.yac = [ar.alloc([128, 512], F32, f"y{i}") for i in range(2)]
    st.Bbc = [ar.alloc([128, 512], BF16, f"Bb{i}") for i in range(DST)]
    st.Cbc = [ar.alloc([128, 512], BF16, f"Cb{i}") for i in range(DST)]
    st.dr = [ar.alloc([128, 512], BF16, "dr")]
    st.dA = [ar.alloc([128, 512], F32, "dA") for _ in range(3)]
    st.dB = [ar.alloc([128, 512], BF16, "dB") for _ in range(3)]
    st.hsc = [ar.alloc([128, 512], BF16, "hsc") for _ in range(3)]
    st.yt = [ar.alloc([128, 512], BF16, "yt") for _ in range(3)]
    st.du = [ar.alloc([128, 512], BF16, "du")]
    st.t0b = [ar.alloc([128, 512], F32, "t0b") for _ in range(2)]
    st.nab = [ar.alloc([128, 512], F32, "nab") for _ in range(2)]
    st.bcv = [ar.alloc([32, 512], F32, "bcv")]
    st.ybf = [ar.alloc([128, 512], BF16, "ybf")]
    return st


def scan_chunk(ctx, st, t4):
    sch, d, db = ctx.sch, ctx.d, ctx.db
    xp, dt = st.xp, st.dt
    u, dl, yac, Bbc, Cbc = st.u, st.dl, st.yac, st.Bbc, st.Cbc
    dr, dA, dB, hsc, yt, du = st.dr, st.dA, st.dB, st.hsc, st.yt, st.du
    t0b, nab, bcv, ybf = st.t0b, st.nab, st.bcv, st.ybf
    u2 = st.u2
    if True:
        t0 = t4 * 512
        pe1 = ps_take(ctx)
        pe2 = ps_take(ctx)
        for cc in range(NC):
            ub = u[cc % 2]
            tload(ctx, ub, ub.ap,
                  d["XN2"][t0:t0 + 512, cc * 128:(cc + 1) * 128],
                  dbx(ctx, "XN2", t0, t0 + 512))
            sch.task("pe", lambda e, p=pe1, ub=ub, cc=cc: e.matmul(
                p.ap, xp.ap[:, cc * 160:cc * 160 + 128], ub.ap,
                start=(cc == 0), stop=(cc == NC - 1)),
                     reads=[ub, xp], writes=[pe1])
            sch.task("pe", lambda e, p=pe2, ub=ub, cc=cc: e.matmul(
                p.ap[0:32, :], xp.ap[:, cc * 160 + 128:cc * 160 + 160], ub.ap,
                start=(cc == 0), stop=(cc == NC - 1)),
                     reads=[ub, xp], writes=[pe2])
        drb = dr[0]
        sch.task("act", lambda e, p=pe1, o=drb: e.activation(o.ap, p.ap, AF.Copy),
                 reads=[pe1], writes=[drb])
        bc = bcv[0]
        sch.task("dve", lambda e, p=pe2, o=bc: e.tensor_copy(
            o.ap, p.ap[0:32, :]), reads=[pe2], writes=[bc])
        sch.task(QP, E_dma(d["BCR"][:, t0:t0 + 512], bc.ap),
                 reads=[bc], writes=db["BCR"], dma=QP)
        for n in range(DST):
            sch.task("pool", E_dma(
                Bbc[n].ap,
                d["BCR"][n:n + 1, t0:t0 + 512].partition_broadcast(128)),
                     reads=db["BCR"], writes=[Bbc[n]], dma="pool")
            sch.task("pool", E_dma(
                Cbc[n].ap,
                d["BCR"][DST + n:DST + n + 1, t0:t0 + 512].partition_broadcast(128)),
                     reads=db["BCR"], writes=[Cbc[n]], dma="pool")
        for dc in range(NC):
            pd = ps_take(ctx)
            sch.task("pe", lambda e, p=pd, dc=dc, b=drb: e.matmul(
                p.ap, dt.ap[:, dc * 128:(dc + 1) * 128], b.ap,
                start=True, stop=True),
                     reads=[dt, drb], writes=[pd])
            tt, nn = t0b[dc % 2], nab[dc % 2]
            sch.task("dve", lambda e, p=pd, dc=dc, o=tt: e.tensor_scalar(
                o.ap, p.ap, ctx.dtb.ap[:, dc:dc + 1], None, ALU.add),
                     reads=[pd, ctx.dtb], writes=[tt])
            sch.task("dve", lambda e, a=tt, o=nn: e.scalar_tensor_tensor(
                o.ap, a.ap, -1.0, a.ap, ALU.mult, ALU.min),
                     reads=[tt], writes=[nn])
            sch.task("act", lambda e, o=nn: e.activation(o.ap, o.ap, AF.Exp),
                     reads=[nn], writes=[nn])
            sch.task("act", lambda e, o=nn: e.activation(o.ap, o.ap, AF.Ln,
                                                         bias=1.0),
                     reads=[nn], writes=[nn])
            sch.task("dve", lambda e, a=tt, o=tt: e.tensor_single_scalar(
                o.ap, a.ap, 0.0, ALU.max),
                     reads=[tt], writes=[tt])
            sch.task("dve", lambda e, a=tt, b=nn, o=dl[dc % 2]: e.tensor_tensor(
                o.ap, a.ap, b.ap, ALU.add),
                     reads=[tt, nn], writes=[dl[dc % 2]])
            ub2 = u2[dc % 2]
            tload(ctx, ub2, ub2.ap,
                  d["XN2"][t0:t0 + 512, dc * 128:(dc + 1) * 128],
                  dbx(ctx, "XN2", t0, t0 + 512))
            sch.task("dve", lambda e, a=dl[dc % 2], b=ub2, o=du[0]:
                     e.tensor_tensor(o.ap, a.ap, b.ap, ALU.mult),
                     reads=[dl[dc % 2], ub2], writes=[du[0]])
            for n in range(DST):
                col = dc * DST + n
                da, db_, hs, yy = dA[n % 3], dB[n % 3], hsc[n % 3], yt[n % 3]
                sch.task("act", lambda e, a=dl[dc % 2], o=da, col=col: e.activation(
                    o.ap, a.ap, AF.Exp, scale=ctx.A_sb.ap[:, col:col + 1]),
                         reads=[dl[dc % 2], ctx.A_sb], writes=[da])
                sch.task("dve", lambda e, a=du[0], b=Bbc[n], o=db_:
                         e.tensor_tensor(o.ap, a.ap, b.ap, ALU.mult),
                         reads=[du[0], Bbc[n]], writes=[db_])
                sch.task("dve", lambda e, a=da, b=db_, o=hs, col=col:
                         e.tensor_tensor_scan(o.ap, a.ap, b.ap,
                                              ctx.hst.ap[:, col:col + 1],
                                              ALU.mult, ALU.add),
                         reads=[da, db_, ctx.hst], writes=[hs])
                sch.task("dve", lambda e, a=hs, col=col: e.tensor_copy(
                    ctx.hst.ap[:, col:col + 1], a.ap[:, 511:512]),
                         reads=[hs], writes=[ctx.hst])
                sch.task("dve", lambda e, a=hs, b=Cbc[n], o=yy:
                         e.tensor_tensor(o.ap, a.ap, b.ap, ALU.mult),
                         reads=[hs, Cbc[n]], writes=[yy])
                if n == 0:
                    sch.task("pool", lambda e, a=yy, o=yac[dc % 2]: e.tensor_copy(
                        o.ap, a.ap), reads=[yy], writes=[yac[dc % 2]])
                else:
                    sch.task("pool", lambda e, a=yy, o=yac[dc % 2]: e.tensor_tensor(
                        o.ap, o.ap, a.ap, ALU.add),
                             reads=[yy, yac[dc % 2]], writes=[yac[dc % 2]])
            sch.task("dve", lambda e, a=ub2, o=yac[dc % 2], dc=dc:
                     e.scalar_tensor_tensor(o.ap, a.ap,
                                            ctx.D_sb.ap[:, dc:dc + 1],
                                            o.ap, ALU.mult, ALU.add),
                     reads=[ub2, ctx.D_sb, yac[dc % 2]], writes=[yac[dc % 2]])
            yb = ybf[0]
            sch.task("dve", lambda e, a=yac[dc % 2], o=yb: e.tensor_copy(o.ap, a.ap),
                     reads=[yac[dc % 2]], writes=[yb])
            sch.task(QP, E_dma(d["Y"][dc * 128:(dc + 1) * 128, t0:t0 + 512],
                                 yb.ap),
                     reads=[yb], writes=db["Y"], dma=QP)



def phase_moe(ctx):
    sch, d, db, ar = ctx.sch, ctx.d, ctx.db, ctx.arena
    ar.reset()
    fill_xnf(ctx, "H2")
    moe = [ar.alloc([128, C], BF16, f"moe{i}") for i in range(8)]
    gsl = [ar.alloc([128, 16 * 256], BF16, "mgsl") for _ in range(2)]
    usl = [ar.alloc([128, 16 * 256], BF16, "musl") for _ in range(2)]
    hb = [ar.alloc([128, 1024], BF16, f"mhb{i}") for i in range(NI1)]
    dslb = [ar.alloc([128, 6 * 512], BF16, "mdsl") for _ in range(2)]
    sgt = [ar.alloc([128, 512], BF16, "msg") for _ in range(2)]
    sut = [ar.alloc([128, 512], BF16, "msu") for _ in range(2)]
    r_m1 = ar.alloc([128, 1], F32, "rm1"); r_m2 = ar.alloc([128, 1], F32, "rm2")
    r_tr = ar.alloc([128, 8], F32, "rtr"); r_g2 = ar.alloc([128, 8], F32, "rg2")
    r_p = ar.alloc([128, 8], F32, "rp"); r_sg = ar.alloc([128, 8], F32, "rsg")
    r_ms = ar.alloc([128, 8], F32, "rms"); r_s = ar.alloc([128, 1], F32, "rS")
    for tc in range(NT):
        pl = ps_take(ctx)

        def mmr(e, tc=tc, p=pl):
            for cc in range(NC):
                inst = e.matmul(p.ap[:, 0:8],
                                ctx.XNF[cc].ap[:, tc * 128:(tc + 1) * 128],
                                ctx.rw_sb.ap[:, cc * 8:(cc + 1) * 8],
                                start=(cc == 0), stop=(cc == NC - 1))
            return inst
        sch.task("pe", mmr, reads=ctx.XNF + [ctx.rw_sb], writes=[pl])
        sch.task("dve", lambda e, p=pl: e.reduce_max(
            r_m1.ap, p.ap[:, 0:8], axis=AX.X, negate=True),
                 reads=[pl], writes=[r_m1])
        sch.task("act", lambda e, p=pl: e.activation(
            r_p.ap, p.ap[:, 0:8], AF.Exp, bias=r_m1.ap, accum_out=r_s.ap),
                 reads=[pl, r_m1], writes=[r_p, r_s])
        sch.task("dve", lambda e: e.reciprocal(r_s.ap, r_s.ap),
                 reads=[r_s], writes=[r_s])
        sch.task("dve", lambda e: e.tensor_scalar_mul(r_p.ap, r_p.ap, r_s.ap),
                 reads=[r_p, r_s], writes=[r_p])
        sch.task("dve", lambda e: e.reduce_max(r_m1.ap, r_p.ap, axis=AX.X),
                 reads=[r_p], writes=[r_m1])
        sch.task("dve", lambda e: e.memset(r_tr.ap, -1e38), writes=[r_tr])
        sch.task("dve", lambda e: e.tensor_copy(r_tr.ap[:, 0:1], r_m1.ap),
                 reads=[r_m1], writes=[r_tr])
        sch.task("dve", lambda e: e.match_replace(r_g2.ap, r_tr.ap, r_p.ap, -1e30),
                 reads=[r_tr, r_p], writes=[r_g2])
        sch.task("dve", lambda e: e.reduce_max(r_m2.ap, r_g2.ap, axis=AX.X),
                 reads=[r_g2], writes=[r_m2])
        sch.task("act", lambda e: e.activation(r_sg.ap, r_p.ap, AF.Sign,
                                               scale=-1.0, bias=r_m2.ap),
                 reads=[r_p, r_m2], writes=[r_sg])
        sch.task("dve", lambda e: e.tensor_single_scalar(
            r_ms.ap, r_sg.ap, 0.0, ALU.max), reads=[r_sg], writes=[r_ms])
        sch.task("dve", lambda e: e.tensor_scalar(
            r_ms.ap, r_ms.ap, -1.0, 1.0, ALU.mult, ALU.add),
                 reads=[r_ms], writes=[r_ms])
        sch.task("dve", lambda e: e.tensor_tensor(
            r_m1.ap, r_m1.ap, r_m2.ap, ALU.add), reads=[r_m1, r_m2], writes=[r_m1])
        sch.task("dve", lambda e: e.tensor_scalar_add(r_m1.ap, r_m1.ap, 1e-9),
                 reads=[r_m1], writes=[r_m1])
        sch.task("dve", lambda e: e.reciprocal(r_m1.ap, r_m1.ap),
                 reads=[r_m1], writes=[r_m1])
        sch.task("dve", lambda e, tc=tc: e.scalar_tensor_tensor(
            ctx.comb[tc].ap, r_p.ap, r_m1.ap, r_ms.ap, ALU.mult, ALU.mult),
                 reads=[r_p, r_m1, r_ms], writes=[ctx.comb[tc]])

    gu_loaded = set()

    def gu_load(th, e_i, icg):
        key = (th, e_i, icg)
        if key in gu_loaded:
            return
        gu_loaded.add(key)
        i0 = icg * 256
        w = min(256, I1 - i0)
        gb, ub = gsl[icg % 2], usl[icg % 2]
        sch.task("pool", E_dma_gp(
            gb.ap[:, 0:NC * w], d["wgT"][e_i, :, i0:i0 + w], w),
                 writes=[gb], dma="pool")
        sch.task("pool", E_dma_gp(
            ub.ap[:, 0:NC * w], d["wuT"][e_i, :, i0:i0 + w], w),
                 writes=[ub], dma="pool")

    for th in range(2):
        th0 = th * 1024
        for e_i in range(NE):
            for icg in range(6):
                i0 = icg * 256
                w = min(256, I1 - i0)
                gb, ub = gsl[icg % 2], usl[icg % 2]
                gu_load(th, e_i, icg)
                for j in range(w // 128):
                    ic = icg * 2 + j
                    for t5 in range(2):
                        t0 = th0 + t5 * 512
                        pg = ps_take(ctx)
                        pu = ps_take(ctx)
                        for p, wb in ((pg, gb), (pu, ub)):
                            def mme(e, p=p, wb=wb, j=j, w=w, t0=t0):
                                for cc in range(NC):
                                    inst = e.matmul(
                                        p.ap,
                                        wb.ap[:, cc * w + j * 128:
                                              cc * w + (j + 1) * 128],
                                        ctx.XNF[cc].ap[:, t0:t0 + 512],
                                        start=(cc == 0), stop=(cc == NC - 1))
                                return inst
                            sch.task("pe", mme, reads=ctx.XNF + [wb], writes=[p])
                        sg, su = sgt[t5], sut[t5]
                        sch.task("act", lambda e, p=pg, o=sg: e.activation(
                            o.ap, p.ap, AF.Silu), reads=[pg], writes=[sg])
                        sch.task("act", lambda e, p=pu, o=su: e.activation(
                            o.ap, p.ap, AF.Copy), reads=[pu], writes=[su])
                        sch.task("dve", lambda e, a=sg, b=su, o=hb[ic], t5=t5:
                                 e.tensor_tensor(
                                     o.ap[:, t5 * 512:(t5 + 1) * 512],
                                     a.ap, b.ap, ALU.mult),
                                 reads=[sg, su], writes=[hb[ic]])
            if e_i + 1 < NE:
                gu_load(th, e_i + 1, 0)
                gu_load(th, e_i + 1, 1)
            elif th == 0:
                gu_load(1, 0, 0)
                gu_load(1, 0, 1)
            for co4 in range(4):
                co0 = co4 * 512
                for half in range(2):
                    i0 = half * 6 * 128
                    ni = 6 if half == 0 else 5
                    sch.task("pool", E_dma_gp(
                        dslb[half].ap[:, 0:ni * 512],
                        d["wdT"][e_i, i0:i0 + ni * 128, co0:co0 + 512], 512),
                             writes=[dslb[half]], dma="pool")
                for tci in range(8):
                    tc = th * 8 + tci
                    psd = ps_take(ctx)

                    def mmd(e, tci=tci, p=psd, b0=dslb[0], b1=dslb[1]):
                        for ic in range(NI1):
                            half = ic // 6
                            k = ic % 6
                            wb = b0 if half == 0 else b1
                            inst = e.matmul(
                                p.ap, hb[ic].ap[:, tci * 128:(tci + 1) * 128],
                                wb.ap[:, k * 512:(k + 1) * 512],
                                start=(ic == 0), stop=(ic == NI1 - 1))
                        return inst
                    sch.task("pe", mmd, reads=hb + [dslb[0], dslb[1]],
                             writes=[psd])
                    mtile = moe[tci]
                    if e_i == 0:
                        sch.task("dve", lambda e, p=psd, tc=tc, co0=co0, m=mtile:
                                 e.tensor_scalar_mul(
                                     m.ap[:, co0:co0 + 512], p.ap,
                                     ctx.comb[tc].ap[:, 0:1]),
                                 reads=[psd, ctx.comb[tc]], writes=[mtile])
                    else:
                        sch.task("dve", lambda e, p=psd, tc=tc, co0=co0, ei=e_i,
                                 m=mtile:
                                 e.scalar_tensor_tensor(
                                     m.ap[:, co0:co0 + 512], p.ap,
                                     ctx.comb[tc].ap[:, ei:ei + 1],
                                     m.ap[:, co0:co0 + 512],
                                     ALU.mult, ALU.add),
                                 reads=[psd, ctx.comb[tc], mtile],
                                 writes=[mtile])
        for tci in range(8):
            tc = th * 8 + tci
            sch.task(QP, E_dma(d["MOE"][tc * 128:(tc + 1) * 128, :],
                                 moe[tci].ap),
                     reads=[moe[tci]], writes=db["MOE"], dma=QP)



def phase_final(ctx):
    sch, d, db, ar = ctx.sch, ctx.d, ctx.db, ctx.arena
    ar.reset()
    x1 = [ar.alloc([128, C], F32, "fx1") for _ in range(2)]
    hid = [ar.alloc([128, C], BF16, "fhid") for _ in range(2)]
    mo = [ar.alloc([128, C], BF16, "fmo") for _ in range(2)]
    s2 = [ar.alloc([128, C], BF16, "fs2") for _ in range(2)]
    g2 = [ar.alloc([128, C], BF16, "fg2") for _ in range(2)]
    for tc in range(NT):
        t0 = tc * 128
        par = tc % 2
        xb, hh, mm, ss, gg = x1[par], hid[par], mo[par], s2[par], g2[par]
        sch.task(QP, E_dma(xb.ap, d["X1"][t0:t0 + 128, :]),
                 reads=db["X1"], writes=[xb], dma=QP)
        sch.task(QP, E_dma(hh.ap, d["HID"][t0:t0 + 128, :]),
                 reads=db["HID"], writes=[hh], dma=QP)
        sch.task(QP, E_dma(mm.ap, d["MOE"][t0:t0 + 128, :]),
                 reads=db["MOE"], writes=[mm], dma=QP)
        for c4 in range(4):
            tload(ctx, ss, ss.ap[:, c4 * 512:(c4 + 1) * 512],
                  d["Y"][c4 * 512:(c4 + 1) * 512, t0:t0 + 128], db["Y"])
        sch.task("dve", lambda e, a=hh, b=mm, o=gg: e.tensor_tensor(
            o.ap, a.ap, b.ap, ALU.add), reads=[hh, mm], writes=[gg])
        sch.task("act", lambda e, o=gg: e.activation(o.ap, o.ap, AF.Silu),
                 reads=[gg], writes=[gg])
        sch.task("dve", lambda e, a=ss, b=gg, o=ss: e.tensor_tensor(
            o.ap, a.ap, b.ap, ALU.mult), reads=[ss, gg], writes=[ss])
        sch.task("dve", lambda e, a=xb, b=ss: e.tensor_tensor(
            a.ap, a.ap, b.ap, ALU.add), reads=[xb, ss], writes=[xb])
        dep = sch.task(QP, E_dma(d["out"][t0:t0 + 128, :], xb.ap),
                       reads=[xb], dma=QP)
        ctx.stores.append(dep)



PHASES = ("ln0", "qkv0", "attn0", "ffn0", "combine0", "rms_attn", "qkv1",
          "attn1", "oproj", "scan", "moe", "final")


def build(dev_taps=(), upto=None):
    ctx = Ctx()
    ctx.nc = bass.Bass("TRN2", target_bir_lowering=False)
    ctx.sch = Sched(ctx.nc)
    _mk_dram(ctx, dev_taps)
    _mk_static(ctx)
    ctx.stores = []
    load_consts(ctx)
    def attn1_with_scan(c):
        phase_attn(c, 1)

    def scan_phase(c):
        c.arena.reset()
        stt_ = scan_setup(c)
        for t4 in range(4):
            scan_chunk(c, stt_, t4)

    emit = {
        "ln0": phase_ln0,
        "qkv0": phase_qkv0,
        "attn0": lambda c: phase_attn(c, 0),
        "ffn0": phase_ffn0,
        "combine0": phase_combine0,
        "rms_attn": phase_rms_attn,
        "qkv1": phase_qkv1,
        "attn1": attn1_with_scan,
        "oproj": phase_oproj,
        "scan": scan_phase,
        "moe": phase_moe,
        "final": phase_final,
    }
    for ph in PHASES:
        emit[ph](ctx)
        if upto is not None and ph == upto:
            break
    deps = list(ctx.stores)
    for name, bl in ctx.db.items():
        for b in bl:
            if b.writer is not None:
                deps.append(b.writer)
    ctx.sch.barrier_wait(QP, deps)
    ctx.sch.build()
    return ctx


def host_prep(inputs):
    f = lambda a: np.ascontiguousarray(a, dtype=np.float32)
    p = {}
    p["wqT"] = f(inputs["L0_attn_wq"].T)
    p["wkT"] = f(inputs["L0_attn_wk"].T)
    p["wvT"] = f(inputs["L0_attn_wv"].T)
    p["tau"] = f(np.reshape(inputs["L0_attn_tau"], (1, 1)))
    p["nu"] = f(np.reshape(inputs["L0_attn_nu"], (1, 1)))
    p["gwT"] = f(inputs["L0_ffn_gate_w"].T)
    p["uwT"] = f(inputs["L0_ffn_up_w"].T)
    p["dwT"] = f(inputs["L0_ffn_down_w"].T)
    p["gs"] = f(np.reshape(inputs["L0_ffn_gate_s"], (NI0, 128)).T)
    p["us"] = f(np.reshape(inputs["L0_ffn_up_s"], (NI0, 128)).T)
    p["ds_row"] = f(np.reshape(inputs["L0_ffn_down_s"], (1, C)))
    p["AlogR"] = f(np.reshape(inputs["L1_attn_A_log"], (NC, 128, DST))
                   .transpose(1, 0, 2).reshape(128, NC * DST))
    p["Dr"] = f(np.reshape(inputs["L1_attn_D"], (NC, 128)).T)
    p["xpT"] = f(inputs["L1_attn_x_proj_w"].T)
    p["dtT"] = f(inputs["L1_attn_dt_proj_w"].T)
    p["dtb"] = f(np.reshape(inputs["L1_attn_dt_proj_b"], (NC, 128)).T)
    p["anw"] = f(np.reshape(inputs["L1_ffn_attn_norm_w"], (1, C)))
    p["fnw"] = f(np.reshape(inputs["L1_ffn_ffn_norm_w"], (1, C)))
    p["qwT"] = f(inputs["L1_ffn_q_w"].T)
    p["kwT"] = f(inputs["L1_ffn_k_w"].T)
    p["vwT"] = f(inputs["L1_ffn_v_w"].T)
    p["owT"] = f(inputs["L1_ffn_o_w"].T)
    p["qnw"] = f(np.reshape(inputs["L1_ffn_q_norm_w"], (1, HD)))
    p["knw"] = f(np.reshape(inputs["L1_ffn_k_norm_w"], (1, HD)))
    p["rwT"] = f(inputs["L1_ffn_gate_w"].T)
    p["wgT"] = f(inputs["L1_ffn_wg"].transpose(0, 2, 1))
    p["wuT"] = f(inputs["L1_ffn_wu"].transpose(0, 2, 1))
    p["wdT"] = f(inputs["L1_ffn_wd"].transpose(0, 2, 1))
    freqs = 1.0 / (1000000.0 ** (np.arange(0, HD, 2, dtype=np.float32) / HD))
    tt = np.arange(T, dtype=np.float32)
    fr = np.outer(tt, freqs)
    p["rcos"] = np.cos(fr).astype(np.float32)
    p["rsin"] = np.sin(fr).astype(np.float32)
    cm = np.zeros((4, 128, 512), np.float32)
    for v in range(4):
        for pp in range(128):
            cm[v, pp, :v * 128 + pp + 1] = 1.0
    p["cmask"] = cm
    return p


def kernel(**inputs):
    prep = host_prep(inputs)
    ctx = build(DEV_TAPS)
    x = np.asarray(inputs["L0_attn_x"], dtype=np.float32)
    in_maps = []
    for c in range(8):
        m = dict(prep)
        m["x"] = np.ascontiguousarray(x[c])
        in_maps.append(m)
    res = run_bass_kernel_spmd(ctx.nc, in_maps, core_ids=list(range(8)))
    return np.stack([np.asarray(res.results[c]["out"]) for c in range(8)],
                    axis=0)


# version 12
# speedup vs baseline: 1.2663x; 1.0019x over previous
import numpy as np
import ml_dtypes
import concourse.bass as bass
import concourse.mybir as mybir
from concourse.bass_utils import run_bass_kernel_spmd

F32 = mybir.dt.float32
BF16 = mybir.dt.bfloat16
AF = mybir.ActivationFunctionType
ALU = mybir.AluOpType
AX = mybir.AxisListType

ENGINES = ("pe", "act", "dve", "pool", "sp")
_BLOCK_ATTR = {"pe": "tensor", "act": "scalar", "dve": "vector",
               "pool": "gpsimd", "sp": "sync"}

QP = "act"
QT = "sp"
T = 2048
C = 2048
NT = T // 128
NC = C // 128
NH = 16
HD = 128
I0 = 5632
NI0 = I0 // 128
NE = 8
I1 = 1408
NI1 = I1 // 128
DST = 16
ISQ = 1.0 / np.sqrt(128.0)

DEV_TAPS = ()



class Buf:
    __slots__ = ("ap", "name", "writer", "readers", "pending", "_rng")

    def __init__(self, ap, name=""):
        self.ap = ap
        self.name = name
        self.writer = None
        self.readers = []
        self.pending = []


class _Task:
    __slots__ = ("waits", "emit", "sem", "amt")


class Sched:
    DMA_RING = 8

    def __init__(self, nc):
        self.nc = nc
        self.tasks = {e: [] for e in ENGINES}
        self.cnt = {}
        self.sem_names = []
        for e in ENGINES:
            self._new_sem("p_" + e)
        self.dma_ring = {}
        self.dma_prev = {}
        for q in ("sp", "act", "pool"):
            names = []
            for i in range(self.DMA_RING):
                n = f"d_{q}{i}"
                self._new_sem(n)
                names.append(n)
            self.dma_ring[q] = names
            for n in names:
                self.dma_prev[n] = None
        self.dma_rr = {q: 0 for q in self.dma_ring}
        self.clock = {e: {} for e in ENGINES}
        self.n_waits = 0
        self.n_tasks = 0

    def _new_sem(self, name):
        self.sem_names.append(name)
        self.cnt[name] = 0

    def buf(self, ap, name=""):
        return Buf(ap, name)

    def task(self, engine, emit, reads=(), writes=(), dma=None, extra_waits=()):
        eng_clock = self.clock[engine]
        waits = {}

        def need(dep):
            if dep is None:
                return
            sem, val, _clk = dep
            if eng_clock.get(sem, 0) < val and waits.get(sem, 0) < val:
                waits[sem] = val

        for b in reads:
            need(b.writer)
        for b in writes:
            need(b.writer)
            for r in b.readers:
                need(r)
            for dp in b.pending:
                need(dp)
        for wdep in extra_waits:
            need(wdep)

        if dma is not None:
            ring = self.dma_ring[dma]
            sem = ring[self.dma_rr[dma] % len(ring)]
            self.dma_rr[dma] += 1
            amt = 16
            need(self.dma_prev[sem])
        else:
            sem, amt = "p_" + engine, 1
        self.cnt[sem] += amt
        val = self.cnt[sem]

        t = _Task()
        t.waits = sorted(waits.items())
        t.emit = emit
        t.sem, t.amt = sem, amt
        self.tasks[engine].append(t)
        self.n_waits += len(t.waits)
        self.n_tasks += 1

        for s, v in waits.items():
            if eng_clock.get(s, 0) < v:
                eng_clock[s] = v
        for b in reads:
            if b.writer is not None:
                self._merge(eng_clock, b.writer[2])
        for b in writes:
            if b.writer is not None:
                self._merge(eng_clock, b.writer[2])
            for r in b.readers:
                self._merge(eng_clock, r[2])
            b.pending = []

        clk = dict(eng_clock)
        clk[sem] = max(clk.get(sem, 0), val)
        dep = (sem, val, clk)
        if dma is not None:
            self.dma_prev[sem] = dep
        for b in writes:
            b.writer = dep
            b.readers = []
        for b in reads:
            b.readers.append(dep)
        return dep

    @staticmethod
    def _merge(dst, src):
        for s, v in src.items():
            if dst.get(s, 0) < v:
                dst[s] = v

    def barrier_wait(self, engine, deps):
        t = _Task()
        eng_clock = self.clock[engine]
        waits = {}
        for dep in deps:
            if dep is None:
                continue
            sem, val, _ = dep
            if eng_clock.get(sem, 0) < val and waits.get(sem, 0) < val:
                waits[sem] = val
        t.waits = sorted(waits.items())
        t.emit = None
        t.sem = None
        self.tasks[engine].append(t)
        for s, v in waits.items():
            if eng_clock.get(s, 0) < v:
                eng_clock[s] = v

    def build(self):
        from contextlib import ExitStack
        nc = self.nc
        with ExitStack() as ctx:
            sems = {}
            for name in self.sem_names:
                sems[name] = ctx.enter_context(nc.semaphore(name))
            block = ctx.enter_context(nc.Block())

            def mk(tasks):
                def emit_all(eng):
                    for t in tasks:
                        for s, v in t.waits:
                            eng.wait_ge(sems[s], v)
                        if t.emit is not None:
                            inst = t.emit(eng)
                            inst.then_inc(sems[t.sem], t.amt)
                return emit_all

            for e in ENGINES:
                getattr(block, _BLOCK_ATTR[e])(mk(self.tasks[e]))


class Arena:
    def __init__(self, nc, sch, name, nbytes):
        h = nc.alloc_sbuf_tensor(name, [128, nbytes], mybir.dt.uint8)
        self.base = nc.lookup_mloc(h).addr
        self.nbytes = nbytes
        self.nc = nc
        self.sch = sch
        self.off = 0
        self.live = []
        self.retired = []
        self.n = 0

    def alloc(self, shape, dtype, name=""):
        esz = 4 if dtype == F32 else 2
        fbytes = int(np.prod(shape[1:])) * esz
        off = (self.off + 31) & ~31
        assert off + fbytes <= self.nbytes, \
            f"arena overflow: {name} needs {fbytes}B at {off}/{self.nbytes}"
        self.off = off + fbytes
        h = self.nc.alloc_sbuf_tensor_at(
            f"ar{self.n}_{name}", list(shape), dtype, offset=self.base + off)
        self.n += 1
        b = self.sch.buf(h.ap(), name)
        pend = []
        for ro, rs_, deps in self.retired:
            if off < ro + rs_ and ro < off + fbytes:
                pend.extend(deps)
        b.pending = pend
        b._rng = (off, fbytes)
        self.live.append(b)
        return b

    def reset(self):
        retired = []
        for b in self.live:
            deps = list(b.readers)
            if b.writer is not None:
                deps.append(b.writer)
            if deps:
                retired.append((b._rng[0], b._rng[1], deps))
        self.retired = retired + self.retired
        if len(self.retired) > 400:
            keep, old = self.retired[:200], self.retired[200:]
            best = {}
            for _o, _s, deps in old:
                for dp in deps:
                    if best.get(dp[0], (None, -1))[1] < dp[1]:
                        best[dp[0]] = (dp, dp[1])
            self.retired = keep + [(0, self.nbytes,
                                    [v[0] for v in best.values()])]
        self.live = []
        self.off = 0


class Ctx:
    pass


def E_dma(out_ap, in_ap, **kw):
    return lambda e: e.dma_start(out=out_ap, in_=in_ap, **kw)


def dbx(ctx, name, lo=None, hi=None):
    regs = ctx.db[name]
    if lo is None or len(regs) == 1:
        return list(regs)
    nrows = ctx.d[name].shape[0]
    step = nrows // len(regs)
    return regs[lo // step:(hi + step - 1) // step]


def E_dma_gp(out_ap2d, in_ap2d, w):
    in3 = in_ap2d.rearrange("(g p) w -> p g w", p=128)
    out3 = out_ap2d.rearrange("p (g w) -> p g w", w=w)
    return lambda e: e.dma_start(out=out3, in_=in3)



def _mk_dram(ctx, dev_taps):
    nc = ctx.nc
    d = {}
    db = {}

    def din(name, shape, dt=F32):
        d[name] = nc.dram_tensor(name, list(shape), dt, kind="ExternalInput").ap()

    REGIONED = {"XN": 4, "GTM": 4, "XN2": 4, "HT": 4, "H2": 4,
                "AFM": 16, "SFM": 16}

    def dsc(name, shape, dt):
        kind = "ExternalOutput" if name in dev_taps else "Internal"
        d[name] = nc.dram_tensor(name, list(shape), dt, kind=kind).ap()
        nreg = REGIONED.get(name, 1)
        if nreg == 1:
            db[name] = [Buf(d[name], name)]
        else:
            step = shape[0] // nreg
            db[name] = [Buf(d[name][i * step:(i + 1) * step], f"{name}_{i}")
                        for i in range(nreg)]

    din("x", [T, C])
    din("wqT", [C, C]); din("wkT", [C, C]); din("wvT", [C, C])
    din("tau", [1, 1]); din("nu", [1, 1])
    din("gwT", [C, I0]); din("uwT", [C, I0]); din("dwT", [I0, C])
    din("gs", [128, NI0]); din("us", [128, NI0]); din("ds_row", [1, C])
    din("AlogR", [128, NC * DST]); din("Dr", [128, NC])
    din("xpT", [C, 160]); din("dtT", [128, C]); din("dtb", [128, NC])
    din("anw", [1, C]); din("fnw", [1, C])
    din("qwT", [C, C]); din("kwT", [C, 512]); din("vwT", [C, 512])
    din("owT", [C, C]); din("qnw", [1, HD]); din("knw", [1, HD])
    din("rwT", [C, NE])
    din("wgT", [NE, C, I1]); din("wuT", [NE, C, I1]); din("wdT", [NE, I1, C])
    din("rcos", [T, 64]); din("rsin", [T, 64])
    din("cmask", [4, 128, 512])

    d["out"] = nc.dram_tensor("out", [T, C], F32, kind="ExternalOutput").ap()

    dsc("XN", [T, C], BF16)
    dsc("Q0", [T, C], BF16); dsc("K0", [T, C], BF16); dsc("V0", [T, C], BF16)
    dsc("PD", [2, T, T], BF16)
    dsc("SFM", [C, T], BF16)
    dsc("GTM", [T, C], BF16)
    dsc("X1", [T, C], F32)
    dsc("XN2", [T, C], BF16)
    dsc("HT", [T, C], BF16)
    dsc("Q1", [T, C], BF16); dsc("K1", [T, 512], BF16); dsc("V1", [T, 512], BF16)
    dsc("AFM", [C, T], BF16)
    dsc("HID", [T, C], BF16)
    dsc("H2", [T, C], BF16)
    dsc("BCR", [32, T], F32)
    dsc("Y", [C, T], BF16)
    dsc("MOE", [T, C], BF16)
    ctx.d = d
    ctx.db = db
    ctx.pd_reg = [[Buf(d["PD"][par, i * 128:(i + 1) * 128, :], f"pd{par}_{i}")
                   for i in range(NT)] for par in range(2)]


def _mk_static(ctx):
    nc, sch = ctx.nc, ctx.sch
    A = lambda name, shape, dt: sch.buf(
        nc.alloc_sbuf_tensor(name, list(shape), dt).ap(), name)

    ctx.XNF = [A(f"xnf{i}", [128, T], BF16) for i in range(NC)]
    ctx.cmask = [A(f"cmask{i}", [128, 512], F32) for i in range(4)]
    ctx.tau_bc = A("tau_bc", [128, 1], F32)
    ctx.nu_bc = A("nu_bc", [128, 1], F32)
    ctx.gs = A("gs_sb", [128, NI0], F32)
    ctx.us = A("us_sb", [128, NI0], F32)
    ctx.A_sb = A("A_sb", [128, NC * DST], F32)
    ctx.D_sb = A("D_sb", [128, NC], F32)
    ctx.dtb = A("dtb_sb", [128, NC], F32)
    ctx.qw_bc = A("qw_bc", [128, HD], F32)
    ctx.kw_bc = A("kw_bc", [128, HD], F32)
    ctx.rc = A("rc_sb", [128, NT * 64], F32)
    ctx.rs = A("rs_sb", [128, NT * 64], F32)
    ctx.hst = A("hst", [128, NC * DST], F32)
    ctx.c_eps5 = A("c_eps5", [128, 1], F32)
    ctx.neg30k = A("neg30k", [128, 512], F32)
    ctx.c_eps6 = A("c_eps6", [128, 1], F32)
    ctx.comb = [A(f"comb{i}", [128, NE], F32) for i in range(NT)]
    ctx.rw_sb = A("rw_sb", [128, NC * NE], BF16)

    ctx.PS = [sch.buf(nc.alloc_psum_tensor(f"ps{i}", [128, 512], F32).ap(),
                      f"ps{i}") for i in range(8)]
    ctx._ps_rr = 0

    avail = (nc.sbuf_bytes_remaining - 512) & ~31
    ctx.arena = Arena(nc, sch, "arena", avail)


def ps_take(ctx, n=1):
    out = []
    for _ in range(n):
        out.append(ctx.PS[ctx._ps_rr % 8])
        ctx._ps_rr += 1
    return out if n > 1 else out[0]


def load_consts(ctx):
    sch, d = ctx.sch, ctx.d
    for i in range(4):
        sch.task(QP, E_dma(ctx.cmask[i].ap, d["cmask"][i]),
                 writes=[ctx.cmask[i]], dma=QP)

    def bc(buf, src):
        sch.task("pool", E_dma(buf.ap, src.partition_broadcast(128)),
                 writes=[buf], dma="pool")

    sch.task("dve", lambda e: e.memset(ctx.c_eps5.ap, 1e-5), writes=[ctx.c_eps5])
    sch.task("dve", lambda e: e.memset(ctx.neg30k.ap, -30000.0),
             writes=[ctx.neg30k])
    sch.task("dve", lambda e: e.memset(ctx.c_eps6.ap, 1e-6), writes=[ctx.c_eps6])
    bc(ctx.tau_bc, d["tau"])
    bc(ctx.nu_bc, d["nu"])
    bc(ctx.qw_bc, d["qnw"])
    bc(ctx.kw_bc, d["knw"])
    for name in ("gs", "us", "dtb"):
        sch.task(QP, E_dma(getattr(ctx, name).ap, d[name]),
                 writes=[getattr(ctx, name)], dma=QP)
    sch.task(QP, E_dma(ctx.D_sb.ap, d["Dr"]), writes=[ctx.D_sb], dma=QP)
    sch.task(QP, E_dma(ctx.A_sb.ap, d["AlogR"]), writes=[ctx.A_sb], dma=QP)
    sch.task("act", lambda e: e.activation(ctx.A_sb.ap, ctx.A_sb.ap, AF.Exp),
             reads=[ctx.A_sb], writes=[ctx.A_sb])
    sch.task("dve", lambda e: e.tensor_scalar_mul(ctx.A_sb.ap, ctx.A_sb.ap, -1.0),
             reads=[ctx.A_sb], writes=[ctx.A_sb])
    sch.task(QP, E_dma_gp(ctx.rc.ap, d["rcos"], 64),
             writes=[ctx.rc], dma=QP)
    sch.task(QP, E_dma_gp(ctx.rs.ap, d["rsin"], 64),
             writes=[ctx.rs], dma=QP)
    sch.task("pool", E_dma_gp(ctx.rw_sb.ap, d["rwT"], NE),
             writes=[ctx.rw_sb], dma="pool")



def tload(ctx, dst_buf, dst_ap, src_ap, reads, q=QT):
    ctx.sch.task(q, E_dma(dst_ap, src_ap, transpose=True),
                 reads=reads, writes=[dst_buf], dma=q)


def fill_xnf(ctx, src_name):
    d = ctx.d
    for t4 in range(4):
        for cc in range(NC):
            tload(ctx, ctx.XNF[cc],
                  ctx.XNF[cc].ap[:, t4 * 512:(t4 + 1) * 512],
                  d[src_name][t4 * 512:(t4 + 1) * 512, cc * 128:(cc + 1) * 128],
                  dbx(ctx, src_name, t4 * 512, (t4 + 1) * 512))


def emit_rms(ctx, sm, src_buf, src_ap, wbc_buf, dst_buf, dst_ap):
    sch = ctx.sch
    ss, rstd, junk, t1 = sm["ss"], sm["rstd"], sm["junk"], sm["t1"]
    sch.task("act", lambda e: e.activation(junk.ap, src_ap, AF.Square,
                                           accum_out=ss.ap),
             reads=[src_buf], writes=[junk, ss])
    sch.task("act", lambda e: e.activation(rstd.ap, ss.ap, AF.Ln,
                                           scale=1.0 / C, bias=ctx.c_eps6.ap),
             reads=[ss, ctx.c_eps6], writes=[rstd])
    sch.task("act", lambda e: e.activation(rstd.ap, rstd.ap, AF.Exp, scale=-0.5),
             reads=[rstd], writes=[rstd])
    sch.task("dve", lambda e: e.tensor_scalar_mul(t1.ap, src_ap, rstd.ap),
             reads=[src_buf, rstd], writes=[t1])
    sch.task("dve", lambda e: e.tensor_tensor(dst_ap, t1.ap, wbc_buf.ap, ALU.mult),
             reads=[t1, wbc_buf], writes=[dst_buf])



def phase_ln0(ctx):
    sch, d, db, ar = ctx.sch, ctx.d, ctx.db, ctx.arena
    ar.reset()
    xt = [ar.alloc([128, C], F32, "p1x") for _ in range(2)]
    xn = [ar.alloc([128, C], BF16, "p1n") for _ in range(2)]
    sm = [dict(s1=ar.alloc([128, 1], F32, "s1"), ss=ar.alloc([128, 1], F32, "ss"),
               m=ar.alloc([128, 1], F32, "m"), v=ar.alloc([128, 1], F32, "v"),
               rstd=ar.alloc([128, 1], F32, "rstd"),
               nmr=ar.alloc([128, 1], F32, "nmr"),
               junk=ar.alloc([128, C], BF16, "junk")) for _ in range(2)]
    for tc in range(NT):
        xb, nb, s = xt[tc % 2], xn[tc % 2], sm[tc % 2]
        sch.task(QP, E_dma(xb.ap, d["x"][tc * 128:(tc + 1) * 128, :]),
                 writes=[xb], dma=QP)
        s1, ss, m, v, rstd, nmr, junk = (s[k] for k in
                                         ("s1", "ss", "m", "v", "rstd", "nmr", "junk"))
        sch.task("dve", lambda e, a=xb, o=s1: e.reduce_sum(o.ap, a.ap, axis=AX.X),
                 reads=[xb], writes=[s1])
        sch.task("act", lambda e, a=xb, j=junk, o=ss: e.activation(
            j.ap, a.ap, AF.Square, accum_out=o.ap),
                 reads=[xb], writes=[junk, ss])
        sch.task("dve", lambda e, a=s1, o=m: e.tensor_scalar_mul(o.ap, a.ap, 1.0 / C),
                 reads=[s1], writes=[m])
        sch.task("dve", lambda e, a=m, o=v: e.scalar_tensor_tensor(
            o.ap, a.ap, 1.0, a.ap, ALU.mult, ALU.mult),
                 reads=[m], writes=[v])
        sch.task("dve", lambda e, a=ss, o=v: e.scalar_tensor_tensor(
            o.ap, a.ap, 1.0 / C, o.ap, ALU.mult, ALU.subtract),
                 reads=[ss, v], writes=[v])
        sch.task("act", lambda e, a=v, o=rstd: e.activation(
            o.ap, a.ap, AF.Ln, bias=ctx.c_eps5.ap),
                 reads=[v, ctx.c_eps5], writes=[rstd])
        sch.task("act", lambda e, o=rstd: e.activation(
            o.ap, o.ap, AF.Exp, scale=-0.5),
                 reads=[rstd], writes=[rstd])
        sch.task("dve", lambda e, a=m, b=rstd, o=nmr: e.scalar_tensor_tensor(
            o.ap, a.ap, -1.0, b.ap, ALU.mult, ALU.mult),
                 reads=[m, rstd], writes=[nmr])
        sch.task("dve", lambda e, a=xb, b=rstd, c=nmr, o=nb: e.tensor_scalar(
            o.ap, a.ap, b.ap, c.ap, ALU.mult, ALU.add),
                 reads=[xb, rstd, nmr], writes=[nb])
        sch.task(QP, E_dma(d["XN"][tc * 128:(tc + 1) * 128, :], nb.ap),
                 reads=[nb], writes=dbx(ctx, "XN", tc * 128, (tc + 1) * 128),
                 dma=QP)



def phase_qkv0(ctx):
    sch, d, db, ar = ctx.sch, ctx.d, ctx.db, ctx.arena
    ar.reset()
    fill_xnf(ctx, "XN")
    W = [ar.alloc([128, C], BF16, f"w0_{i}") for i in range(NC)]
    qh = [ar.alloc([128, C], BF16, "qh") for _ in range(2)]
    ssl = [ar.alloc([128, 16], F32, "ssl") for _ in range(2)]
    rf = [ar.alloc([128, 16], F32, "rf") for _ in range(2)]
    junk = [ar.alloc([128, 128], BF16, "jk") for _ in range(2)]

    for wname, dst, is_q, is_v in (("wqT", "Q0", True, False),
                                   ("wkT", "K0", False, False),
                                   ("wvT", "V0", False, True)):
        for cc in range(NC):
            sch.task("pool", E_dma(W[cc].ap,
                                   d[wname][cc * 128:(cc + 1) * 128, :]),
                     writes=[W[cc]], dma="pool")
        for tc in range(NT):
            ps = [ps_take(ctx) for _ in range(4)]
            for c4 in range(4):
                def mme(e, tc=tc, c4=c4, p=ps[c4]):
                    for cc in range(NC):
                        inst = e.matmul(
                            p.ap,
                            ctx.XNF[cc].ap[:, tc * 128:(tc + 1) * 128],
                            W[cc].ap[:, c4 * 512:(c4 + 1) * 512],
                            start=(cc == 0), stop=(cc == NC - 1))
                    return inst
                sch.task("pe", mme, reads=ctx.XNF + W, writes=[ps[c4]])
            q, sl, r, jk = qh[tc % 2], ssl[tc % 2], rf[tc % 2], junk[tc % 2]
            for c4 in range(4):
                for j in range(4):
                    hidx = c4 * 4 + j
                    sch.task("act", lambda e, p=ps[c4], j=j, h=hidx, jk=jk, sl=sl:
                             e.activation(
                                 jk.ap, p.ap[:, j * 128:(j + 1) * 128], AF.Square,
                                 accum_out=sl.ap[:, h:h + 1]),
                             reads=[ps[c4]], writes=[jk, sl])
            sch.task("act", lambda e, a=sl, o=r: e.activation(o.ap, a.ap, AF.Ln),
                     reads=[sl], writes=[r])
            sch.task("act", lambda e, o=r: e.activation(o.ap, o.ap, AF.Exp,
                                                        scale=0.5),
                     reads=[r], writes=[r])
            sch.task("dve", lambda e, o=r: e.tensor_scalar_add(o.ap, o.ap, 1e-8),
                     reads=[r], writes=[r])
            sch.task("dve", lambda e, o=r: e.reciprocal(o.ap, o.ap),
                     reads=[r], writes=[r])
            if is_q:
                sch.task("dve", lambda e, o=r: e.tensor_scalar_mul(
                    o.ap, o.ap, ctx.tau_bc.ap),
                         reads=[r, ctx.tau_bc], writes=[r])
            for c4 in range(4):
                for j in range(4):
                    hidx = c4 * 4 + j
                    sch.task("dve", lambda e, p=ps[c4], j=j, h=hidx, q=q, r=r:
                             e.tensor_scalar_mul(
                                 q.ap[:, h * 128:(h + 1) * 128],
                                 p.ap[:, j * 128:(j + 1) * 128], r.ap[:, h:h + 1]),
                             reads=[ps[c4], r], writes=[q])
            sch.task(QP, E_dma(d[dst][tc * 128:(tc + 1) * 128, :], q.ap),
                     reads=[q], writes=db[dst], dma=QP)



def phase_attn(ctx, layer, hooks=None, reset=True):
    sch, d, db, ar = ctx.sch, ctx.d, ctx.db, ctx.arena
    if reset:
        ar.reset()
    if layer == 0:
        QN, KN, VN, ON = "Q0", "K0", "V0", "SFM"
        n_kv = 16
        kv_w = C
    else:
        QN, KN, VN, ON = "Q1", "K1", "V1", "AFM"
        n_kv = 4
        kv_w = 512
    rep = NH // n_kv

    qf = [ar.alloc([128, T], BF16, "qf") for _ in range(2)]
    kf = [ar.alloc([128, T], BF16, "kf") for _ in range(2)]
    vt = [ar.alloc([128, T], BF16, "vt") for _ in range(2)]
    psb = [[ar.alloc([128, T], BF16, "psb") for _ in range(3)]
           for _ in range(2)]
    pt = [ar.alloc([128, 512], BF16, "pt") for _ in range(4)]
    att = [ar.alloc([128, 512], BF16, "att") for _ in range(2)]
    negm = [[ar.alloc([128, 4], F32, "negm") for _ in range(3)]
            for _ in range(2)]
    rssb = [[ar.alloc([128, 4], F32, "rssb") for _ in range(3)]
            for _ in range(2)]
    den = [[ar.alloc([128, 1], F32, "den") for _ in range(3)]
           for _ in range(2)]

    def emit_S(h):
        par = h % 2
        g = h // rep
        tload(ctx, qf[par], qf[par].ap,
              d[QN][:, h * 128:(h + 1) * 128], db[QN])
        if h % rep == 0:
            tload(ctx, kf[g % 2], kf[g % 2].ap,
                  d[KN][:, g * 128:(g + 1) * 128], db[KN])
            for tkc in range(NT):
                sch.task(QP, E_dma(vt[g % 2].ap[:, tkc * 128:(tkc + 1) * 128],
                                     d[VN][tkc * 128:(tkc + 1) * 128,
                                           g * 128:(g + 1) * 128]),
                         reads=db[VN], writes=[vt[g % 2]], dma=QP)
        kbuf = kf[g % 2]

        def s_pair(pair):
            cinfo = []
            for tqc in pair:
                P, nm, rs, dn = (psb[par][tqc % 3], negm[par][tqc % 3],
                                 rssb[par][tqc % 3], den[par][tqc % 3])
                nch = tqc // 4 + 1
                banks = [ps_take(ctx) for _ in range(nch)]
                cinfo.append((tqc, P, nm, rs, dn, nch, banks))
            for tqc, P, nm, rs, dn, nch, banks in cinfo:
                for c, ps in enumerate(banks):
                    sch.task("pe", lambda e, p=ps, tqc=tqc, c=c, q=qf[par],
                             k=kbuf:
                             e.matmul(p.ap, q.ap[:, tqc * 128:(tqc + 1) * 128],
                                      k.ap[:, c * 512:(c + 1) * 512],
                                      start=True, stop=True),
                             reads=[qf[par], kbuf], writes=[ps])
            for tqc, P, nm, rs, dn, nch, banks in cinfo:
                dcb = banks[nch - 1]
                mv = tqc % 4
                sch.task("dve", lambda e, p=dcb, m=mv: e.scalar_tensor_tensor(
                    p.ap, p.ap, 30000.0, ctx.cmask[m].ap, ALU.add, ALU.mult),
                         reads=[dcb, ctx.cmask[mv]], writes=[dcb])
                sch.task("dve", lambda e, p=dcb: e.tensor_scalar_add(
                    p.ap, p.ap, -30000.0),
                         reads=[dcb], writes=[dcb])
            for tqc, P, nm, rs, dn, nch, banks in cinfo:
                for c, ps in enumerate(banks):
                    sch.task("dve", lambda e, p=ps, c=c, nm=nm: e.reduce_max(
                        nm.ap[:, c:c + 1], p.ap, axis=AX.X, negate=True),
                             reads=[ps], writes=[nm])
                if nch > 1:
                    sch.task("dve", lambda e, n=nch, nm=nm, dn=dn:
                             e.tensor_reduce(
                        dn.ap, nm.ap[:, 0:n], AX.X, ALU.min),
                             reads=[nm], writes=[dn])
                else:
                    sch.task("dve", lambda e, nm=nm, dn=dn: e.tensor_copy(
                        dn.ap, nm.ap[:, 0:1]),
                             reads=[nm], writes=[dn])
            for tqc, P, nm, rs, dn, nch, banks in cinfo:
                for c, ps in enumerate(banks):
                    sch.task("act", lambda e, p=ps, c=c, P=P, dn=dn, rs=rs:
                             e.activation(
                        P.ap[:, c * 512:(c + 1) * 512], p.ap, AF.Exp,
                        bias=dn.ap, accum_out=rs.ap[:, c:c + 1]),
                             reads=[ps, dn], writes=[P, rs])
            for tqc, P, nm, rs, dn, nch, banks in cinfo:
                sch.task("dve", lambda e, n=nch, rs=rs, dn=dn: e.reduce_sum(
                    dn.ap, rs.ap[:, 0:n], axis=AX.X),
                         reads=[rs], writes=[dn])
                sch.task("dve", lambda e, dn=dn: e.reciprocal(dn.ap, dn.ap),
                         reads=[dn], writes=[dn])
                if layer == 0:
                    sch.task("dve", lambda e, dn=dn: e.tensor_scalar_mul(
                        dn.ap, dn.ap, ctx.nu_bc.ap),
                             reads=[dn, ctx.nu_bc], writes=[dn])
                sch.task("dve", lambda e, n=nch, P=P, dn=dn: e.tensor_scalar_mul(
                    P.ap[:, 0:n * 512], P.ap[:, 0:n * 512], dn.ap),
                         reads=[P, dn], writes=[P])
                sch.task(QP, E_dma(
                    d["PD"][par, tqc * 128:(tqc + 1) * 128, 0:nch * 512],
                    P.ap[:, 0:nch * 512]),
                         reads=[P], writes=[ctx.pd_reg[par][tqc]], dma=QP)
        return s_pair

    def emit_PV_group(h, g4):
        par = h % 2
        g = h // rep
        vbuf = vt[g % 2]
        ao = att[g4 % 2]
        ntk = g4 * 4 + 4
        psum = ps_take(ctx)
        for tkc in range(ntk):
            ptb = pt[tkc % 4]
            regs = [ctx.pd_reg[par][g4 * 4 + i] for i in range(4)]
            tload(ctx, ptb, ptb.ap,
                  d["PD"][par, g4 * 512:(g4 + 1) * 512,
                          tkc * 128:(tkc + 1) * 128], regs)
            sch.task("pe", lambda e, p=psum, v=vbuf, tkc=tkc, b=ptb,
                     st=(tkc == 0), sp=(tkc == ntk - 1):
                     e.matmul(p.ap, v.ap[:, tkc * 128:(tkc + 1) * 128],
                              b.ap, start=st, stop=sp),
                     reads=[vbuf, ptb], writes=[psum])
        sch.task("act", lambda e, p=psum, ao=ao: e.activation(
            ao.ap, p.ap, AF.Copy),
                 reads=[psum], writes=[ao])
        sch.task(QP, E_dma(d[ON][h * 128:(h + 1) * 128,
                                 g4 * 512:(g4 + 1) * 512], ao.ap),
                 reads=[ao], writes=dbx(ctx, ON, h * 128, (h + 1) * 128),
                 dma=QP)

    for h in range(NH):
        s_pair = emit_S(h)
        for pi in range(8):
            s_pair((2 * pi, 2 * pi + 1))
            if h > 0 and pi % 2 == 1:
                emit_PV_group(h - 1, pi // 2)
        if hooks and h in hooks:
            hooks[h]()
    for g4 in range(4):
        emit_PV_group(NH - 1, g4)



def phase_ffn0(ctx):
    sch, d, db, ar = ctx.sch, ctx.d, ctx.db, ctx.arena
    ar.reset()
    gsl = [ar.alloc([128, 16 * 256], BF16, "gsl") for _ in range(2)]
    usl = [ar.alloc([128, 16 * 256], BF16, "usl") for _ in range(2)]
    hb = [ar.alloc([128, 512], BF16, f"hb{i}") for i in range(NI0)]
    dslb = [ar.alloc([128, 11 * 512], BF16, "dslb") for _ in range(2)]
    sgt = [ar.alloc([128, 512], BF16, "sgt") for _ in range(2)]
    sut = [ar.alloc([128, 512], BF16, "sut") for _ in range(2)]
    tmpf = [ar.alloc([128, 512], F32, "tmpf") for _ in range(2)]
    gt = [ar.alloc([128, 512], BF16, "gt") for _ in range(2)]
    dsbc = ar.alloc([128, C], F32, "dsbc")
    sch.task("pool", E_dma(dsbc.ap, d["ds_row"].partition_broadcast(128)),
             writes=[dsbc], dma="pool")

    f_loaded = set()

    def f_gu_load(t4, icg):
        if (t4, icg) in f_loaded:
            return
        f_loaded.add((t4, icg))
        i0 = icg * 256
        gb, ub = gsl[icg % 2], usl[icg % 2]
        sch.task("pool", E_dma_gp(gb.ap, d["gwT"][:, i0:i0 + 256], 256),
                 writes=[gb], dma="pool")
        sch.task("pool", E_dma_gp(ub.ap, d["uwT"][:, i0:i0 + 256], 256),
                 writes=[ub], dma="pool")

    for t4 in range(4):
        t0 = t4 * 512
        for icg in range(22):
            i0 = icg * 256
            gb, ub = gsl[icg % 2], usl[icg % 2]
            f_gu_load(t4, icg)
            for j in range(2):
                ic = icg * 2 + j
                pg = ps_take(ctx)
                pu = ps_take(ctx)

                def mme(e, wb=gb, j=j, p=pg, t0=t0):
                    for cc in range(NC):
                        inst = e.matmul(
                            p.ap,
                            wb.ap[:, cc * 256 + j * 128: cc * 256 + (j + 1) * 128],
                            ctx.XNF[cc].ap[:, t0:t0 + 512],
                            start=(cc == 0), stop=(cc == NC - 1))
                    return inst
                sch.task("pe", lambda e, f=mme: f(e),
                         reads=ctx.XNF + [gb], writes=[pg])
                sch.task("pe", lambda e, wb=ub, j=j, p=pu, f=mme: f(e, wb, j, p),
                         reads=ctx.XNF + [ub], writes=[pu])
                sg, su = sgt[ic % 2], sut[ic % 2]
                sch.task("act", lambda e, p=pg, ic=ic, o=sg: e.activation(
                    o.ap, p.ap, AF.Silu, scale=ctx.gs.ap[:, ic:ic + 1]),
                         reads=[pg, ctx.gs], writes=[sg])
                sch.task("act", lambda e, p=pu, ic=ic, o=su: e.activation(
                    o.ap, p.ap, AF.Copy, scale=ctx.us.ap[:, ic:ic + 1]),
                         reads=[pu, ctx.us], writes=[su])
                sch.task("dve", lambda e, a=sg, b=su, o=hb[ic]: e.tensor_tensor(
                    o.ap, a.ap, b.ap, ALU.mult),
                         reads=[sg, su], writes=[hb[ic]])
        if t4 + 1 < 4:
            f_gu_load(t4 + 1, 0)
            f_gu_load(t4 + 1, 1)
        for co4 in range(4):
            co0 = co4 * 512
            psds = [ps_take(ctx) for _ in range(4)]
            for qr in range(4):
                r0 = qr * 11 * 128
                wb = dslb[qr % 2]
                sch.task("pool", E_dma_gp(
                    wb.ap, d["dwT"][r0:r0 + 11 * 128, co0:co0 + 512], 512),
                         writes=[wb], dma="pool")
                for tci in range(4):
                    def mmd(e, tci=tci, p=psds[tci], wb=wb, qr=qr):
                        for k in range(11):
                            ic = qr * 11 + k
                            inst = e.matmul(
                                p.ap,
                                hb[ic].ap[:, tci * 128:(tci + 1) * 128],
                                wb.ap[:, k * 512:(k + 1) * 512],
                                start=(qr == 0 and k == 0),
                                stop=(qr == 3 and k == 10))
                        return inst
                    sch.task("pe", mmd,
                             reads=hb[qr * 11:(qr + 1) * 11] + [wb],
                             writes=[psds[tci]])
            for tci in range(4):
                psd = psds[tci]
                tf, g = tmpf[tci % 2], gt[tci % 2]
                sch.task("dve", lambda e, p=psd, o=tf, co0=co0: e.tensor_tensor(
                    o.ap, p.ap, dsbc.ap[:, co0:co0 + 512], ALU.mult),
                         reads=[psd, dsbc], writes=[tf])
                sch.task("act", lambda e, a=tf, o=g: e.activation(
                    o.ap, a.ap, AF.Silu),
                         reads=[tf], writes=[g])
                tc = t4 * 4 + tci
                sch.task(QP, E_dma(
                    d["GTM"][tc * 128:(tc + 1) * 128, co0:co0 + 512], g.ap),
                         reads=[g],
                         writes=dbx(ctx, "GTM", tc * 128, (tc + 1) * 128),
                         dma=QP)



def phase_combine0(ctx):
    sch, d, db, ar = ctx.sch, ctx.d, ctx.db, ctx.arena
    ar.reset()
    xt = [ar.alloc([128, C], F32, "c0x") for _ in range(2)]
    gt = [ar.alloc([128, C], BF16, "c0g") for _ in range(2)]
    st = [ar.alloc([128, C], BF16, "c0s") for _ in range(2)]
    sg = [ar.alloc([128, C], BF16, "c0sg") for _ in range(2)]
    nb = [ar.alloc([128, C], BF16, "c0n") for _ in range(2)]
    sm = [dict(s1=ar.alloc([128, 1], F32, "s1"), ss=ar.alloc([128, 1], F32, "ss"),
               m=ar.alloc([128, 1], F32, "m"), v=ar.alloc([128, 1], F32, "v"),
               rstd=ar.alloc([128, 1], F32, "rstd"),
               nmr=ar.alloc([128, 1], F32, "nmr"),
               junk=ar.alloc([128, C], BF16, "junk")) for _ in range(2)]
    for tc in range(NT):
        t0 = tc * 128
        xb, g, s, p, nbb, smm = (xt[tc % 2], gt[tc % 2], st[tc % 2],
                                 sg[tc % 2], nb[tc % 2], sm[tc % 2])
        sch.task(QP, E_dma(xb.ap, d["x"][t0:t0 + 128, :]),
                 writes=[xb], dma=QP)
        sch.task(QP, E_dma(g.ap, d["GTM"][t0:t0 + 128, :]),
                 reads=dbx(ctx, "GTM", t0, t0 + 128), writes=[g], dma=QP)
        for c4 in range(4):
            tload(ctx, s, s.ap[:, c4 * 512:(c4 + 1) * 512],
                  d["SFM"][c4 * 512:(c4 + 1) * 512, t0:t0 + 128],
                  dbx(ctx, "SFM", c4 * 512, (c4 + 1) * 512))
        sch.task("dve", lambda e, a=s, b=g, o=p: e.tensor_tensor(
            o.ap, a.ap, b.ap, ALU.mult), reads=[s, g], writes=[p])
        sch.task("dve", lambda e, a=xb, b=p: e.tensor_tensor(
            a.ap, a.ap, b.ap, ALU.add), reads=[xb, p], writes=[xb])
        sch.task(QP, E_dma(d["X1"][t0:t0 + 128, :], xb.ap),
                 reads=[xb], writes=db["X1"], dma=QP)
        s1, ss, m, v, rstd, nmr, junk = (smm[k] for k in
                                         ("s1", "ss", "m", "v", "rstd", "nmr", "junk"))
        sch.task("dve", lambda e, a=xb, o=s1: e.reduce_sum(o.ap, a.ap, axis=AX.X),
                 reads=[xb], writes=[s1])
        sch.task("act", lambda e, a=xb, j=junk, o=ss: e.activation(
            j.ap, a.ap, AF.Square, accum_out=o.ap),
                 reads=[xb], writes=[junk, ss])
        sch.task("dve", lambda e, a=s1, o=m: e.tensor_scalar_mul(o.ap, a.ap, 1.0 / C),
                 reads=[s1], writes=[m])
        sch.task("dve", lambda e, a=m, o=v: e.scalar_tensor_tensor(
            o.ap, a.ap, 1.0, a.ap, ALU.mult, ALU.mult), reads=[m], writes=[v])
        sch.task("dve", lambda e, a=ss, o=v: e.scalar_tensor_tensor(
            o.ap, a.ap, 1.0 / C, o.ap, ALU.mult, ALU.subtract),
                 reads=[ss, v], writes=[v])
        sch.task("act", lambda e, a=v, o=rstd: e.activation(
            o.ap, a.ap, AF.Ln, bias=ctx.c_eps5.ap),
                 reads=[v, ctx.c_eps5], writes=[rstd])
        sch.task("act", lambda e, o=rstd: e.activation(
            o.ap, o.ap, AF.Exp, scale=-0.5), reads=[rstd], writes=[rstd])
        sch.task("dve", lambda e, a=m, b=rstd, o=nmr: e.scalar_tensor_tensor(
            o.ap, a.ap, -1.0, b.ap, ALU.mult, ALU.mult),
                 reads=[m, rstd], writes=[nmr])
        sch.task("dve", lambda e, a=xb, b=rstd, c=nmr, o=nbb: e.tensor_scalar(
            o.ap, a.ap, b.ap, c.ap, ALU.mult, ALU.add),
                 reads=[xb, rstd, nmr], writes=[nbb])
        sch.task(QP, E_dma(d["XN2"][t0:t0 + 128, :], nbb.ap),
                 reads=[nbb], writes=dbx(ctx, "XN2", t0, t0 + 128), dma=QP)



def phase_rms_attn(ctx):
    sch, d, db, ar = ctx.sch, ctx.d, ctx.db, ctx.arena
    ar.reset()
    anw = ar.alloc([128, C], F32, "anwbc")
    sch.task("pool", E_dma(anw.ap, d["anw"].partition_broadcast(128)),
             writes=[anw], dma="pool")
    xb = [ar.alloc([128, C], BF16, "p5x") for _ in range(2)]
    hb = [ar.alloc([128, C], BF16, "p5h") for _ in range(2)]
    sm = [dict(ss=ar.alloc([128, 1], F32, "ss"),
               rstd=ar.alloc([128, 1], F32, "rstd"),
               junk=ar.alloc([128, C], BF16, "junk"),
               t1=ar.alloc([128, C], BF16, "t1")) for _ in range(2)]
    for tc in range(NT):
        t0 = tc * 128
        a, o, s = xb[tc % 2], hb[tc % 2], sm[tc % 2]
        sch.task(QP, E_dma(a.ap, d["XN2"][t0:t0 + 128, :]),
                 reads=dbx(ctx, "XN2", t0, t0 + 128), writes=[a], dma=QP)
        emit_rms(ctx, s, a, a.ap, anw, o, o.ap)
        sch.task(QP, E_dma(d["HT"][t0:t0 + 128, :], o.ap),
                 reads=[o], writes=dbx(ctx, "HT", t0, t0 + 128), dma=QP)



def phase_qkv1(ctx):
    sch, d, db, ar = ctx.sch, ctx.d, ctx.db, ctx.arena
    ar.reset()
    fill_xnf(ctx, "HT")
    W = [ar.alloc([128, C], BF16, f"w1_{i}") for i in range(NC)]
    kw = [ar.alloc([128, 512], BF16, f"kw{i}") for i in range(NC)]
    vw = [ar.alloc([128, 512], BF16, f"vw{i}") for i in range(NC)]
    qh = [ar.alloc([128, C], BF16, "q1h") for _ in range(2)]
    kh = [ar.alloc([128, 512], BF16, "k1h") for _ in range(2)]
    vh = [ar.alloc([128, 512], BF16, "v1h") for _ in range(2)]
    ssl = [ar.alloc([128, 16], F32, "ssl1") for _ in range(2)]
    rstd = [ar.alloc([128, 16], F32, "rstd1") for _ in range(2)]
    junk = [ar.alloc([128, 128], BF16, "jk1") for _ in range(2)]
    qn = [ar.alloc([128, 128], F32, "qn") for _ in range(2)]
    ma = [ar.alloc([128, 64], F32, "ma") for _ in range(2)]
    mb = [ar.alloc([128, 64], F32, "mb") for _ in range(2)]
    cw1q = ar.alloc([128, 64], F32, "cw1q"); sw2q = ar.alloc([128, 64], F32, "sw2q")
    cw2q = ar.alloc([128, 64], F32, "cw2q"); sw1q = ar.alloc([128, 64], F32, "sw1q")
    cw1k = ar.alloc([128, 64], F32, "cw1k"); sw2k = ar.alloc([128, 64], F32, "sw2k")
    cw2k = ar.alloc([128, 64], F32, "cw2k"); sw1k = ar.alloc([128, 64], F32, "sw1k")

    for cc in range(NC):
        sch.task("pool", E_dma(W[cc].ap, d["qwT"][cc * 128:(cc + 1) * 128, :]),
                 writes=[W[cc]], dma="pool")
        sch.task("pool", E_dma(kw[cc].ap, d["kwT"][cc * 128:(cc + 1) * 128, :]),
                 writes=[kw[cc]], dma="pool")
        sch.task("pool", E_dma(vw[cc].ap, d["vwT"][cc * 128:(cc + 1) * 128, :]),
                 writes=[vw[cc]], dma="pool")

    def rope_tabs(tc, w_bc, c1, s2, c2, s1):
        f0 = tc * 64
        sch.task("dve", lambda e: e.tensor_tensor(
            c1.ap, ctx.rc.ap[:, f0:f0 + 64], w_bc.ap[:, 0:64], ALU.mult),
                 reads=[ctx.rc, w_bc], writes=[c1])
        sch.task("dve", lambda e: e.tensor_tensor(
            s2.ap, ctx.rs.ap[:, f0:f0 + 64], w_bc.ap[:, 64:128], ALU.mult),
                 reads=[ctx.rs, w_bc], writes=[s2])
        sch.task("dve", lambda e: e.tensor_tensor(
            c2.ap, ctx.rc.ap[:, f0:f0 + 64], w_bc.ap[:, 64:128], ALU.mult),
                 reads=[ctx.rc, w_bc], writes=[c2])
        sch.task("dve", lambda e: e.tensor_tensor(
            s1.ap, ctx.rs.ap[:, f0:f0 + 64], w_bc.ap[:, 0:64], ALU.mult),
                 reads=[ctx.rs, w_bc], writes=[s1])

    def rstd_batch(sl, rr, n, extra_scale):
        sch.task("act", lambda e: e.activation(
            rr.ap[:, 0:n], sl.ap[:, 0:n], AF.Ln, scale=1.0 / HD,
            bias=ctx.c_eps6.ap),
                 reads=[sl, ctx.c_eps6], writes=[rr])
        sch.task("act", lambda e: e.activation(
            rr.ap[:, 0:n], rr.ap[:, 0:n], AF.Exp, scale=-0.5),
                 reads=[rr], writes=[rr])
        if extra_scale != 1.0:
            sch.task("dve", lambda e: e.tensor_scalar_mul(
                rr.ap[:, 0:n], rr.ap[:, 0:n], float(extra_scale)),
                     reads=[rr], writes=[rr])

    for tc in range(NT):
        t0 = tc * 128
        par = tc % 2
        rope_tabs(tc, ctx.qw_bc, cw1q, sw2q, cw2q, sw1q)
        rope_tabs(tc, ctx.kw_bc, cw1k, sw2k, cw2k, sw1k)
        ps = [ps_take(ctx) for _ in range(4)]
        for c4 in range(4):
            def mme(e, tc=tc, c4=c4, p=ps[c4]):
                for cc in range(NC):
                    inst = e.matmul(
                        p.ap, ctx.XNF[cc].ap[:, tc * 128:(tc + 1) * 128],
                        W[cc].ap[:, c4 * 512:(c4 + 1) * 512],
                        start=(cc == 0), stop=(cc == NC - 1))
                return inst
            sch.task("pe", mme, reads=ctx.XNF + W, writes=[ps[c4]])
        sl, rr = ssl[par], rstd[par]
        jk = junk[par]
        for c4 in range(4):
            for j in range(4):
                sch.task("act", lambda e, p=ps[c4], j=j, h=c4 * 4 + j,
                         jk=jk, sl=sl: e.activation(
                    jk.ap, p.ap[:, j * 128:(j + 1) * 128], AF.Square,
                    accum_out=sl.ap[:, h:h + 1]),
                         reads=[ps[c4]], writes=[jk, sl])
        rstd_batch(sl, rr, 16, ISQ)
        q, a, b, oh = qn[par], ma[par], mb[par], qh[par]
        for c4 in range(4):
            for j in range(4):
                hidx = c4 * 4 + j
                sch.task("dve", lambda e, p=ps[c4], j=j, h=hidx, q=q, rr=rr:
                         e.tensor_scalar_mul(
                             q.ap, p.ap[:, j * 128:(j + 1) * 128],
                             rr.ap[:, h:h + 1]),
                         reads=[ps[c4], rr], writes=[q])
                off = hidx * 128
                sch.task("dve", lambda e, a=a, q=q: e.tensor_tensor(
                    a.ap, q.ap[:, 0:64], cw1q.ap, ALU.mult),
                         reads=[q, cw1q], writes=[a])
                sch.task("dve", lambda e, b=b, q=q: e.tensor_tensor(
                    b.ap, q.ap[:, 64:128], sw2q.ap, ALU.mult),
                         reads=[q, sw2q], writes=[b])
                sch.task("dve", lambda e, o=off, a=a, b=b, oh=oh: e.tensor_tensor(
                    oh.ap[:, o:o + 64], a.ap, b.ap, ALU.subtract),
                         reads=[a, b], writes=[oh])
                sch.task("dve", lambda e, a=a, q=q: e.tensor_tensor(
                    a.ap, q.ap[:, 64:128], cw2q.ap, ALU.mult),
                         reads=[q, cw2q], writes=[a])
                sch.task("dve", lambda e, b=b, q=q: e.tensor_tensor(
                    b.ap, q.ap[:, 0:64], sw1q.ap, ALU.mult),
                         reads=[q, sw1q], writes=[b])
                sch.task("dve", lambda e, o=off, a=a, b=b, oh=oh: e.tensor_tensor(
                    oh.ap[:, o + 64:o + 128], a.ap, b.ap, ALU.add),
                         reads=[a, b], writes=[oh])
        sch.task(QP, E_dma(d["Q1"][t0:t0 + 128, :], oh.ap),
                 reads=[oh], writes=db["Q1"], dma=QP)
        pk = ps_take(ctx)
        pv = ps_take(ctx)
        for p, wws in ((pk, kw), (pv, vw)):
            def mme2(e, p=p, wws=wws, tc=tc):
                for cc in range(NC):
                    inst = e.matmul(
                        p.ap, ctx.XNF[cc].ap[:, tc * 128:(tc + 1) * 128],
                        wws[cc].ap, start=(cc == 0), stop=(cc == NC - 1))
                return inst
            sch.task("pe", mme2, reads=ctx.XNF + wws, writes=[p])
        for j in range(4):
            sch.task("act", lambda e, p=pk, j=j, jk=jk, sl=sl: e.activation(
                jk.ap, p.ap[:, j * 128:(j + 1) * 128], AF.Square,
                accum_out=sl.ap[:, j:j + 1]),
                     reads=[pk], writes=[jk, sl])
        rstd_batch(sl, rr, 4, 1.0)
        ok = kh[par]
        for j in range(4):
            sch.task("dve", lambda e, p=pk, j=j, q=q, rr=rr: e.tensor_scalar_mul(
                q.ap, p.ap[:, j * 128:(j + 1) * 128], rr.ap[:, j:j + 1]),
                     reads=[pk, rr], writes=[q])
            off = j * 128
            sch.task("dve", lambda e, a=a, q=q: e.tensor_tensor(
                a.ap, q.ap[:, 0:64], cw1k.ap, ALU.mult),
                     reads=[q, cw1k], writes=[a])
            sch.task("dve", lambda e, b=b, q=q: e.tensor_tensor(
                b.ap, q.ap[:, 64:128], sw2k.ap, ALU.mult),
                     reads=[q, sw2k], writes=[b])
            sch.task("dve", lambda e, o=off, a=a, b=b, ok=ok: e.tensor_tensor(
                ok.ap[:, o:o + 64], a.ap, b.ap, ALU.subtract),
                     reads=[a, b], writes=[ok])
            sch.task("dve", lambda e, a=a, q=q: e.tensor_tensor(
                a.ap, q.ap[:, 64:128], cw2k.ap, ALU.mult),
                     reads=[q, cw2k], writes=[a])
            sch.task("dve", lambda e, b=b, q=q: e.tensor_tensor(
                b.ap, q.ap[:, 0:64], sw1k.ap, ALU.mult),
                     reads=[q, sw1k], writes=[b])
            sch.task("dve", lambda e, o=off, a=a, b=b, ok=ok: e.tensor_tensor(
                ok.ap[:, o + 64:o + 128], a.ap, b.ap, ALU.add),
                     reads=[a, b], writes=[ok])
        sch.task(QP, E_dma(d["K1"][t0:t0 + 128, :], ok.ap),
                 reads=[ok], writes=db["K1"], dma=QP)
        sch.task("act", lambda e, p=pv, o=vh[par]: e.activation(
            o.ap, p.ap, AF.Copy), reads=[pv], writes=[vh[par]])
        sch.task(QP, E_dma(d["V1"][t0:t0 + 128, :], vh[par].ap),
                 reads=[vh[par]], writes=db["V1"], dma=QP)



def phase_oproj(ctx):
    sch, d, db, ar = ctx.sch, ctx.d, ctx.db, ctx.arena
    ar.reset()
    for hc in range(NC):
        sch.task(QP, E_dma(ctx.XNF[hc].ap, d["AFM"][hc * 128:(hc + 1) * 128, :]),
                 reads=dbx(ctx, "AFM", hc * 128, (hc + 1) * 128),
                 writes=[ctx.XNF[hc]], dma=QP)
    W = [ar.alloc([128, C], BF16, f"ow{i}") for i in range(NC)]
    for cc in range(NC):
        sch.task("pool", E_dma(W[cc].ap, d["owT"][cc * 128:(cc + 1) * 128, :]),
                 writes=[W[cc]], dma="pool")
    fnw = ar.alloc([128, C], F32, "fnwbc")
    sch.task("pool", E_dma(fnw.ap, d["fnw"].partition_broadcast(128)),
             writes=[fnw], dma="pool")
    xt = [ar.alloc([128, C], BF16, "p8x") for _ in range(2)]
    hid = [ar.alloc([128, C], BF16, "p8h") for _ in range(2)]
    h2 = [ar.alloc([128, C], BF16, "p8h2") for _ in range(2)]
    sm = [dict(ss=ar.alloc([128, 1], F32, "ss"),
               rstd=ar.alloc([128, 1], F32, "rstd"),
               junk=ar.alloc([128, C], BF16, "junk"),
               t1=ar.alloc([128, C], BF16, "t1")) for _ in range(2)]
    for tc in range(NT):
        t0 = tc * 128
        par = tc % 2
        xb, hh, h2b, s = xt[par], hid[par], h2[par], sm[par]
        sch.task(QP, E_dma(xb.ap, d["XN2"][t0:t0 + 128, :]),
                 reads=dbx(ctx, "XN2", t0, t0 + 128), writes=[xb], dma=QP)
        for c4 in range(4):
            p = ps_take(ctx)

            def mme(e, tc=tc, c4=c4, p=p):
                for hc in range(NC):
                    inst = e.matmul(
                        p.ap, ctx.XNF[hc].ap[:, tc * 128:(tc + 1) * 128],
                        W[hc].ap[:, c4 * 512:(c4 + 1) * 512],
                        start=(hc == 0), stop=(hc == NC - 1))
                return inst
            sch.task("pe", mme, reads=ctx.XNF + W, writes=[p])
            sch.task("dve", lambda e, p=p, c4=c4, a=xb, o=hh: e.tensor_tensor(
                o.ap[:, c4 * 512:(c4 + 1) * 512],
                a.ap[:, c4 * 512:(c4 + 1) * 512], p.ap, ALU.add),
                     reads=[p, xb], writes=[hh])
        sch.task(QP, E_dma(d["HID"][t0:t0 + 128, :], hh.ap),
                 reads=[hh], writes=db["HID"], dma=QP)
        emit_rms(ctx, s, hh, hh.ap, fnw, h2b, h2b.ap)
        sch.task(QP, E_dma(d["H2"][t0:t0 + 128, :], h2b.ap),
                 reads=[h2b], writes=dbx(ctx, "H2", t0, t0 + 128), dma=QP)



def scan_setup(ctx):
    sch, d, db, ar = ctx.sch, ctx.d, ctx.db, ctx.arena
    st = Ctx()
    xp = ar.alloc([128, NC * 160], BF16, "xpT")
    dt = ar.alloc([128, C], BF16, "dtT")
    sch.task("pool", E_dma_gp(xp.ap, d["xpT"], 160), writes=[xp], dma="pool")
    sch.task("pool", E_dma(dt.ap, d["dtT"]), writes=[dt], dma="pool")
    sch.task("dve", lambda e: e.memset(ctx.hst.ap, 0.0), writes=[ctx.hst])

    st.xp, st.dt = xp, dt
    st.u = [ar.alloc([128, 512], BF16, f"u{i}") for i in range(2)]
    st.u2 = [ar.alloc([128, 512], BF16, f"u2_{i}") for i in range(2)]
    st.dl = [ar.alloc([128, 512], F32, f"dl{i}") for i in range(2)]
    st.yac = [ar.alloc([128, 512], F32, f"y{i}") for i in range(2)]
    st.Bbc = [ar.alloc([128, 512], BF16, f"Bb{i}") for i in range(DST)]
    st.Cbc = [ar.alloc([128, 512], BF16, f"Cb{i}") for i in range(DST)]
    st.dr = [ar.alloc([128, 512], BF16, "dr")]
    st.dA = [ar.alloc([128, 512], F32, "dA") for _ in range(3)]
    st.dB = [ar.alloc([128, 512], BF16, "dB") for _ in range(3)]
    st.hsc = [ar.alloc([128, 512], BF16, "hsc") for _ in range(3)]
    st.yt = [ar.alloc([128, 512], BF16, "yt") for _ in range(3)]
    st.du = [ar.alloc([128, 512], BF16, "du")]
    st.t0b = [ar.alloc([128, 512], F32, "t0b") for _ in range(2)]
    st.nab = [ar.alloc([128, 512], F32, "nab") for _ in range(2)]
    st.bcv = [ar.alloc([32, 512], F32, "bcv")]
    st.ybf = [ar.alloc([128, 512], BF16, "ybf")]
    return st


def scan_chunk(ctx, st, t4):
    sch, d, db = ctx.sch, ctx.d, ctx.db
    xp, dt = st.xp, st.dt
    u, dl, yac, Bbc, Cbc = st.u, st.dl, st.yac, st.Bbc, st.Cbc
    dr, dA, dB, hsc, yt, du = st.dr, st.dA, st.dB, st.hsc, st.yt, st.du
    t0b, nab, bcv, ybf = st.t0b, st.nab, st.bcv, st.ybf
    u2 = st.u2
    if True:
        t0 = t4 * 512
        pe1 = ps_take(ctx)
        pe2 = ps_take(ctx)
        for cc in range(NC):
            ub = u[cc % 2]
            tload(ctx, ub, ub.ap,
                  d["XN2"][t0:t0 + 512, cc * 128:(cc + 1) * 128],
                  dbx(ctx, "XN2", t0, t0 + 512))
            sch.task("pe", lambda e, p=pe1, ub=ub, cc=cc: e.matmul(
                p.ap, xp.ap[:, cc * 160:cc * 160 + 128], ub.ap,
                start=(cc == 0), stop=(cc == NC - 1)),
                     reads=[ub, xp], writes=[pe1])
            sch.task("pe", lambda e, p=pe2, ub=ub, cc=cc: e.matmul(
                p.ap[0:32, :], xp.ap[:, cc * 160 + 128:cc * 160 + 160], ub.ap,
                start=(cc == 0), stop=(cc == NC - 1)),
                     reads=[ub, xp], writes=[pe2])
        drb = dr[0]
        sch.task("act", lambda e, p=pe1, o=drb: e.activation(o.ap, p.ap, AF.Copy),
                 reads=[pe1], writes=[drb])
        bc = bcv[0]
        sch.task("dve", lambda e, p=pe2, o=bc: e.tensor_copy(
            o.ap, p.ap[0:32, :]), reads=[pe2], writes=[bc])
        sch.task(QP, E_dma(d["BCR"][:, t0:t0 + 512], bc.ap),
                 reads=[bc], writes=db["BCR"], dma=QP)
        for n in range(DST):
            sch.task("pool", E_dma(
                Bbc[n].ap,
                d["BCR"][n:n + 1, t0:t0 + 512].partition_broadcast(128)),
                     reads=db["BCR"], writes=[Bbc[n]], dma="pool")
            sch.task("pool", E_dma(
                Cbc[n].ap,
                d["BCR"][DST + n:DST + n + 1, t0:t0 + 512].partition_broadcast(128)),
                     reads=db["BCR"], writes=[Cbc[n]], dma="pool")
        for dc in range(NC):
            pd = ps_take(ctx)
            sch.task("pe", lambda e, p=pd, dc=dc, b=drb: e.matmul(
                p.ap, dt.ap[:, dc * 128:(dc + 1) * 128], b.ap,
                start=True, stop=True),
                     reads=[dt, drb], writes=[pd])
            tt, nn = t0b[dc % 2], nab[dc % 2]
            sch.task("dve", lambda e, p=pd, dc=dc, o=tt: e.tensor_scalar(
                o.ap, p.ap, ctx.dtb.ap[:, dc:dc + 1], None, ALU.add),
                     reads=[pd, ctx.dtb], writes=[tt])
            sch.task("dve", lambda e, a=tt, o=nn: e.scalar_tensor_tensor(
                o.ap, a.ap, -1.0, a.ap, ALU.mult, ALU.min),
                     reads=[tt], writes=[nn])
            sch.task("act", lambda e, o=nn: e.activation(o.ap, o.ap, AF.Exp),
                     reads=[nn], writes=[nn])
            sch.task("act", lambda e, o=nn: e.activation(o.ap, o.ap, AF.Ln,
                                                         bias=1.0),
                     reads=[nn], writes=[nn])
            sch.task("dve", lambda e, a=tt, o=tt: e.tensor_single_scalar(
                o.ap, a.ap, 0.0, ALU.max),
                     reads=[tt], writes=[tt])
            sch.task("dve", lambda e, a=tt, b=nn, o=dl[dc % 2]: e.tensor_tensor(
                o.ap, a.ap, b.ap, ALU.add),
                     reads=[tt, nn], writes=[dl[dc % 2]])
            ub2 = u2[dc % 2]
            tload(ctx, ub2, ub2.ap,
                  d["XN2"][t0:t0 + 512, dc * 128:(dc + 1) * 128],
                  dbx(ctx, "XN2", t0, t0 + 512))
            sch.task("dve", lambda e, a=dl[dc % 2], b=ub2, o=du[0]:
                     e.tensor_tensor(o.ap, a.ap, b.ap, ALU.mult),
                     reads=[dl[dc % 2], ub2], writes=[du[0]])
            for n in range(DST):
                col = dc * DST + n
                da, db_, hs, yy = dA[n % 3], dB[n % 3], hsc[n % 3], yt[n % 3]
                sch.task("act", lambda e, a=dl[dc % 2], o=da, col=col: e.activation(
                    o.ap, a.ap, AF.Exp, scale=ctx.A_sb.ap[:, col:col + 1]),
                         reads=[dl[dc % 2], ctx.A_sb], writes=[da])
                sch.task("dve", lambda e, a=du[0], b=Bbc[n], o=db_:
                         e.tensor_tensor(o.ap, a.ap, b.ap, ALU.mult),
                         reads=[du[0], Bbc[n]], writes=[db_])
                sch.task("dve", lambda e, a=da, b=db_, o=hs, col=col:
                         e.tensor_tensor_scan(o.ap, a.ap, b.ap,
                                              ctx.hst.ap[:, col:col + 1],
                                              ALU.mult, ALU.add),
                         reads=[da, db_, ctx.hst], writes=[hs])
                sch.task("dve", lambda e, a=hs, col=col: e.tensor_copy(
                    ctx.hst.ap[:, col:col + 1], a.ap[:, 511:512]),
                         reads=[hs], writes=[ctx.hst])
                sch.task("dve", lambda e, a=hs, b=Cbc[n], o=yy:
                         e.tensor_tensor(o.ap, a.ap, b.ap, ALU.mult),
                         reads=[hs, Cbc[n]], writes=[yy])
                if n == 0:
                    sch.task("pool", lambda e, a=yy, o=yac[dc % 2]: e.tensor_copy(
                        o.ap, a.ap), reads=[yy], writes=[yac[dc % 2]])
                else:
                    sch.task("pool", lambda e, a=yy, o=yac[dc % 2]: e.tensor_tensor(
                        o.ap, o.ap, a.ap, ALU.add),
                             reads=[yy, yac[dc % 2]], writes=[yac[dc % 2]])
            sch.task("dve", lambda e, a=ub2, o=yac[dc % 2], dc=dc:
                     e.scalar_tensor_tensor(o.ap, a.ap,
                                            ctx.D_sb.ap[:, dc:dc + 1],
                                            o.ap, ALU.mult, ALU.add),
                     reads=[ub2, ctx.D_sb, yac[dc % 2]], writes=[yac[dc % 2]])
            yb = ybf[0]
            sch.task("dve", lambda e, a=yac[dc % 2], o=yb: e.tensor_copy(o.ap, a.ap),
                     reads=[yac[dc % 2]], writes=[yb])
            sch.task(QP, E_dma(d["Y"][dc * 128:(dc + 1) * 128, t0:t0 + 512],
                                 yb.ap),
                     reads=[yb], writes=db["Y"], dma=QP)



def phase_moe(ctx):
    sch, d, db, ar = ctx.sch, ctx.d, ctx.db, ctx.arena
    ar.reset()
    fill_xnf(ctx, "H2")
    moe = [ar.alloc([128, C], BF16, f"moe{i}") for i in range(8)]
    gsl = [ar.alloc([128, 16 * 256], BF16, "mgsl") for _ in range(2)]
    usl = [ar.alloc([128, 16 * 256], BF16, "musl") for _ in range(2)]
    hb = [ar.alloc([128, 1024], BF16, f"mhb{i}") for i in range(NI1)]
    dslb = [ar.alloc([128, 6 * 512], BF16, "mdsl") for _ in range(2)]
    sgt = [ar.alloc([128, 512], BF16, "msg") for _ in range(2)]
    sut = [ar.alloc([128, 512], BF16, "msu") for _ in range(2)]
    r_m1 = ar.alloc([128, 1], F32, "rm1"); r_m2 = ar.alloc([128, 1], F32, "rm2")
    r_tr = ar.alloc([128, 8], F32, "rtr"); r_g2 = ar.alloc([128, 8], F32, "rg2")
    r_p = ar.alloc([128, 8], F32, "rp"); r_sg = ar.alloc([128, 8], F32, "rsg")
    r_ms = ar.alloc([128, 8], F32, "rms"); r_s = ar.alloc([128, 1], F32, "rS")
    for tc in range(NT):
        pl = ps_take(ctx)

        def mmr(e, tc=tc, p=pl):
            for cc in range(NC):
                inst = e.matmul(p.ap[:, 0:8],
                                ctx.XNF[cc].ap[:, tc * 128:(tc + 1) * 128],
                                ctx.rw_sb.ap[:, cc * 8:(cc + 1) * 8],
                                start=(cc == 0), stop=(cc == NC - 1))
            return inst
        sch.task("pe", mmr, reads=ctx.XNF + [ctx.rw_sb], writes=[pl])
        sch.task("dve", lambda e, p=pl: e.reduce_max(
            r_m1.ap, p.ap[:, 0:8], axis=AX.X, negate=True),
                 reads=[pl], writes=[r_m1])
        sch.task("act", lambda e, p=pl: e.activation(
            r_p.ap, p.ap[:, 0:8], AF.Exp, bias=r_m1.ap, accum_out=r_s.ap),
                 reads=[pl, r_m1], writes=[r_p, r_s])
        sch.task("dve", lambda e: e.reciprocal(r_s.ap, r_s.ap),
                 reads=[r_s], writes=[r_s])
        sch.task("dve", lambda e: e.tensor_scalar_mul(r_p.ap, r_p.ap, r_s.ap),
                 reads=[r_p, r_s], writes=[r_p])
        sch.task("dve", lambda e: e.reduce_max(r_m1.ap, r_p.ap, axis=AX.X),
                 reads=[r_p], writes=[r_m1])
        sch.task("dve", lambda e: e.memset(r_tr.ap, -1e38), writes=[r_tr])
        sch.task("dve", lambda e: e.tensor_copy(r_tr.ap[:, 0:1], r_m1.ap),
                 reads=[r_m1], writes=[r_tr])
        sch.task("dve", lambda e: e.match_replace(r_g2.ap, r_tr.ap, r_p.ap, -1e30),
                 reads=[r_tr, r_p], writes=[r_g2])
        sch.task("dve", lambda e: e.reduce_max(r_m2.ap, r_g2.ap, axis=AX.X),
                 reads=[r_g2], writes=[r_m2])
        sch.task("act", lambda e: e.activation(r_sg.ap, r_p.ap, AF.Sign,
                                               scale=-1.0, bias=r_m2.ap),
                 reads=[r_p, r_m2], writes=[r_sg])
        sch.task("dve", lambda e: e.tensor_single_scalar(
            r_ms.ap, r_sg.ap, 0.0, ALU.max), reads=[r_sg], writes=[r_ms])
        sch.task("dve", lambda e: e.tensor_scalar(
            r_ms.ap, r_ms.ap, -1.0, 1.0, ALU.mult, ALU.add),
                 reads=[r_ms], writes=[r_ms])
        sch.task("dve", lambda e: e.tensor_tensor(
            r_m1.ap, r_m1.ap, r_m2.ap, ALU.add), reads=[r_m1, r_m2], writes=[r_m1])
        sch.task("dve", lambda e: e.tensor_scalar_add(r_m1.ap, r_m1.ap, 1e-9),
                 reads=[r_m1], writes=[r_m1])
        sch.task("dve", lambda e: e.reciprocal(r_m1.ap, r_m1.ap),
                 reads=[r_m1], writes=[r_m1])
        sch.task("dve", lambda e, tc=tc: e.scalar_tensor_tensor(
            ctx.comb[tc].ap, r_p.ap, r_m1.ap, r_ms.ap, ALU.mult, ALU.mult),
                 reads=[r_p, r_m1, r_ms], writes=[ctx.comb[tc]])

    gu_loaded = set()

    def gu_load(th, e_i, icg):
        key = (th, e_i, icg)
        if key in gu_loaded:
            return
        gu_loaded.add(key)
        i0 = icg * 256
        w = min(256, I1 - i0)
        gb, ub = gsl[icg % 2], usl[icg % 2]
        sch.task("pool", E_dma_gp(
            gb.ap[:, 0:NC * w], d["wgT"][e_i, :, i0:i0 + w], w),
                 writes=[gb], dma="pool")
        sch.task("pool", E_dma_gp(
            ub.ap[:, 0:NC * w], d["wuT"][e_i, :, i0:i0 + w], w),
                 writes=[ub], dma="pool")

    for th in range(2):
        th0 = th * 1024
        for e_i in range(NE):
            for icg in range(6):
                i0 = icg * 256
                w = min(256, I1 - i0)
                gb, ub = gsl[icg % 2], usl[icg % 2]
                gu_load(th, e_i, icg)
                for j in range(w // 128):
                    ic = icg * 2 + j
                    for t5 in range(2):
                        t0 = th0 + t5 * 512
                        pg = ps_take(ctx)
                        pu = ps_take(ctx)
                        for p, wb in ((pg, gb), (pu, ub)):
                            def mme(e, p=p, wb=wb, j=j, w=w, t0=t0):
                                for cc in range(NC):
                                    inst = e.matmul(
                                        p.ap,
                                        wb.ap[:, cc * w + j * 128:
                                              cc * w + (j + 1) * 128],
                                        ctx.XNF[cc].ap[:, t0:t0 + 512],
                                        start=(cc == 0), stop=(cc == NC - 1))
                                return inst
                            sch.task("pe", mme, reads=ctx.XNF + [wb], writes=[p])
                        sg, su = sgt[t5], sut[t5]
                        sch.task("act", lambda e, p=pg, o=sg: e.activation(
                            o.ap, p.ap, AF.Silu), reads=[pg], writes=[sg])
                        sch.task("act", lambda e, p=pu, o=su: e.activation(
                            o.ap, p.ap, AF.Copy), reads=[pu], writes=[su])
                        sch.task("dve", lambda e, a=sg, b=su, o=hb[ic], t5=t5:
                                 e.tensor_tensor(
                                     o.ap[:, t5 * 512:(t5 + 1) * 512],
                                     a.ap, b.ap, ALU.mult),
                                 reads=[sg, su], writes=[hb[ic]])
            if e_i + 1 < NE:
                gu_load(th, e_i + 1, 0)
                gu_load(th, e_i + 1, 1)
            elif th == 0:
                gu_load(1, 0, 0)
                gu_load(1, 0, 1)
            for co4 in range(4):
                co0 = co4 * 512
                for half in range(2):
                    i0 = half * 6 * 128
                    ni = 6 if half == 0 else 5
                    sch.task("pool", E_dma_gp(
                        dslb[half].ap[:, 0:ni * 512],
                        d["wdT"][e_i, i0:i0 + ni * 128, co0:co0 + 512], 512),
                             writes=[dslb[half]], dma="pool")
                for tci in range(8):
                    tc = th * 8 + tci
                    psd = ps_take(ctx)

                    def mmd(e, tci=tci, p=psd, b0=dslb[0], b1=dslb[1]):
                        for ic in range(NI1):
                            half = ic // 6
                            k = ic % 6
                            wb = b0 if half == 0 else b1
                            inst = e.matmul(
                                p.ap, hb[ic].ap[:, tci * 128:(tci + 1) * 128],
                                wb.ap[:, k * 512:(k + 1) * 512],
                                start=(ic == 0), stop=(ic == NI1 - 1))
                        return inst
                    sch.task("pe", mmd, reads=hb + [dslb[0], dslb[1]],
                             writes=[psd])
                    mtile = moe[tci]
                    if e_i == 0:
                        sch.task("dve", lambda e, p=psd, tc=tc, co0=co0, m=mtile:
                                 e.tensor_scalar_mul(
                                     m.ap[:, co0:co0 + 512], p.ap,
                                     ctx.comb[tc].ap[:, 0:1]),
                                 reads=[psd, ctx.comb[tc]], writes=[mtile])
                    else:
                        sch.task("dve", lambda e, p=psd, tc=tc, co0=co0, ei=e_i,
                                 m=mtile:
                                 e.scalar_tensor_tensor(
                                     m.ap[:, co0:co0 + 512], p.ap,
                                     ctx.comb[tc].ap[:, ei:ei + 1],
                                     m.ap[:, co0:co0 + 512],
                                     ALU.mult, ALU.add),
                                 reads=[psd, ctx.comb[tc], mtile],
                                 writes=[mtile])
        for tci in range(8):
            tc = th * 8 + tci
            sch.task(QP, E_dma(d["MOE"][tc * 128:(tc + 1) * 128, :],
                                 moe[tci].ap),
                     reads=[moe[tci]], writes=db["MOE"], dma=QP)



def phase_final(ctx):
    sch, d, db, ar = ctx.sch, ctx.d, ctx.db, ctx.arena
    ar.reset()
    x1 = [ar.alloc([128, C], F32, "fx1") for _ in range(2)]
    hid = [ar.alloc([128, C], BF16, "fhid") for _ in range(2)]
    mo = [ar.alloc([128, C], BF16, "fmo") for _ in range(2)]
    s2 = [ar.alloc([128, C], BF16, "fs2") for _ in range(2)]
    g2 = [ar.alloc([128, C], BF16, "fg2") for _ in range(2)]
    for tc in range(NT):
        t0 = tc * 128
        par = tc % 2
        xb, hh, mm, ss, gg = x1[par], hid[par], mo[par], s2[par], g2[par]
        sch.task(QP, E_dma(xb.ap, d["X1"][t0:t0 + 128, :]),
                 reads=db["X1"], writes=[xb], dma=QP)
        sch.task(QP, E_dma(hh.ap, d["HID"][t0:t0 + 128, :]),
                 reads=db["HID"], writes=[hh], dma=QP)
        sch.task(QP, E_dma(mm.ap, d["MOE"][t0:t0 + 128, :]),
                 reads=db["MOE"], writes=[mm], dma=QP)
        for c4 in range(4):
            tload(ctx, ss, ss.ap[:, c4 * 512:(c4 + 1) * 512],
                  d["Y"][c4 * 512:(c4 + 1) * 512, t0:t0 + 128], db["Y"])
        sch.task("dve", lambda e, a=hh, b=mm, o=gg: e.tensor_tensor(
            o.ap, a.ap, b.ap, ALU.add), reads=[hh, mm], writes=[gg])
        sch.task("act", lambda e, o=gg: e.activation(o.ap, o.ap, AF.Silu),
                 reads=[gg], writes=[gg])
        sch.task("dve", lambda e, a=ss, b=gg, o=ss: e.tensor_tensor(
            o.ap, a.ap, b.ap, ALU.mult), reads=[ss, gg], writes=[ss])
        sch.task("dve", lambda e, a=xb, b=ss: e.tensor_tensor(
            a.ap, a.ap, b.ap, ALU.add), reads=[xb, ss], writes=[xb])
        dep = sch.task(QP, E_dma(d["out"][t0:t0 + 128, :], xb.ap),
                       reads=[xb], dma=QP)
        ctx.stores.append(dep)



PHASES = ("ln0", "qkv0", "attn0", "ffn0", "combine0", "rms_attn", "qkv1",
          "attn1", "oproj", "scan", "moe", "final")


def build(dev_taps=(), upto=None):
    ctx = Ctx()
    ctx.nc = bass.Bass("TRN2", target_bir_lowering=False)
    ctx.sch = Sched(ctx.nc)
    _mk_dram(ctx, dev_taps)
    _mk_static(ctx)
    ctx.stores = []
    load_consts(ctx)
    def attn1_with_scan(c):
        phase_attn(c, 1)

    def scan_phase(c):
        c.arena.reset()
        stt_ = scan_setup(c)
        for t4 in range(4):
            scan_chunk(c, stt_, t4)

    emit = {
        "ln0": phase_ln0,
        "qkv0": phase_qkv0,
        "attn0": lambda c: phase_attn(c, 0),
        "ffn0": phase_ffn0,
        "combine0": phase_combine0,
        "rms_attn": phase_rms_attn,
        "qkv1": phase_qkv1,
        "attn1": attn1_with_scan,
        "oproj": phase_oproj,
        "scan": scan_phase,
        "moe": phase_moe,
        "final": phase_final,
    }
    for ph in PHASES:
        emit[ph](ctx)
        if upto is not None and ph == upto:
            break
    deps = list(ctx.stores)
    for name, bl in ctx.db.items():
        for b in bl:
            if b.writer is not None:
                deps.append(b.writer)
    ctx.sch.barrier_wait(QP, deps)
    ctx.sch.build()
    return ctx


def host_prep(inputs):
    f = lambda a: np.ascontiguousarray(a, dtype=np.float32)
    p = {}
    p["wqT"] = f(inputs["L0_attn_wq"].T)
    p["wkT"] = f(inputs["L0_attn_wk"].T)
    p["wvT"] = f(inputs["L0_attn_wv"].T)
    p["tau"] = f(np.reshape(inputs["L0_attn_tau"], (1, 1)))
    p["nu"] = f(np.reshape(inputs["L0_attn_nu"], (1, 1)))
    p["gwT"] = f(inputs["L0_ffn_gate_w"].T)
    p["uwT"] = f(inputs["L0_ffn_up_w"].T)
    p["dwT"] = f(inputs["L0_ffn_down_w"].T)
    p["gs"] = f(np.reshape(inputs["L0_ffn_gate_s"], (NI0, 128)).T)
    p["us"] = f(np.reshape(inputs["L0_ffn_up_s"], (NI0, 128)).T)
    p["ds_row"] = f(np.reshape(inputs["L0_ffn_down_s"], (1, C)))
    p["AlogR"] = f(np.reshape(inputs["L1_attn_A_log"], (NC, 128, DST))
                   .transpose(1, 0, 2).reshape(128, NC * DST))
    p["Dr"] = f(np.reshape(inputs["L1_attn_D"], (NC, 128)).T)
    p["xpT"] = f(inputs["L1_attn_x_proj_w"].T)
    p["dtT"] = f(inputs["L1_attn_dt_proj_w"].T)
    p["dtb"] = f(np.reshape(inputs["L1_attn_dt_proj_b"], (NC, 128)).T)
    p["anw"] = f(np.reshape(inputs["L1_ffn_attn_norm_w"], (1, C)))
    p["fnw"] = f(np.reshape(inputs["L1_ffn_ffn_norm_w"], (1, C)))
    p["qwT"] = f(inputs["L1_ffn_q_w"].T)
    p["kwT"] = f(inputs["L1_ffn_k_w"].T)
    p["vwT"] = f(inputs["L1_ffn_v_w"].T)
    p["owT"] = f(inputs["L1_ffn_o_w"].T)
    p["qnw"] = f(np.reshape(inputs["L1_ffn_q_norm_w"], (1, HD)))
    p["knw"] = f(np.reshape(inputs["L1_ffn_k_norm_w"], (1, HD)))
    p["rwT"] = f(inputs["L1_ffn_gate_w"].T)
    p["wgT"] = f(inputs["L1_ffn_wg"].transpose(0, 2, 1))
    p["wuT"] = f(inputs["L1_ffn_wu"].transpose(0, 2, 1))
    p["wdT"] = f(inputs["L1_ffn_wd"].transpose(0, 2, 1))
    freqs = 1.0 / (1000000.0 ** (np.arange(0, HD, 2, dtype=np.float32) / HD))
    tt = np.arange(T, dtype=np.float32)
    fr = np.outer(tt, freqs)
    p["rcos"] = np.cos(fr).astype(np.float32)
    p["rsin"] = np.sin(fr).astype(np.float32)
    cm = np.zeros((4, 128, 512), np.float32)
    for v in range(4):
        for pp in range(128):
            cm[v, pp, :v * 128 + pp + 1] = 1.0
    p["cmask"] = cm
    return p


def kernel(**inputs):
    prep = host_prep(inputs)
    ctx = build(DEV_TAPS)
    x = np.asarray(inputs["L0_attn_x"], dtype=np.float32)
    in_maps = []
    for c in range(8):
        m = dict(prep)
        m["x"] = np.ascontiguousarray(x[c])
        in_maps.append(m)
    res = run_bass_kernel_spmd(ctx.nc, in_maps, core_ids=list(range(8)))
    return np.stack([np.asarray(res.results[c]["out"]) for c in range(8)],
                    axis=0)
